# version 09
# baseline (speedup 1.0000x reference)
import contextlib

import numpy as np

import concourse.bass as bass
import concourse.mybir as mybir
from concourse.bass_utils import run_bass_kernel_spmd

F32 = mybir.dt.float32
BF16 = mybir.dt.bfloat16
I32 = mybir.dt.int32
U8 = mybir.dt.uint8
ALU = mybir.AluOpType
AF = mybir.ActivationFunctionType
AX = mybir.AxisListType

D = 1024
T = 2048
FF = 4096
NCH = 8
NST = 4
TSW = 512
NH = 16
NFO = 32
EPS = 1e-5


class Ctx:
    pass


_EMIT_OPS = {
    "tensor_tensor", "tensor_scalar", "tensor_copy", "tensor_add",
    "tensor_sub", "tensor_mul", "tensor_max", "scalar_tensor_tensor",
    "tensor_single_scalar", "tensor_scalar_add", "tensor_scalar_sub",
    "tensor_scalar_mul", "tensor_scalar_max", "tensor_scalar_min",
    "tensor_relu", "tensor_reduce", "reduce_max", "reduce_sum", "select",
    "copy_predicated", "activation", "square", "sqrt", "mul", "copy", "add",
    "memset", "memzero", "engine_nop", "reciprocal",
}

_INC_REC = {}
_SEM_TOT = {}
_ORIG_THEN_INC = bass.BassInstruction.then_inc


def _rec_then_inc(self, sem, n):
    tot = _SEM_TOT.get(id(sem), 0) + n
    _SEM_TOT[id(sem)] = tot
    _INC_REC[id(self)] = (self, sem, tot)
    return _ORIG_THEN_INC(self, sem, n)


bass.BassInstruction.then_inc = _rec_then_inc


class _Chained:

    def __init__(self, eng, seq_sem, nop_fn=None):
        self._eng = eng
        self._seq = seq_sem
        self._nop = nop_fn
        self._prev = None

    def wait_ge(self, sem, val):
        return self._eng.wait_ge(sem, val)

    def _edge(self):
        if self._prev is None:
            return
        rec = _INC_REC.get(id(self._prev))
        if rec is None or rec[1] is not self._seq:
            if rec is None:
                tgt = self._prev
            elif self._nop is not None:
                tgt = self._nop(self._eng)
            else:
                tgt = self._eng.engine_nop()
            _rec_then_inc(tgt, self._seq, 1)
            rec = _INC_REC[id(tgt)]
        self._eng.wait_ge(rec[1], rec[2])

    def __getattr__(self, name):
        f = getattr(self._eng, name)
        if name == "dma_start":
            def gd(*a, **k):
                self._edge()
                ins = f(*a, **k)
                self._prev = None
                return ins
            return gd
        if name not in _EMIT_OPS:
            return f

        def g(*a, **k):
            self._edge()
            ins = f(*a, **k)
            self._prev = ins
            return ins

        return g




def build_nc(stop="full"):
    _INC_REC.clear()
    _SEM_TOT.clear()
    nc = bass.Bass("TRN2", target_bir_lowering=False)
    c = Ctx()
    c.nc = nc
    c.stop = stop

    dram = lambda n, sh, ty, kind="ExternalInput": nc.dram_tensor(n, sh, ty, kind=kind).ap()
    c.x = dram("x", [128, NCH * T], F32)
    c.wvt = dram("wvt", [D, D], F32)
    c.wot = dram("wot", [D, D], F32)
    c.fc1q = dram("fc1q", [D, FF], I32)
    c.fc1sb = dram("fc1sb", [128, 8 * FF], F32)
    c.fc2q = dram("fc2q", [FF, D], I32)
    c.fc2sb = dram("fc2sb", [128, 32 * D], F32)
    c.gam = dram("gam", [128, NCH], F32)
    c.bet = dram("bet", [128, NCH], F32)
    c.f1b = dram("f1b", [128, NFO], F32)
    c.f2b = dram("f2b", [128, NCH], F32)
    c.qkvt = dram("qkvt", [D, 3 * D], F32)
    c.projt = dram("projt", [D, D], F32)
    c.bpj = dram("bpj", [128, NCH], F32)
    c.w1t = dram("w1t", [D, FF], F32)
    c.b1l1 = dram("b1l1", [128, NFO], F32)
    c.w2t = dram("w2t", [FF, D], F32)
    c.b2l1 = dram("b2l1", [128, NCH], F32)
    c.cvec = dram("cvec", [128, 8], F32)
    c.m4 = dram("m4", [128, 4 * TSW], BF16)
    c.out = dram("out", [128, NCH * T], F32, "ExternalOutput")
    c.w2sp = dram("w2sp", [FF, D], BF16, "Internal")
    c.wqsp = dram("wqsp", [FF, D], BF16, "Internal")
    c.rowsp = dram("rowsp", [4, T], F32, "Internal")
    c.rowspb = dram("rowspb", [1, D], BF16, "Internal")
    c.qksp = dram("qksp", [128, 2 * NCH * T], BF16, "Internal")
    c.f1sp = dram("f1sp", [128, NCH * T], F32, "Internal")

    with contextlib.ExitStack() as es:
        _body(c, es)
    return nc


def _body(c, es):
    nc = c.nc
    sb = lambda n, sh, ty: es.enter_context(nc.sbuf_tensor(n, sh, ty))

    c.X = sb("X", [128, NCH * T], F32)
    c.SCR = sb("SCR", [128, 3072], F32)
    c.YA = sb("YA", [128, NFO * TSW], BF16)
    c.MSK = sb("MSK", [128, D], U8)
    c.CON = sb("CON", [128, 8], F32)
    c.ONE = sb("ONE", [128, 1], BF16)
    c.ONE64 = sb("ONE64", [1, 64], F32)
    c.GAM = sb("GAM", [128, NCH], F32)
    c.BET = sb("BET", [128, NCH], F32)
    c.F1B = sb("F1B", [128, NFO], F32)
    c.F2B = sb("F2B", [128, NCH], F32)
    c.C0 = sb("C0", [128, NCH], F32)
    c.B1L = sb("B1L", [128, NFO], F32)
    c.BPJ = sb("BPJ", [128, NCH], F32)
    c.BQ = sb("BQ", [128, NCH], F32)
    c.B2L = sb("B2L", [128, NCH], F32)
    c.XS = sb("XS", [128, NCH], F32)
    c.XSB = sb("XSB", [128, NCH], BF16)
    c.VSB = sb("VSB", [128, NCH], BF16)
    c.SSB = sb("SSB", [128, 1], F32)
    c.BCA = sb("BCA", [128, 3 * TSW], F32)
    c.RR = c.SCR[0:1, 2048:2560]

    c.PS = es.enter_context(nc.psum_tensor("PS", [128, 4096], F32))

    for name in ["sINd", "sCN", "sSPd", "sWQ", "sWQd", "sST", "sMM", "sVR",
                 "sBCd", "sXSv", "sXSd", "sA0d", "sA0t", "sA0v", "sDQ",
                 "sDQd", "sSCd", "sW2", "sW2d", "sHB", "sEV", "sEP", "sRBd",
                 "sRBt", "sL1", "sL1d", "sQKd", "sQKt", "sQKe", "sVTd",
                 "sVTt", "sVTe", "sEXP", "sMSKv", "sPVc", "sPVt", "sNRM",
                 "sPRd", "sPRt", "sOUd", "sME", "sSPe", "sSPo", "sDQe",
                 "sDQo", "sRBe", "sRBo", "sQPe", "sQPo", "sQSe", "sQSo",
                 "sKSe", "sKSo", "sF1e", "sF1o", "sF1v", "sFPe", "sFPo"]:
        s = Ctx(); s.h = es.enter_context(nc.semaphore(name)); s.v = 0
        setattr(c, name, s)

    for name in ["qV", "qS", "qG", "qY"]:
        s = Ctx(); s.h = es.enter_context(nc.semaphore(name)); s.v = 0
        setattr(c, name, s)
    c.blk = es.enter_context(nc.Block())
    c._chains = {}

    def _wrap(engname, e):
        key = (engname,)
        if key not in c._chains:
            seq = {"vector": c.qV, "scalar": c.qS, "gpsimd": c.qG,
                   "sync": c.qY}[engname]
            nop = None
            if engname == "scalar":
                nop = lambda eng: eng.copy(out=c.CON[:, 7:8], in_=c.CON[:, 7:8])
            c._chains[key] = _Chained(e, seq.h, nop)
        ch = c._chains[key]
        ch._eng = e
        return ch
    c.wrap = _wrap

    class _BlkProxy:
        def __init__(self, blk):
            self._blk = blk

        def _mk(self, engname):
            dec = getattr(self._blk, engname)

            def deco(fn):
                return dec(lambda e, fn=fn: fn(c.wrap(engname, e)))
            return deco

        @property
        def vector(self):
            return self._mk("vector")

        @property
        def scalar(self):
            return self._mk("scalar")

        @property
        def gpsimd(self):
            return getattr(self._blk, "gpsimd")

        @property
        def sync(self):
            return getattr(self._blk, "sync")

        @property
        def tensor(self):
            return self._blk.tensor

    c.blk = _BlkProxy(c.blk)

    _ph_loads(c)
    _ph_consts(c)
    if c.stop == "consts":
        _ph_out(c, wait=[c.sCN])
        return

    with contextlib.ExitStack() as esA:
        c.WB = esA.enter_context(nc.sbuf_tensor("WB", [128, NCH * FF], BF16))
        with contextlib.ExitStack() as esd:
            c.WQB = esd.enter_context(nc.sbuf_tensor("WQB", [128, D], BF16))
            c.QIB = esd.enter_context(nc.sbuf_tensor("QIB", [128, FF], BF16))
            c.SCB = esd.enter_context(nc.sbuf_tensor("SCB", [128, FF // 2], F32))
            _ph_wq_spill(c)
            if c.stop == "wq":
                _ph_out(c, wait=[c.sWQd])
                return
            _ph_w1_dequant(c)
            _ph_w2_dequant_spill(c)
            if c.stop == "deq":
                _ph_out(c, wait=[c.sDQ, c.sW2d])
                return
            with contextlib.ExitStack() as esb:
                c.STAT = esb.enter_context(nc.sbuf_tensor("STAT", [128, T], F32))
                _ph_ln(c, layer=0)
                if c.stop == "ln":
                    _ph_out(c, wait=[c.sBCd])
                    return
                _ph_xsum_a0(c)
                if c.stop == "a0":
                    _ph_out(c, wait=[c.sA0v])
                    return
        c.W2B = esA.enter_context(nc.sbuf_tensor("W2B", [128, 4 * D], BF16))
        c.HBB = esA.enter_context(nc.sbuf_tensor("HBB", [128, NCH * TSW], BF16))
        _ph_ffn(c, layer=0, nst=(1 if c.stop == "s1" else NST))
        if c.stop == "s1":
            _ph_out(c, wait=[c.sEP])
            return
        if c.stop == "l0":
            _ph_out(c, wait=[c.sEP])
            return
        with contextlib.ExitStack() as esb:
            c.STAT = esb.enter_context(nc.sbuf_tensor("STAT1", [128, T], F32))
            _ph_ln(c, layer=1)
        _ph_w1_load_l1(c)
        _ph_ffn(c, layer=1)
    if c.stop == "ffn1":
        _ph_out(c, wait=[c.sEP])
        return
    with contextlib.ExitStack() as esq:
        c.VT = esq.enter_context(
            nc.sbuf_tensor("VT", [128, 16 * 1040], BF16))
        with contextlib.ExitStack() as esk:
            c.XNS = esk.enter_context(
                nc.sbuf_tensor("XNS", [128, NCH * TSW], BF16))
            c.QKS = esk.enter_context(
                nc.sbuf_tensor("QKS", [128, 2 * TSW], BF16))
            with contextlib.ExitStack() as esw:
                c.QKW = esw.enter_context(
                    nc.sbuf_tensor("QKW", [128, NCH * D], BF16))
                _ph_qk(c)
            with contextlib.ExitStack() as esv:
                c.VW = esv.enter_context(nc.sbuf_tensor("VW", [128, D], BF16))
                _ph_vt(c)
        c.M4 = esq.enter_context(nc.sbuf_tensor("M4", [128, 4 * TSW], BF16))
        c.PT = esq.enter_context(nc.sbuf_tensor("PT", [128, 4 * TSW], BF16))
        c.QKP = esq.enter_context(nc.sbuf_tensor("QKP", [128, 4 * T], BF16))
        _ph_heads(c)
        with contextlib.ExitStack() as esp:
            c.PRW = esp.enter_context(nc.sbuf_tensor("PRW", [128, NCH * D], BF16))
            _ph_proj(c)
    _ph_out(c, wait=[c.sPRt])


def _emit_bcast(c, e, s, rows):
    for i in rows:
        e.dma_start(out=c.BCA[:, TSW * i:TSW * (i + 1)],
                    in_=c.rowsp[i:i + 1, TSW * s:TSW * (s + 1)].broadcast_to([128, TSW])
                    ).then_inc(c.sBCd.h, 16)
        c.sBCd.v += 16


def _equant(c, e, src, dst, man, s1, s2, msk, signed, s3=None):
    if man == 3:
        Map, Cap, thr = c.CON[:, 3:4], c.CON[:, 4:5], 2.0 ** -5
    else:
        Map, Cap, thr = c.CON[:, 5:6], c.CON[:, 6:7], 2.0 ** -13
    x = src
    if signed:
        e.tensor_tensor(out=s3, in0=x, in1=x, op=ALU.mult)
        e.tensor_scalar(out=msk, in0=s3, scalar1=float(thr * thr), scalar2=None,
                        op0=ALU.is_ge)
    else:
        e.tensor_scalar(out=msk, in0=x, scalar1=float(thr), scalar2=None,
                        op0=ALU.is_ge)
    e.tensor_scalar(out=s1, in0=x, scalar1=Map, scalar2=None, op0=ALU.mult)
    e.tensor_tensor(out=s2, in0=s1, in1=x, op=ALU.subtract)
    e.tensor_tensor(out=s1, in0=s1, in1=s2, op=ALU.subtract)
    e.tensor_scalar(out=s2, in0=x, scalar1=Cap, scalar2=Cap,
                    op0=ALU.add, op1=ALU.subtract)
    e.tensor_tensor(out=s1, in0=s1, in1=s2, op=ALU.subtract)
    e.tensor_tensor(out=s1, in0=s1, in1=msk, op=ALU.mult)
    return e.tensor_tensor(out=dst, in0=s2, in1=s1, op=ALU.add)


def _ph_loads(c):
    blk = c.blk

    @blk.sync
    def _(e):
        e.dma_start(out=c.X[:, :], in_=c.x).then_inc(c.sINd.h, 16); c.sINd.v += 16
        for dst, src in [(c.GAM, c.gam), (c.BET, c.bet), (c.F1B, c.f1b),
                         (c.F2B, c.f2b), (c.B1L, c.b1l1), (c.BPJ, c.bpj),
                         (c.B2L, c.b2l1), (c.CON, c.cvec)]:
            e.dma_start(out=dst[:, :], in_=src).then_inc(c.sINd.h, 16)
            c.sINd.v += 16

    @blk.gpsimd
    def _(e):
        e.memset(c.ONE[:, :], 1.0)
        e.memset(c.ONE64[:, :], 1.0).then_inc(c.sME.h, 1); c.sME.v += 1

    c.nc.const_aps.register(F32, EPS, c.CON[:, 1:2])


def _ph_consts(c):
    blk = c.blk
    C52 = 1.5 * 2.0 ** 52

    @blk.vector
    def _(e):
        e.wait_ge(c.sINd.h, c.sINd.v)
        e.wait_ge(c.sME.h, 1)
        e.tensor_scalar(out=c.CON[:, 2:3], in0=c.CON[:, 0:1], scalar1=C52,
                        scalar2=C52, op0=ALU.add, op1=ALU.subtract)
        e.tensor_copy(out=c.SCR[:, 0:512], in_=c.X[:, 0:512])
        e.tensor_scalar(out=c.CON[:, 3:4], in0=c.CON[:, 2:3],
                        scalar1=float(2.0 ** 49 - 2.0 ** 20),
                        scalar2=float(2.0 ** 20 + 1.0), op0=ALU.mult, op1=ALU.add)
        e.tensor_scalar(out=c.CON[:, 4:5], in0=c.CON[:, 2:3],
                        scalar1=float(1.5 * 2.0 ** 43 - 1.5 * 2.0 ** 14),
                        scalar2=float(1.5 * 2.0 ** 14), op0=ALU.mult, op1=ALU.add)
        e.tensor_scalar(out=c.CON[:, 5:6], in0=c.CON[:, 2:3],
                        scalar1=float(2.0 ** 42 - 2.0 ** 13),
                        scalar2=float(2.0 ** 13 + 1.0), op0=ALU.mult, op1=ALU.add)
        e.tensor_scalar(out=c.CON[:, 6:7], in0=c.CON[:, 2:3],
                        scalar1=float(1.5 * 2.0 ** 28 - 0.75),
                        scalar2=0.75, op0=ALU.mult, op1=ALU.add)
        e.tensor_copy(out=c.SCR[:, 512:1024], in_=c.X[:, 0:512])
        _equant(c, e, c.B2L[:, :], c.BQ[:, :], man=10,
                s1=c.SCR[:, 8:16], s2=c.SCR[:, 16:24], s3=c.SCR[:, 0:8],
                msk=c.SCR[:, 24:32], signed=True).then_inc(c.sCN.h, 1)
        c.sCN.v += 1


def _ph_wq_spill(c):
    blk = c.blk
    wq0, wd0 = c.sWQ.v, c.sWQd.v

    @blk.sync
    def _(e):
        for g in range(32):
            if g >= 2:
                e.wait_ge(c.sWQ.h, wq0 + g - 1)
            sb_ = c.sSPe if g % 2 == 0 else c.sSPo
            e.dma_start(out=c.SCB[:, 1024 * (g % 2):1024 * (g % 2) + D],
                        in_=c.w2t[128 * g:128 * (g + 1), :]
                        ).then_inc(sb_.h, 16); sb_.v += 16

    @blk.vector
    def _(e):
        for g in range(32):
            sb_ = c.sSPe if g % 2 == 0 else c.sSPo
            e.wait_ge(sb_.h, 16 * (g // 2 + 1))
            if g == 0:
                e.wait_ge(c.sCN.h, c.sCN.v)
            if g >= 1:
                e.wait_ge(c.sWQd.h, wd0 + 16 * g)
            bo = 1024 * (g % 2)
            _equant(c, e, c.SCB[:, bo:bo + D], c.WQB[:, :], man=3,
                    s1=c.SCR[:, 0:D], s2=c.SCR[:, 1024:1024 + D],
                    s3=c.SCR[:, 0:D], msk=c.SCR[:, 2048:2048 + D], signed=True
                    ).then_inc(c.sWQ.h, 1); c.sWQ.v += 1

    @blk.gpsimd
    def _(e):
        for g in range(32):
            e.wait_ge(c.sWQ.h, wq0 + g + 1)
            e.dma_start(out=c.wqsp[128 * g:128 * (g + 1), :], in_=c.WQB[:, :]
                        ).then_inc(c.sWQd.h, 16); c.sWQd.v += 16


def _ph_w1_dequant(c):
    blk = c.blk
    dq0, dd0, sc0 = c.sDQ.v, c.sDQd.v, c.sSCd.v

    @blk.gpsimd
    def _(e):
        for g in range(NCH):
            if g >= 1:
                e.wait_ge(c.sDQ.h, dq0 + 2 * g)
            e.dma_start(out=c.QIB[:, :], in_=c.fc1q[128 * g:128 * (g + 1), :]
                        ).then_inc(c.sDQd.h, 16); c.sDQd.v += 16

    @blk.scalar
    def _(e):
        for g in range(NCH):
            for hf in range(2):
                if g == 0 and hf == 0:
                    e.wait_ge(c.sWQ.h, 32)
                if g + hf >= 1:
                    e.wait_ge(c.sDQ.h, dq0 + 2 * g + hf)
                e.dma_start(out=c.SCB[:, :],
                            in_=c.fc1sb[:, FF * g + 2048 * hf:FF * g + 2048 * (hf + 1)]
                            ).then_inc(c.sSCd.h, 16); c.sSCd.v += 16

    @blk.vector
    def _(e):
        for g in range(NCH):
            e.wait_ge(c.sDQd.h, dd0 + 16 * (g + 1))
            for hf in range(2):
                e.wait_ge(c.sSCd.h, sc0 + 16 * (2 * g + hf + 1))
                e.scalar_tensor_tensor(
                    out=c.WB[:, FF * g + 2048 * hf:FF * g + 2048 * (hf + 1)],
                    in0=c.QIB[:, 2048 * hf:2048 * (hf + 1)], scalar=-128.0,
                    in1=c.SCB[:, :], op0=ALU.add, op1=ALU.mult
                    ).then_inc(c.sDQ.h, 1); c.sDQ.v += 1


def _ph_w2_dequant_spill(c):
    blk = c.blk
    w20, wd0, sc0 = c.sW2.v, c.sW2d.v, c.sSCd.v

    @blk.gpsimd
    def _(e):
        for g in range(33):
            if g < 32:
                if g < 2:
                    e.wait_ge(c.sDQ.h, c.sDQ.v)
                if g >= 2:
                    e.wait_ge(c.sW2.h, w20 + g - 1)
                sb_ = c.sDQe if g % 2 == 0 else c.sDQo
                e.dma_start(out=c.QIB[:, D * (g % 2):D * (g % 2) + D],
                            in_=c.fc2q[128 * g:128 * (g + 1), :]
                            ).then_inc(sb_.h, 16); sb_.v += 16
            if g >= 1:
                e.wait_ge(c.sW2.h, w20 + g)
                e.dma_start(out=c.w2sp[128 * (g - 1):128 * g, :], in_=c.WQB[:, :]
                            ).then_inc(c.sW2d.h, 16); c.sW2d.v += 16

    @blk.scalar
    def _(e):
        for g in range(32):
            if g == 0:
                e.wait_ge(c.sDQ.h, c.sDQ.v)
            if g >= 1:
                e.wait_ge(c.sW2.h, w20 + g)
            e.dma_start(out=c.SCB[:, 0:D], in_=c.fc2sb[:, D * g:D * (g + 1)]
                        ).then_inc(c.sSCd.h, 16); c.sSCd.v += 16

    @blk.vector
    def _(e):
        for g in range(32):
            sb_ = c.sDQe if g % 2 == 0 else c.sDQo
            e.wait_ge(sb_.h, 16 * (g // 2 + 1))
            e.wait_ge(c.sSCd.h, sc0 + 16 * (g + 1))
            if g == 0:
                e.wait_ge(c.sWQd.h, c.sWQd.v)
            if g >= 1:
                e.wait_ge(c.sW2d.h, wd0 + 16 * g)
            e.scalar_tensor_tensor(out=c.WQB[:, :],
                                   in0=c.QIB[:, D * (g % 2):D * (g % 2) + D],
                                   scalar=-128.0, in1=c.SCB[:, 0:D],
                                   op0=ALU.add, op1=ALU.mult
                                   ).then_inc(c.sW2.h, 1); c.sW2.v += 1


def _ph_ln(c, layer):
    blk = c.blk
    stat = c.STAT
    xb = c.YA[:, 0:T]
    sq = c.YA[:, T:2 * T]
    st0, mm0, vr0, bc0 = c.sST.v, c.sMM.v, c.sVR.v, c.sBCd.v
    COL = lambda i: slice(16 * i, 16 * (i + 1))

    @blk.vector
    def _(e):
        if layer == 1:
            e.wait_ge(c.sEP.h, c.sEP.v)
        else:
            e.wait_ge(c.sCN.h, c.sCN.v)
        for g in range(NCH):
            if g >= 1:
                e.wait_ge(c.sMM.h, mm0 + g)
            xg = c.X[:, T * g:T * (g + 1)]
            e.tensor_copy(out=xb, in_=xg)
            e.tensor_tensor(out=sq, in0=xg, in1=xg, op=ALU.mult
                            ).then_inc(c.sST.h, 1)
            c.sST.v += 1

    @blk.tensor
    def _(e):
        for g in range(NCH):
            e.wait_ge(c.sST.h, st0 + g + 1)
            for s in range(NST):
                sl = slice(TSW * s, TSW * (s + 1))
                e.matmul(c.PS[0:1, sl], c.ONE[:, :], xb[:, sl],
                         start=(g == 0), stop=(g == NCH - 1))
                mm = e.matmul(c.PS[0:1, 2048 + TSW * s:2048 + TSW * (s + 1)],
                              c.ONE[:, :], sq[:, sl],
                              start=(g == 0), stop=(g == NCH - 1))
            mm.then_inc(c.sMM.h, 1)
            c.sMM.v += 1

    @blk.vector
    def _(e):
        e.wait_ge(c.sMM.h, mm0 + NCH)
        e.tensor_copy(out=c.SCR[0:1, 0:T], in_=c.PS[0:1, 0:T])
        e.tensor_copy(out=c.SCR[32:33, 0:T], in_=c.PS[0:1, 2048:2048 + T]
                      ).then_inc(c.sVR.h, 1)
        c.sVR.v += 1

    b0 = c.sBCd.v

    @blk.sync
    def _(e):
        e.wait_ge(c.sVR.h, vr0 + 1)
        e.dma_start(out=c.rowsp[0:1, :], in_=c.SCR[0:1, 0:T]
                    ).then_inc(c.sBCd.h, 16); c.sBCd.v += 16
        e.dma_start(out=c.rowsp[1:2, :], in_=c.SCR[32:33, 0:T]
                    ).then_inc(c.sBCd.h, 16); c.sBCd.v += 16
        e.wait_ge(c.sBCd.h, b0 + 32)
        e.dma_start(out=stat[:, COL(0)],
                    in_=c.rowsp[0:1, :].rearrange("o (t p) -> (o p) t", p=128),
                    allow_slow_non_contiguous=True
                    ).then_inc(c.sBCd.h, 16); c.sBCd.v += 16
        e.dma_start(out=stat[:, COL(1)],
                    in_=c.rowsp[1:2, :].rearrange("o (t p) -> (o p) t", p=128),
                    allow_slow_non_contiguous=True
                    ).then_inc(c.sBCd.h, 16); c.sBCd.v += 16

    b1 = c.sBCd.v

    @blk.vector
    def _(e):
        e.wait_ge(c.sBCd.h, b1)
        spc = lambda: e.tensor_copy(out=c.SCR[:, 1024:1536], in_=c.X[:, 0:512])
        e.tensor_scalar(out=stat[:, COL(2)], in0=stat[:, COL(0)],
                        scalar1=1.0 / D, scalar2=None, op0=ALU.mult)
        e.tensor_scalar(out=stat[:, COL(3)], in0=stat[:, COL(1)],
                        scalar1=1.0 / D, scalar2=None, op0=ALU.mult); spc()
        e.tensor_tensor(out=stat[:, COL(8)], in0=stat[:, COL(2)],
                        in1=stat[:, COL(2)], op=ALU.mult); spc()
        e.tensor_tensor(out=stat[:, COL(3)], in0=stat[:, COL(3)],
                        in1=stat[:, COL(8)], op=ALU.subtract
                        ).then_inc(c.sVR.h, 1)
        c.sVR.v += 1

    @blk.scalar
    def _(e):
        e.wait_ge(c.sVR.h, vr0 + 2)
        e.activation(out=stat[:, COL(8)], in_=stat[:, COL(3)], func=AF.Ln,
                     bias=EPS)
        e.activation(out=c.SCR[:, 1536:2048], in_=c.X[:, 0:512],
                     func=AF.Identity)
        e.activation(out=stat[:, COL(4)], in_=stat[:, COL(8)], func=AF.Exp,
                     scale=-0.5).then_inc(c.sVR.h, 1)
        c.sVR.v += 1

    @blk.vector
    def _(e, layer=layer):
        e.wait_ge(c.sVR.h, vr0 + 3)
        spc = lambda: e.tensor_copy(out=c.SCR[:, 1024:1536], in_=c.X[:, 0:512])
        e.scalar_tensor_tensor(out=stat[:, COL(5)], in0=stat[:, COL(2)],
                               scalar=-1.0, in1=stat[:, COL(4)],
                               op0=ALU.mult, op1=ALU.mult)
        if layer == 0:
            e.tensor_tensor(out=stat[:, COL(6)], in0=stat[:, COL(3)],
                            in1=stat[:, COL(4)], op=ALU.mult); spc()
            e.tensor_tensor(out=stat[:, COL(6)], in0=stat[:, COL(6)],
                            in1=stat[:, COL(4)], op=ALU.mult)
            spc()
            e.tensor_copy(out=c.YA[0:128, 2 * T:2 * T + 16], in_=stat[:, COL(5)])
        e.engine_nop().then_inc(c.sVR.h, 1)
        c.sVR.v += 1

    if layer == 0:
        @blk.scalar
        def _(e):
            e.wait_ge(c.sVR.h, vr0 + 4)
            e.activation(out=stat[:, COL(8)], in_=stat[:, COL(6)],
                         func=AF.Ln, bias=EPS)
            e.activation(out=c.SCR[:, 1536:2048], in_=c.X[:, 0:512],
                         func=AF.Identity)
            e.activation(out=stat[:, COL(7)], in_=stat[:, COL(8)],
                         func=AF.Exp, scale=-0.5).then_inc(c.sVR.h, 1)
            c.sVR.v += 1

        smb = c.sMM.v

        @blk.tensor
        def _(e):
            e.wait_ge(c.sVR.h, vr0 + 4)
            e.matmul(c.PS[0:1, 3072:3088], c.ONE[:, :],
                     c.YA[0:128, 2 * T:2 * T + 16], start=True, stop=True
                     ).then_inc(c.sMM.h, 1)
            c.sMM.v += 1

        @blk.vector
        def _(e, smb=smb):
            e.wait_ge(c.sMM.h, smb + 1)
            e.reduce_sum(out=c.SCR[0:1, 2560:2561], in_=c.PS[0:1, 3072:3088],
                         axis=AX.X).then_inc(c.sVR.h, 1)
            c.sVR.v += 1

    vfin = c.sVR.v
    b2 = c.sBCd.v

    @blk.sync
    def _(e, layer=layer):
        e.wait_ge(c.sVR.h, vfin)
        e.dma_start(out=c.rowsp[0:1, :].rearrange("o (t p) -> (o p) t", p=128),
                    in_=stat[:, COL(4)],
                    allow_slow_non_contiguous=True
                    ).then_inc(c.sBCd.h, 16); c.sBCd.v += 16
        e.dma_start(out=c.rowsp[1:2, :].rearrange("o (t p) -> (o p) t", p=128),
                    in_=stat[:, COL(5)],
                    allow_slow_non_contiguous=True
                    ).then_inc(c.sBCd.h, 16); c.sBCd.v += 16
        if layer == 0:
            e.dma_start(out=c.rowsp[2:3, :].rearrange(
                            "o (t p) -> (o p) t", p=128),
                        in_=stat[:, COL(7)],
                        allow_slow_non_contiguous=True
                        ).then_inc(c.sBCd.h, 16); c.sBCd.v += 16
            e.dma_start(out=c.rowsp[3:4, 0:1], in_=c.SCR[0:1, 2560:2561]
                        ).then_inc(c.sBCd.h, 16); c.sBCd.v += 16
        e.wait_ge(c.sBCd.h, c.sBCd.v)
        if layer == 0:
            e.dma_start(out=c.SSB[:, :],
                        in_=c.rowsp[3:4, 0:1].broadcast_to([128, 1])
                        ).then_inc(c.sBCd.h, 16); c.sBCd.v += 16


def _ph_xsum_a0(c):
    blk = c.blk
    xs0 = c.sXSv.v
    bb = c.sBCd.v

    @blk.sync
    def _(e):
        for s in range(NST):
            if s >= 1:
                e.wait_ge(c.sXSv.h, xs0 + s)
            _emit_bcast(c, e, s, rows=[0])

    @blk.vector
    def _(e):
        for s in range(NST):
            e.wait_ge(c.sBCd.h, bb + 16 * (s + 1))
            for g in range(NCH):
                e.tensor_tensor(out=c.SCR[:, 0:TSW],
                                in0=c.X[:, T * g + TSW * s:T * g + TSW * (s + 1)],
                                in1=c.BCA[:, 0:TSW], op=ALU.mult)
                e.reduce_sum(out=c.SCR[:, 512 + 8 * s + g:513 + 8 * s + g],
                             in_=c.SCR[:, 0:TSW], axis=AX.X)
            e.engine_nop().then_inc(c.sXSv.h, 1)
            c.sXSv.v += 1
        parts = c.SCR[:, 512:544].rearrange("p (s g) -> p g s", g=NCH)
        for g in range(NCH):
            e.reduce_sum(out=c.XS[:, g:g + 1], in_=parts[:, g, :], axis=AX.X)
        e.tensor_copy(out=c.SCR[:, 0:512], in_=c.X[:, 0:512])
        e.tensor_scalar(out=c.XS[:, :], in0=c.XS[:, :], scalar1=c.SSB[:, 0:1],
                        scalar2=None, op0=ALU.add)
        e.tensor_copy(out=c.SCR[:, 0:512], in_=c.X[:, 0:512])
        e.tensor_copy(out=c.XSB[:, :], in_=c.XS[:, :]
                      ).then_inc(c.sXSv.h, 1); c.sXSv.v += 1

    a0d, a0t = c.sA0d.v, c.sA0t.v

    @blk.gpsimd
    def _(e):
        e.wait_ge(c.sW2d.h, c.sW2d.v)
        for g in range(NCH):
            if g >= 1:
                e.wait_ge(c.sA0t.h, a0t + g)
            e.dma_start(out=c.WQB[:, :], in_=c.wvt[128 * g:128 * (g + 1), :]
                        ).then_inc(c.sA0d.h, 16); c.sA0d.v += 16

    @blk.tensor
    def _(e):
        e.wait_ge(c.sXSv.h, c.sXSv.v)
        for g in range(NCH):
            e.wait_ge(c.sA0d.h, a0d + 16 * (g + 1))
            for h in range(2):
                mm = e.matmul(c.PS[0:1, 512 * h:512 * (h + 1)],
                              c.XSB[:, g:g + 1], c.WQB[:, 512 * h:512 * (h + 1)],
                              start=(g == 0), stop=(g == NCH - 1))
            mm.then_inc(c.sA0t.h, 1)
            c.sA0t.v += 1

    a0v = c.sA0v.v

    @blk.vector
    def _(e):
        e.wait_ge(c.sA0t.h, c.sA0t.v)
        e.tensor_copy(out=c.YA[0:1, 0:D], in_=c.PS[0:1, 0:D]
                      ).then_inc(c.sA0v.h, 1); c.sA0v.v += 1

    xd1 = c.sXSd.v

    @blk.sync
    def _(e):
        e.wait_ge(c.sA0v.h, a0v + 1)
        e.dma_start(out=c.rowspb[0:1, :], in_=c.YA[0:1, 0:D]
                    ).then_inc(c.sXSd.h, 16); c.sXSd.v += 16
        e.wait_ge(c.sXSd.h, xd1 + 16)
        e.dma_start(out=c.VSB[:, :],
                    in_=c.rowspb[0:1, :].rearrange("o (g p) -> (o p) g", p=128),
                    allow_slow_non_contiguous=True
                    ).then_inc(c.sXSd.h, 16); c.sXSd.v += 16

    a1d, a1t = c.sA0d.v, c.sA0t.v

    @blk.gpsimd
    def _(e):
        for g in range(NCH):
            e.wait_ge(c.sA0t.h, a1t + g)
            e.dma_start(out=c.WQB[:, :], in_=c.wot[128 * g:128 * (g + 1), :]
                        ).then_inc(c.sA0d.h, 16); c.sA0d.v += 16

    @blk.tensor
    def _(e):
        e.wait_ge(c.sXSd.h, c.sXSd.v)
        for g in range(NCH):
            e.wait_ge(c.sA0d.h, a1d + 16 * (g + 1))
            for h in range(2):
                mm = e.matmul(c.PS[0:1, 1024 + 512 * h:1024 + 512 * (h + 1)],
                              c.VSB[:, g:g + 1], c.WQB[:, 512 * h:512 * (h + 1)],
                              start=(g == 0), stop=(g == NCH - 1))
            mm.then_inc(c.sA0t.h, 1)
            c.sA0t.v += 1

    a0v2 = c.sA0v.v

    @blk.vector
    def _(e):
        e.wait_ge(c.sA0t.h, c.sA0t.v)
        e.tensor_copy(out=c.SCR[0:1, 0:D], in_=c.PS[0:1, 1024:2048]
                      ).then_inc(c.sA0v.h, 1); c.sA0v.v += 1

    xd2 = c.sXSd.v

    @blk.sync
    def _(e):
        e.wait_ge(c.sA0v.h, a0v2 + 1)
        e.dma_start(out=c.rowsp[3:4, 0:D], in_=c.SCR[0:1, 0:D]
                    ).then_inc(c.sXSd.h, 16); c.sXSd.v += 16
        e.wait_ge(c.sXSd.h, xd2 + 16)
        e.dma_start(out=c.C0[:, :],
                    in_=c.rowsp[3:4, 0:D].rearrange("o (g p) -> (o p) g", p=128),
                    allow_slow_non_contiguous=True
                    ).then_inc(c.sXSd.h, 16); c.sXSd.v += 16

    @blk.vector
    def _(e):
        e.wait_ge(c.sXSd.h, c.sXSd.v)
        e.scalar_tensor_tensor(out=c.C0[:, :], in0=c.F2B[:, :], scalar=0.5,
                               in1=c.C0[:, :], op0=ALU.mult, op1=ALU.add
                               ).then_inc(c.sA0v.h, 1); c.sA0v.v += 1


def _ph_ffn(c, layer, nst=NST):
    blk = c.blk
    spill = c.w2sp if layer == 0 else c.wqsp
    rows = [0, 1, 2] if layer == 0 else [0, 1]
    for s in range(nst):
        bb = c.sBCd.v

        @blk.scalar
        def _(e, s=s, rows=rows, layer=layer):
            if s == 0:
                e.wait_ge(c.sBCd.h, c.sBCd.v)
                if layer == 0:
                    e.wait_ge(c.sA0v.h, c.sA0v.v)
            if s >= 1:
                e.wait_ge(c.sEP.h, c.sEP.v)
            _emit_bcast(c, e, s, rows=rows)

        hb0 = c.sHB.v

        @blk.vector
        def _(e, s=s, hb0=hb0, bb=bb, layer=layer, rows=rows):
            if s == 0:
                if layer == 0:
                    e.wait_ge(c.sA0v.h, c.sA0v.v)
                    e.wait_ge(c.sDQ.h, c.sDQ.v)
                    e.wait_ge(c.sW2d.h, c.sW2d.v)
                else:
                    e.wait_ge(c.sL1d.h, c.sL1d.v)
            if layer == 1 and s >= 1:
                e.wait_ge(c.sF1e.h, 64 * s)
                e.wait_ge(c.sF1o.h, 64 * s)
            e.wait_ge(c.sBCd.h, bb + 16 * len(rows))
            for g in range(NCH):
                xg = c.X[:, T * g + TSW * s:T * g + TSW * (s + 1)]
                if layer == 0:
                    if g >= 1:
                        e.wait_ge(c.sHB.h, hb0 + 2 * g)
                    e.tensor_tensor(out=c.SCR[:, 0:TSW], in0=xg,
                                    in1=c.BCA[:, 0:TSW], op=ALU.mult)
                    e.tensor_tensor(out=c.SCR[:, 512:1024], in0=c.SCR[:, 0:TSW],
                                    in1=c.BCA[:, TSW:2 * TSW], op=ALU.add)
                    e.tensor_tensor(out=c.SCR[:, 0:TSW], in0=c.SCR[:, 512:1024],
                                    in1=c.BCA[:, 2 * TSW:3 * TSW],
                                    op=ALU.mult).then_inc(c.sHB.h, 1)
                    c.sHB.v += 2
                else:
                    e.tensor_tensor(out=c.SCR[:, 0:TSW], in0=xg,
                                    in1=c.BCA[:, 0:TSW], op=ALU.mult)
                    e.tensor_tensor(out=c.HBB[:, TSW * g:TSW * (g + 1)],
                                    in0=c.SCR[:, 0:TSW],
                                    in1=c.BCA[:, TSW:2 * TSW], op=ALU.add
                                    ).then_inc(c.sHB.h, 1)
                    c.sHB.v += 1

        if layer == 0:
            @blk.scalar
            def _(e, hb0=hb0):
                for g in range(NCH):
                    e.wait_ge(c.sHB.h, hb0 + 2 * g + 1)
                    e.activation(out=c.HBB[:, TSW * g:TSW * (g + 1)],
                                 in_=c.SCR[:, 0:TSW], func=AF.Identity,
                                 bias=c.BET[:, g:g + 1], scale=c.GAM[:, g:g + 1]
                                 ).then_inc(c.sHB.h, 1)

        mm0, ev0 = c.sMM.v, c.sEV.v

        @blk.tensor
        def _(e, s=s, layer=layer, mm0=mm0, ev0=ev0):
            e.wait_ge(c.sHB.h, c.sHB.v)
            if s > 0 or layer == 1:
                e.wait_ge(c.sEP.h, c.sEP.v)
            for fo in range(NFO):
                if fo >= 2:
                    e.wait_ge(c.sEV.h, ev0 + (fo - 1) * (2 if layer else 1))
                pb = c.PS[:, 512 * (fo % 2):512 * (fo % 2 + 1)]
                for g in range(NCH):
                    mm = e.matmul(pb, c.WB[:, FF * g + 128 * fo:FF * g + 128 * (fo + 1)],
                                  c.HBB[:, TSW * g:TSW * (g + 1)],
                                  start=(g == 0), stop=(g == NCH - 1))
                mm.then_inc(c.sMM.h, 1)
                c.sMM.v += 1

        if layer == 0:
            @blk.scalar
            def _(e, mm0=mm0):
                for fo in range(NFO):
                    e.wait_ge(c.sMM.h, mm0 + fo + 1)
                    pb = c.PS[:, 512 * (fo % 2):512 * (fo % 2 + 1)]
                    e.activation(out=c.YA[:, TSW * fo:TSW * (fo + 1)], in_=pb,
                                 func=AF.Silu, bias=c.F1B[:, fo:fo + 1]
                                 ).then_inc(c.sEV.h, 1)
                    c.sEV.v += 1
        else:
            @blk.scalar
            def _(e, mm0=mm0, ev0=ev0):
                for fo in range(NFO):
                    e.wait_ge(c.sMM.h, mm0 + fo + 1)
                    if fo >= 1:
                        e.wait_ge(c.sEV.h, ev0 + 2 * fo)
                    pb = c.PS[:, 512 * (fo % 2):512 * (fo % 2 + 1)]
                    e.activation(out=c.SCR[:, 0:TSW], in_=pb, func=AF.Relu,
                                 bias=c.B1L[:, fo:fo + 1]).then_inc(c.sEV.h, 1)
                    c.sEV.v += 2

            @blk.vector
            def _(e, ev0=ev0):
                for fo in range(NFO):
                    e.wait_ge(c.sEV.h, ev0 + 2 * fo + 1)
                    _equant(c, e, c.SCR[:, 0:TSW], c.YA[:, TSW * fo:TSW * (fo + 1)],
                            man=3, s1=c.SCR[:, 512:1024], s2=c.SCR[:, 1024:1536],
                            msk=c.SCR[:, 1536:2048], signed=False
                            ).then_inc(c.sEV.h, 1)

        rb0, rt0 = (c.sRBe.v, c.sRBo.v), c.sRBt.v

        @blk.sync
        def _(e, spill=spill, rt0=rt0, s=s, layer=layer):
            if s == 0:
                e.wait_ge((c.sW2d if layer == 0 else c.sWQd).h,
                          (c.sW2d if layer == 0 else c.sWQd).v)
            for q in range(16):
                if rt0 + q >= 2:
                    e.wait_ge(c.sRBt.h, rt0 + q - 1)
                sb_ = c.sRBe if q % 2 == 0 else c.sRBo
                e.dma_start(out=c.W2B[:, 2 * D * (q % 2):2 * D * (q % 2 + 1)
                                      ].rearrange("p (g d) -> p g d", g=2),
                            in_=spill[256 * q:256 * (q + 1), :].rearrange(
                                "(g p) d -> p g d", p=128)
                            ).then_inc(sb_.h, 16); sb_.v += 16

        @blk.tensor
        def _(e, rb0=rb0, rt0=rt0):
            e.wait_ge(c.sEV.h, c.sEV.v)
            for q in range(16):
                sb_ = c.sRBe if q % 2 == 0 else c.sRBo
                e.wait_ge(sb_.h, rb0[q % 2] + 16 * (q // 2 + 1))
                for gg in range(2):
                    g2 = 2 * q + gg
                    off = 2 * D * (q % 2) + D * gg
                    for d in range(NCH):
                        mm = e.matmul(c.PS[:, 512 * d:512 * (d + 1)],
                                      c.W2B[:, off + 128 * d:off + 128 * (d + 1)],
                                      c.YA[:, TSW * g2:TSW * (g2 + 1)],
                                      start=(g2 == 0), stop=(g2 == 31))
                mm.then_inc(c.sRBt.h, 1)
                c.sRBt.v += 1

        ep0 = c.sEP.v

        @blk.scalar
        def _(e, rt0=rt0, layer=layer, ep0=ep0):
            e.wait_ge(c.sRBt.h, rt0 + 16)
            for d in range(NCH):
                if d >= 2:
                    e.wait_ge(c.sEP.h, ep0 + 2 * d - 1)
                bias = c.C0[:, d:d + 1] if layer == 0 else c.BQ[:, d:d + 1]
                e.activation(out=c.SCR[:, 2048 + TSW * (d % 2):2048 + TSW * (d % 2 + 1)],
                             in_=c.PS[:, 512 * d:512 * (d + 1)], func=AF.Identity,
                             bias=bias, scale=(0.5 if layer == 0 else 1.0)
                             ).then_inc(c.sEP.h, 1)
                c.sEP.v += 2

        @blk.vector
        def _(e, s=s, ep0=ep0, layer=layer):
            for d in range(NCH):
                e.wait_ge(c.sEP.h, ep0 + 2 * d + 1)
                zc = c.SCR[:, 2048 + TSW * (d % 2):2048 + TSW * (d % 2 + 1)]
                xg = c.X[:, T * d + TSW * s:T * d + TSW * (s + 1)]
                if layer == 0:
                    e.tensor_tensor(out=c.SCR[:, 0:TSW], in0=xg,
                                    in1=c.BCA[:, 0:TSW], op=ALU.mult)
                    e.tensor_tensor(out=c.SCR[:, 512:1024], in0=c.SCR[:, 0:TSW],
                                    in1=c.BCA[:, TSW:2 * TSW], op=ALU.add)
                    e.tensor_tensor(out=c.SCR[:, 0:TSW], in0=c.SCR[:, 512:1024],
                                    in1=zc, op=ALU.add)
                    e.tensor_tensor(out=xg, in0=c.SCR[:, 0:TSW], in1=xg,
                                    op=ALU.add).then_inc(c.sEP.h, 1)
                else:
                    i = NCH * s + d
                    if (i + 1) // 2 > 0:
                        e.wait_ge(c.sF1e.h, 16 * ((i + 1) // 2))
                    if i // 2 > 0:
                        e.wait_ge(c.sF1o.h, 16 * (i // 2))
                    _equant(c, e, zc, c.SCR[:, 0:TSW], man=10,
                            s1=c.SCR[:, 512:1024], s2=c.SCR[:, 1024:1536],
                            s3=c.SCR[:, 1536:2048], msk=c.SCR[:, 1536:2048],
                            signed=True).then_inc(c.sEP.h, 1)
                    e.engine_nop().then_inc(c.sF1v.h, 1)
                    c.sF1v.v += 1

        if layer == 1:
            @blk.gpsimd
            def _(e, s=s):
                for d in range(NCH):
                    i = NCH * s + d
                    e.wait_ge(c.sF1v.h, i + 1)
                    fs = c.sF1e if i % 2 == 0 else c.sF1o
                    e.dma_start(out=c.f1sp[:, T * d + TSW * s:
                                           T * d + TSW * (s + 1)],
                                in_=c.SCR[:, 0:TSW]).then_inc(fs.h, 16)
                    fs.v += 16


def _ph_w1_load_l1(c):
    blk = c.blk

    @blk.gpsimd
    def _(e):
        e.wait_ge(c.sEP.h, c.sEP.v)
        for g in range(NCH):
            e.dma_start(out=c.WB[:, FF * g:FF * (g + 1)],
                        in_=c.w1t[128 * g:128 * (g + 1), :]
                        ).then_inc(c.sL1d.h, 16); c.sL1d.v += 16


def _ph_qk(c):
    blk = c.blk
    qt0, qe0, ke0 = c.sQKt.v, c.sQKe.v, c.sEV.v
    bb, l10, qd0 = c.sBCd.v, c.sL1.v, c.sQKd.v

    @blk.gpsimd
    def _(e):
        e.wait_ge(c.sEP.h, c.sEP.v)
        for bk in range(2):
            if bk == 1:
                e.wait_ge(c.sQKt.h, qt0 + 32)
            for g in range(NCH):
                e.dma_start(out=c.QKW[:, 1024 * g:1024 * (g + 1)],
                            in_=c.qkvt[128 * g:128 * (g + 1),
                                       1024 * bk:1024 * (bk + 1)]
                            ).then_inc(c.sQKd.h, 16); c.sQKd.v += 16
        for s in range(NST):
            for pc in range(NCH):
                i = 8 * s + pc
                e.wait_ge(c.sEV.h, ke0 + i + 1)
                ks = c.sKSe if i % 2 == 0 else c.sKSo
                e.dma_start(out=c.qksp[:, NCH * T + T * pc + TSW * s:
                                       NCH * T + T * pc + TSW * (s + 1)],
                            in_=c.QKS[:, TSW * (i % 2):TSW * (i % 2 + 1)]
                            ).then_inc(ks.h, 16); ks.v += 16

    @blk.scalar
    def _(e):
        e.wait_ge(c.sEP.h, c.sEP.v)
        e.wait_ge(c.sBCd.h, c.sBCd.v)
        for bk in range(2):
            for s in range(NST):
                si = 4 * bk + s
                if si >= 1:
                    e.wait_ge(c.sL1.h, l10 + si)
                _emit_bcast(c, e, s, rows=[0, 1])
                if bk == 0:
                    for pc in range(NCH):
                        i = 8 * s + pc
                        e.wait_ge(c.sQKt.h, qt0 + i + 1)
                        qs = c.sQSe if i % 2 == 0 else c.sQSo
                        if i >= 2:
                            e.wait_ge(qs.h, 16 * (i // 2))
                        pb = c.PS[:, 512 * (i % 2):512 * (i % 2 + 1)]
                        e.activation(out=c.QKS[:, TSW * (i % 2):TSW * (i % 2 + 1)],
                                     in_=pb, func=AF.Identity, scale=0.125
                                     ).then_inc(c.sQKe.h, 1)
                        c.sQKe.v += 1
                        e.wait_ge(c.sQKe.h, qe0 + i + 1)
                        e.dma_start(out=c.qksp[:, T * pc + TSW * s:
                                               T * pc + TSW * (s + 1)],
                                    in_=c.QKS[:, TSW * (i % 2):TSW * (i % 2 + 1)]
                                    ).then_inc(qs.h, 16)
                        qs.v += 16

    @blk.vector
    def _(e):
        for bk in range(2):
            for s in range(NST):
                si = 4 * bk + s
                if si == 0:
                    e.wait_ge(c.sF1e.h, c.sF1e.v)
                    e.wait_ge(c.sF1o.h, c.sF1o.v)
                e.wait_ge(c.sBCd.h, bb + 32 * (si + 1))
                if si >= 1:
                    e.wait_ge(c.sQKt.h, qt0 + 8 * si)
                for g in range(NCH):
                    e.tensor_tensor(out=c.SCR[:, 0:TSW],
                                    in0=c.X[:, T * g + TSW * s:T * g + TSW * (s + 1)],
                                    in1=c.BCA[:, 0:TSW], op=ALU.mult)
                    e.tensor_tensor(out=c.XNS[:, TSW * g:TSW * (g + 1)],
                                    in0=c.SCR[:, 0:TSW], in1=c.BCA[:, TSW:2 * TSW],
                                    op=ALU.add)
                e.engine_nop().then_inc(c.sL1.h, 1)
                c.sL1.v += 1
                if bk == 1:
                    for pc in range(NCH):
                        i = 8 * s + pc
                        e.wait_ge(c.sQKt.h, qt0 + 32 + i + 1)
                        ks = c.sKSe if i % 2 == 0 else c.sKSo
                        if i < 2:
                            e.wait_ge(c.sQSe.h, c.sQSe.v)
                            e.wait_ge(c.sQSo.h, c.sQSo.v)
                        if i >= 2:
                            e.wait_ge(ks.h, 16 * (i // 2))
                        pb = c.PS[:, 512 * (i % 2):512 * (i % 2 + 1)]
                        e.tensor_copy(out=c.QKS[:, TSW * (i % 2):TSW * (i % 2 + 1)],
                                      in_=pb).then_inc(c.sEV.h, 1)
                        c.sEV.v += 1

    @blk.tensor
    def _(e):
        for bk in range(2):
            e.wait_ge(c.sQKd.h, qd0 + 128 * (bk + 1))
            for s in range(NST):
                si = 4 * bk + s
                e.wait_ge(c.sL1.h, l10 + si + 1)
                for pc in range(NCH):
                    i = 32 * bk + 8 * s + pc
                    if i >= 2:
                        if i - 2 < 32:
                            e.wait_ge(c.sQKe.h, qe0 + (i - 2) + 1)
                        else:
                            e.wait_ge(c.sEV.h, ke0 + (i - 2 - 32) + 1)
                    pb = c.PS[:, 512 * (i % 2):512 * (i % 2 + 1)]
                    for g in range(NCH):
                        mm = e.matmul(
                            pb,
                            c.QKW[:, 1024 * g + 128 * pc:1024 * g + 128 * (pc + 1)],
                            c.XNS[:, TSW * g:TSW * (g + 1)],
                            start=(g == 0), stop=(g == NCH - 1))
                    mm.then_inc(c.sQKt.h, 1)
                    c.sQKt.v += 1


def _ph_vt(c):
    blk = c.blk
    vd0, vt0, ve0 = c.sVTd.v, c.sVTt.v, c.sVTe.v
    bb, l10 = c.sBCd.v, c.sL1.v

    @blk.gpsimd
    def _(e):
        ap = c.VT[:, :].rearrange("p (t h u) -> p t h u", h=16, u=65)
        e.memset(ap[:, :, :, 64:65], 1.0).then_inc(c.sME.h, 1); c.sME.v += 1
        for tg in range(4):
            for g in range(NCH):
                i = 8 * tg + g
                if i >= 1:
                    e.wait_ge(c.sVTt.h, vt0 + i)
                e.dma_start(out=c.VW[:, :],
                            in_=c.qkvt[128 * g:128 * (g + 1), 2048:3072]
                            ).then_inc(c.sVTd.h, 16); c.sVTd.v += 16

    @blk.scalar
    def _(e):
        e.wait_ge(c.sBCd.h, c.sBCd.v)
        e.wait_ge(c.sL1.h, l10)
        for tg in range(4):
            if tg >= 1:
                e.wait_ge(c.sL1.h, l10 + tg)
            _emit_bcast(c, e, tg, rows=[0, 1])

    @blk.vector
    def _(e):
        e.wait_ge(c.sME.h, c.sME.v)
        for tg in range(4):
            e.wait_ge(c.sBCd.h, bb + 32 * (tg + 1))
            if tg >= 1:
                e.wait_ge(c.sVTt.h, vt0 + 8 * tg)
            for g in range(NCH):
                e.tensor_tensor(out=c.SCR[:, 0:TSW],
                                in0=c.X[:, T * g + TSW * tg:T * g + TSW * (tg + 1)],
                                in1=c.BCA[:, 0:TSW], op=ALU.mult)
                e.tensor_tensor(out=c.XNS[:, TSW * g:TSW * (g + 1)],
                                in0=c.SCR[:, 0:TSW], in1=c.BCA[:, TSW:2 * TSW],
                                op=ALU.add)
            e.engine_nop().then_inc(c.sL1.h, 1)
            c.sL1.v += 1
            e.wait_ge(c.sVTt.h, vt0 + 8 * (tg + 1))
            for tt in range(4 * tg, 4 * tg + 4):
                dst = c.VT[:, 1040 * tt:1040 * (tt + 1)].rearrange(
                    "p (h u) -> p h u", u=65)[:, :, 0:64]
                src = c.PS[:, 1024 * (tt % 4):1024 * (tt % 4) + 1024].rearrange(
                    "p (h u) -> p h u", u=64)
                e.tensor_copy(out=dst, in_=src).then_inc(c.sVTe.h, 1)
                c.sVTe.v += 1

    @blk.tensor
    def _(e):
        for tg in range(4):
            e.wait_ge(c.sL1.h, l10 + tg + 1)
            if tg >= 1:
                e.wait_ge(c.sVTe.h, ve0 + 4 * tg)
            for g in range(NCH):
                i = 8 * tg + g
                e.wait_ge(c.sVTd.h, vd0 + 16 * (i + 1))
                for tt in range(4):
                    st = c.XNS[:, TSW * g + 128 * tt:TSW * g + 128 * (tt + 1)]
                    for h2 in range(2):
                        mm = e.matmul(
                            c.PS[:, 1024 * tt + 512 * h2:1024 * tt + 512 * (h2 + 1)],
                            st, c.VW[:, 512 * h2:512 * (h2 + 1)],
                            start=(g == 0), stop=(g == NCH - 1))
                mm.then_inc(c.sVTt.h, 1)
                c.sVTt.v += 1


def _ph_heads(c):
    blk = c.blk
    ex0, mk0, pc0, pt0, nr0 = (c.sEXP.v, c.sMSKv.v, c.sPVc.v, c.sPVt.v, c.sNRM.v)
    PTICK = sum((4 * s + 5) for s in range(NST)) * 2

    @blk.gpsimd
    def _(e):
        e.wait_ge(c.sVTe.h, c.sVTe.v)
        for ss in (c.sQSe, c.sQSo, c.sKSe, c.sKSo):
            e.wait_ge(ss.h, ss.v)
        e.dma_start(out=c.M4[:, :], in_=c.m4).then_inc(c.sQKd.h, 16)
        c.sQKd.v += 16
        for p in range(NCH):
            if p >= 2:
                e.wait_ge(c.sPVt.h, pt0 + PTICK * (p - 1))
            b = 2 * T * (p % 2)
            qp = c.sQPe if p % 2 == 0 else c.sQPo
            e.dma_start(out=c.QKP[:, b:b + T], in_=c.qksp[:, T * p:T * (p + 1)]
                        ).then_inc(qp.h, 16); qp.v += 16
            e.dma_start(out=c.QKP[:, b + T:b + 2 * T],
                        in_=c.qksp[:, NCH * T + T * p:NCH * T + T * (p + 1)]
                        ).then_inc(qp.h, 16); qp.v += 16

    @blk.tensor
    def _(e):
        e.wait_ge(c.sVTe.h, c.sVTe.v)
        exp_n = mk_n = nrm_n = 0
        for h in range(NH):
            pc, hh = h // 2, h % 2
            b = 2 * T * (pc % 2)
            prow = slice(64 * hh, 64 * hh + 64)
            if hh == 0:
                qp = c.sQPe if pc % 2 == 0 else c.sQPo
                e.wait_ge(qp.h, 32 * (pc // 2 + 1))
            for s in range(NST):
                jmax = 4 * s + 3
                for j in range(jmax + 1):
                    e.wait_ge(c.sEXP.h, ex0 + exp_n + max(0, j - 1))
                    e.matmul(
                        c.PS[:, 512 * (j % 2):512 * (j % 2 + 1)],
                        c.QKP[prow, b + T + 128 * j:b + T + 128 * (j + 1)],
                        c.QKP[prow, b + TSW * s:b + TSW * (s + 1)],
                        start=True, stop=True).then_inc(c.sPVt.h, 1)
                    c.sPVt.v += 1
                    if j >= 4 * s:
                        e.wait_ge(c.sMSKv.h, mk0 + mk_n + (j - 4 * s) + 1)
                    else:
                        e.wait_ge(c.sEXP.h, ex0 + exp_n + j + 1)
                    e.matmul(c.PS[0:65, 1024:1536],
                             c.VT[:, 1040 * j + 65 * h:1040 * j + 65 * (h + 1)],
                             c.PT[:, TSW * (j % 4):TSW * (j % 4 + 1)],
                             start=(j == 0), stop=(j == jmax)
                             ).then_inc(c.sPVc.h, 1)
                    c.sPVc.v += 1
                exp_n += jmax + 1
                mk_n += 4
                e.wait_ge(c.sNRM.h, nr0 + nrm_n + 1)
                e.matmul(c.PS[0:64, 1536:2048], c.ONE64[:, :], c.RR,
                         start=True, stop=True).then_inc(c.sPVt.h, 1)
                c.sPVt.v += 1
                nrm_n += 3
                e.wait_ge(c.sNRM.h, nr0 + nrm_n)

    @blk.scalar
    def _(e):
        sco_n = pv_n = 0
        for h in range(NH):
            for s in range(NST):
                jmax = 4 * s + 3
                for j in range(jmax + 1):
                    e.wait_ge(c.sPVt.h, pt0 + sco_n + j + 1)
                    if j >= 4:
                        e.wait_ge(c.sPVc.h, pc0 + pv_n + j - 3)
                    e.activation(out=c.PT[:, TSW * (j % 4):TSW * (j % 4 + 1)],
                                 in_=c.PS[:, 512 * (j % 2):512 * (j % 2 + 1)],
                                 func=AF.Exp).then_inc(c.sEXP.h, 1)
                    c.sEXP.v += 1
                sco_n += jmax + 2
                pv_n += jmax + 1
                e.wait_ge(c.sPVc.h, pc0 + pv_n)
                e.activation(out=c.RR, in_=c.PS[64:65, 1024:1536], func=AF.Ln)
                e.activation(out=c.RR, in_=c.RR, func=AF.Exp, scale=-1.0
                             ).then_inc(c.sNRM.h, 1)
                c.sNRM.v += 1

    @blk.vector
    def _(e):
        e.wait_ge(c.sQKd.h, c.sQKd.v)
        exp_n = sco_n = 0
        for h in range(NH):
            pc, hh = h // 2, h % 2
            for s in range(NST):
                jmax = 4 * s + 3
                for j in range(4 * s, jmax + 1):
                    r = j - 4 * s
                    e.wait_ge(c.sEXP.h, ex0 + exp_n + j + 1)
                    pt = c.PT[:, TSW * (j % 4):TSW * (j % 4 + 1)]
                    e.tensor_tensor(out=pt, in0=pt,
                                    in1=c.M4[:, TSW * r:TSW * (r + 1)],
                                    op=ALU.mult).then_inc(c.sMSKv.h, 1)
                    c.sMSKv.v += 1
                exp_n += jmax + 1
                sco_n += jmax + 2
                e.wait_ge(c.sPVt.h, pt0 + sco_n)
                dst = c.YA[64 * hh:64 * hh + 64,
                           T * pc + TSW * s:T * pc + TSW * (s + 1)]
                e.tensor_copy(out=dst, in_=c.PS[0:64, 1024:1536]
                              ).then_inc(c.sNRM.h, 1)
                c.sNRM.v += 1
                e.tensor_tensor(out=dst, in0=dst, in1=c.PS[0:64, 1536:2048],
                                op=ALU.mult).then_inc(c.sNRM.h, 1)
                c.sNRM.v += 1


def _ph_proj(c):
    blk = c.blk
    pd0, pvt0, pr0 = c.sPRd.v, c.sPVt.v, c.sPRt.v

    @blk.gpsimd
    def _(e):
        e.wait_ge(c.sNRM.h, c.sNRM.v)
        for g in range(NCH):
            e.dma_start(out=c.PRW[:, D * g:D * (g + 1)],
                        in_=c.projt[128 * g:128 * (g + 1), :]
                        ).then_inc(c.sPRd.h, 16); c.sPRd.v += 16
        e.wait_ge(c.sF1e.h, c.sF1e.v)
        e.wait_ge(c.sF1o.h, c.sF1o.v)
        for s in range(NST):
            for oc in range(NCH):
                i = NCH * s + oc
                if i >= 2:
                    e.wait_ge(c.sPRt.h, pr0 + i - 1)
                fp = c.sFPe if i % 2 == 0 else c.sFPo
                e.dma_start(out=c.BCA[:, 512 * (i % 2):512 * (i % 2 + 1)],
                            in_=c.f1sp[:, T * oc + TSW * s:
                                       T * oc + TSW * (s + 1)]
                            ).then_inc(fp.h, 16); fp.v += 16

    @blk.tensor
    def _(e):
        e.wait_ge(c.sPRd.h, pd0 + 16 * NCH)
        for s in range(NST):
            for oc in range(NCH):
                i = NCH * s + oc
                if i >= 2:
                    e.wait_ge(c.sPRt.h, pr0 + i - 1)
                pb = c.PS[:, 512 * (i % 2):512 * (i % 2 + 1)]
                for g in range(NCH):
                    mm = e.matmul(pb, c.PRW[:, D * g + 128 * oc:D * g + 128 * (oc + 1)],
                                  c.YA[:, T * g + TSW * s:T * g + TSW * (s + 1)],
                                  start=(g == 0), stop=(g == NCH - 1))
                mm.then_inc(c.sPVt.h, 1)
                c.sPVt.v += 1

    @blk.vector
    def _(e):
        for s in range(NST):
            for oc in range(NCH):
                i = NCH * s + oc
                e.wait_ge(c.sPVt.h, pvt0 + i + 1)
                pb = c.PS[:, 512 * (i % 2):512 * (i % 2 + 1)]
                xg = c.X[:, T * oc + TSW * s:T * oc + TSW * (s + 1)]
                e.scalar_tensor_tensor(out=xg, in0=pb, scalar=c.BPJ[:, oc:oc + 1],
                                       in1=xg, op0=ALU.add, op1=ALU.add)
                fp = c.sFPe if i % 2 == 0 else c.sFPo
                e.wait_ge(fp.h, 16 * (i // 2 + 1))
                e.tensor_tensor(out=xg, in0=xg,
                                in1=c.BCA[:, 512 * (i % 2):512 * (i % 2 + 1)],
                                op=ALU.add).then_inc(c.sPRt.h, 1)
                c.sPRt.v += 1


def _ph_out(c, wait):
    blk = c.blk

    @blk.sync
    def _(e):
        for s in wait:
            e.wait_ge(s.h, s.v)
        e.dma_start(out=c.out, in_=c.X[:, :]).then_inc(c.sOUd.h, 16)
        c.sOUd.v += 16
    @blk.gpsimd
    def _(e):
        e.wait_ge(c.sOUd.h, c.sOUd.v)


def host_prep(L0_attn_x, L0_attn_w_qkv, L0_attn_w_o, L0_ffn_ln_gamma,
              L0_ffn_ln_beta, L0_ffn_fc1_q, L0_ffn_fc1_scales, L0_ffn_fc1_bias,
              L0_ffn_fc2_q, L0_ffn_fc2_scales, L0_ffn_fc2_bias, L1_attn_w_qkv,
              L1_attn_w_proj, L1_attn_b_proj, L1_ffn_w1, L1_ffn_b1, L1_ffn_w2,
              L1_ffn_b2):
    import ml_dtypes
    A = lambda v: np.ascontiguousarray(v)
    f32 = np.float32
    cm = lambda vec, k: A(np.asarray(vec, f32).reshape(k, 128).T)

    sh = {}
    sh["wvt"] = A(np.asarray(L0_attn_w_qkv, f32)[2048:3072, :].T)
    sh["wot"] = A(np.asarray(L0_attn_w_o, f32).T)
    sh["fc1q"] = A(np.asarray(L0_ffn_fc1_q, np.int32))
    sh["fc1sb"] = A(np.broadcast_to(
        np.asarray(L0_ffn_fc1_scales, f32).reshape(1, 8 * FF), (128, 8 * FF)))
    sh["fc2q"] = A(np.asarray(L0_ffn_fc2_q, np.int32))
    sh["fc2sb"] = A(np.broadcast_to(
        np.asarray(L0_ffn_fc2_scales, f32).reshape(1, 32 * D), (128, 32 * D)))
    sh["gam"] = cm(L0_ffn_ln_gamma, NCH)
    sh["bet"] = cm(L0_ffn_ln_beta, NCH)
    sh["f1b"] = cm(L0_ffn_fc1_bias, NFO)
    sh["f2b"] = cm(L0_ffn_fc2_bias, NCH)
    wq = np.asarray(L1_attn_w_qkv, f32)
    qkv = np.concatenate([wq[:, 0:64, :].reshape(NH * 64, D),
                          wq[:, 64:128, :].reshape(NH * 64, D),
                          wq[:, 128:192, :].reshape(NH * 64, D)], axis=0)
    sh["qkvt"] = A(qkv.T)
    sh["projt"] = A(np.asarray(L1_attn_w_proj, f32).T)
    sh["bpj"] = cm(L1_attn_b_proj, NCH)
    sh["w1t"] = A(np.asarray(L1_ffn_w1, f32).T)
    sh["b1l1"] = cm(L1_ffn_b1, NFO)
    sh["w2t"] = A(np.asarray(L1_ffn_w2, f32).T)
    sh["b2l1"] = cm(L1_ffn_b2, NCH)
    cvec = np.zeros((128, 8), f32)
    cvec[:, 0] = 0.5 + 2.0 ** -20
    cvec[:, 1] = EPS
    sh["cvec"] = cvec
    p = np.arange(128)[:, None]
    qq = np.arange(TSW)[None, :]
    sh["m4"] = A(np.concatenate([(qq >= 128 * r + p) for r in range(4)],
                                axis=1).astype(ml_dtypes.bfloat16))
    x = np.asarray(L0_attn_x, f32)
    in_maps = []
    for b in range(x.shape[0]):
        m = dict(sh)
        m["x"] = A(x[b].T.reshape(NCH, 128, T).transpose(1, 0, 2
                                                         ).reshape(128, NCH * T))
        in_maps.append(m)
    return in_maps


def gather_out(results):
    outs = []
    for r in results:
        o = np.asarray(r["out"]).reshape(128, NCH, T).transpose(1, 0, 2
                                                                ).reshape(D, T)
        outs.append(o.T)
    return np.stack(outs, axis=0)


_NC_CACHE = {}


def kernel(**inputs):
    in_maps = host_prep(**inputs)
    if "nc" not in _NC_CACHE:
        _NC_CACHE["nc"] = build_nc(stop="full")
    res = run_bass_kernel_spmd(_NC_CACHE["nc"], in_maps, core_ids=list(range(8)))
    out = gather_out(res.results)
    return np.ascontiguousarray(out, dtype=np.float32)


# version 10
# speedup vs baseline: 1.0601x; 1.0601x over previous
import contextlib

import numpy as np

import concourse.bass as bass
import concourse.mybir as mybir
from concourse.bass_utils import run_bass_kernel_spmd

F32 = mybir.dt.float32
BF16 = mybir.dt.bfloat16
I32 = mybir.dt.int32
U8 = mybir.dt.uint8
ALU = mybir.AluOpType
AF = mybir.ActivationFunctionType
AX = mybir.AxisListType

D = 1024
T = 2048
FF = 4096
NCH = 8
NST = 4
TSW = 512
NH = 16
NFO = 32
EPS = 1e-5


class Ctx:
    pass


_EMIT_OPS = {
    "tensor_tensor", "tensor_scalar", "tensor_copy", "tensor_add",
    "tensor_sub", "tensor_mul", "tensor_max", "scalar_tensor_tensor",
    "tensor_single_scalar", "tensor_scalar_add", "tensor_scalar_sub",
    "tensor_scalar_mul", "tensor_scalar_max", "tensor_scalar_min",
    "tensor_relu", "tensor_reduce", "reduce_max", "reduce_sum", "select",
    "copy_predicated", "activation", "square", "sqrt", "mul", "copy", "add",
    "memset", "memzero", "engine_nop", "reciprocal",
}

_INC_REC = {}
_SEM_TOT = {}
_ORIG_THEN_INC = bass.BassInstruction.then_inc


def _rec_then_inc(self, sem, n):
    tot = _SEM_TOT.get(id(sem), 0) + n
    _SEM_TOT[id(sem)] = tot
    _INC_REC[id(self)] = (self, sem, tot)
    return _ORIG_THEN_INC(self, sem, n)


bass.BassInstruction.then_inc = _rec_then_inc


class _Chained:

    def __init__(self, eng, seq_sem, nop_fn=None):
        self._eng = eng
        self._seq = seq_sem
        self._nop = nop_fn
        self._prev = None

    def wait_ge(self, sem, val):
        return self._eng.wait_ge(sem, val)

    def _edge(self):
        if self._prev is None:
            return
        rec = _INC_REC.get(id(self._prev))
        if rec is None or rec[1] is not self._seq:
            if rec is None:
                tgt = self._prev
            elif self._nop is not None:
                tgt = self._nop(self._eng)
            else:
                tgt = self._eng.engine_nop()
            _rec_then_inc(tgt, self._seq, 1)
            rec = _INC_REC[id(tgt)]
        self._eng.wait_ge(rec[1], rec[2])

    def __getattr__(self, name):
        f = getattr(self._eng, name)
        if name == "dma_start":
            def gd(*a, **k):
                self._edge()
                ins = f(*a, **k)
                self._prev = None
                return ins
            return gd
        if name not in _EMIT_OPS:
            return f

        def g(*a, **k):
            self._edge()
            ins = f(*a, **k)
            self._prev = ins
            return ins

        return g




def build_nc(stop="full"):
    _INC_REC.clear()
    _SEM_TOT.clear()
    nc = bass.Bass("TRN2", target_bir_lowering=False)
    c = Ctx()
    c.nc = nc
    c.stop = stop

    dram = lambda n, sh, ty, kind="ExternalInput": nc.dram_tensor(n, sh, ty, kind=kind).ap()
    c.x = dram("x", [128, NCH * T], F32)
    c.wvt = dram("wvt", [D, D], F32)
    c.wot = dram("wot", [D, D], F32)
    c.fc1q = dram("fc1q", [D, FF], I32)
    c.fc1sb = dram("fc1sb", [128, 8 * FF], F32)
    c.fc2q = dram("fc2q", [FF, D], I32)
    c.fc2sb = dram("fc2sb", [128, 32 * D], F32)
    c.gam = dram("gam", [128, NCH], F32)
    c.bet = dram("bet", [128, NCH], F32)
    c.f1b = dram("f1b", [128, NFO], F32)
    c.f2b = dram("f2b", [128, NCH], F32)
    c.qkvt = dram("qkvt", [D, 3 * D], F32)
    c.projt = dram("projt", [D, D], F32)
    c.bpj = dram("bpj", [128, NCH], F32)
    c.w1t = dram("w1t", [D, FF], F32)
    c.b1l1 = dram("b1l1", [128, NFO], F32)
    c.w2t = dram("w2t", [FF, D], F32)
    c.b2l1 = dram("b2l1", [128, NCH], F32)
    c.cvec = dram("cvec", [128, 8], F32)
    c.m4 = dram("m4", [128, 4 * TSW], BF16)
    c.out = dram("out", [128, NCH * T], F32, "ExternalOutput")
    c.w2sp = dram("w2sp", [FF, D], BF16, "Internal")
    c.wqsp = dram("wqsp", [FF, D], BF16, "Internal")
    c.rowsp = dram("rowsp", [4, T], F32, "Internal")
    c.rowspb = dram("rowspb", [1, D], BF16, "Internal")
    c.qksp = dram("qksp", [128, 2 * NCH * T], BF16, "Internal")
    c.f1sp = dram("f1sp", [128, NCH * T], F32, "Internal")

    with contextlib.ExitStack() as es:
        _body(c, es)
    return nc


def _body(c, es):
    nc = c.nc
    sb = lambda n, sh, ty: es.enter_context(nc.sbuf_tensor(n, sh, ty))

    c.X = sb("X", [128, NCH * T], F32)
    c.SCR = sb("SCR", [128, 3072], F32)
    c.YA = sb("YA", [128, NFO * TSW], BF16)
    c.CON = sb("CON", [128, 8], F32)
    c.ONE = sb("ONE", [128, 1], BF16)
    c.ONE64 = sb("ONE64", [1, 64], F32)
    c.GAM = sb("GAM", [128, NCH], F32)
    c.BET = sb("BET", [128, NCH], F32)
    c.F1B = sb("F1B", [128, NFO], F32)
    c.F2B = sb("F2B", [128, NCH], F32)
    c.C0 = sb("C0", [128, NCH], F32)
    c.B1L = sb("B1L", [128, NFO], F32)
    c.BPJ = sb("BPJ", [128, NCH], F32)
    c.BQ = sb("BQ", [128, NCH], F32)
    c.B2L = sb("B2L", [128, NCH], F32)
    c.XS = sb("XS", [128, NCH], F32)
    c.XSB = sb("XSB", [128, NCH], BF16)
    c.VSB = sb("VSB", [128, NCH], BF16)
    c.SSB = sb("SSB", [128, 1], F32)
    c.BCA = sb("BCA", [128, 3 * TSW], F32)
    c.RR = c.SCR[0:1, 2048:2560]

    c.PS = es.enter_context(nc.psum_tensor("PS", [128, 4096], F32))

    for name in ["sINd", "sCN", "sSPd", "sWQ", "sWQd", "sST", "sMM", "sVR",
                 "sBCd", "sXSv", "sXSd", "sA0d", "sA0t", "sA0v", "sDQ",
                 "sDQd", "sSCd", "sW2", "sW2d", "sHB", "sEV", "sEP", "sRBd",
                 "sRBt", "sL1", "sL1d", "sQKd", "sQKt", "sQKe", "sVTd",
                 "sVTt", "sVTe", "sEXP", "sMSKv", "sPVc", "sPVt", "sNRM",
                 "sPRd", "sPRt", "sOUd", "sME", "sSPe", "sSPo", "sDQe",
                 "sDQo", "sRBe", "sRBo", "sQPe", "sQPo", "sQSe", "sQSo",
                 "sKSe", "sKSo", "sF1e", "sF1o", "sF1v", "sFPe", "sFPo"]:
        s = Ctx(); s.h = es.enter_context(nc.semaphore(name)); s.v = 0
        setattr(c, name, s)

    for name in ["qV", "qS", "qG", "qY"]:
        s = Ctx(); s.h = es.enter_context(nc.semaphore(name)); s.v = 0
        setattr(c, name, s)
    c.blk = es.enter_context(nc.Block())
    c._chains = {}

    def _wrap(engname, e):
        key = (engname,)
        if key not in c._chains:
            seq = {"vector": c.qV, "scalar": c.qS, "gpsimd": c.qG,
                   "sync": c.qY}[engname]
            nop = None
            if engname == "scalar":
                nop = lambda eng: eng.copy(out=c.CON[:, 7:8], in_=c.CON[:, 7:8])
            c._chains[key] = _Chained(e, seq.h, nop)
        ch = c._chains[key]
        ch._eng = e
        return ch
    c.wrap = _wrap

    class _BlkProxy:
        def __init__(self, blk):
            self._blk = blk

        def _mk(self, engname):
            dec = getattr(self._blk, engname)

            def deco(fn):
                return dec(lambda e, fn=fn: fn(c.wrap(engname, e)))
            return deco

        @property
        def vector(self):
            return self._mk("vector")

        @property
        def scalar(self):
            return self._mk("scalar")

        @property
        def gpsimd(self):
            return getattr(self._blk, "gpsimd")

        @property
        def sync(self):
            return getattr(self._blk, "sync")

        @property
        def tensor(self):
            return self._blk.tensor

    c.blk = _BlkProxy(c.blk)

    _ph_loads(c)
    _ph_consts(c)
    if c.stop == "consts":
        _ph_out(c, wait=[c.sCN])
        return

    with contextlib.ExitStack() as esA:
        c.WB = esA.enter_context(nc.sbuf_tensor("WB", [128, NCH * FF], BF16))
        c.WQB = esA.enter_context(nc.sbuf_tensor("WQB", [128, D], BF16))
        c.SCW = esA.enter_context(nc.sbuf_tensor("SCW", [128, 2 * D], F32))
        with contextlib.ExitStack() as esd:
            c.QIB = esd.enter_context(nc.sbuf_tensor("QIB", [128, FF], BF16))
            c.SCB = esd.enter_context(nc.sbuf_tensor("SCB", [128, FF // 2], F32))
            if c.stop == "wq":
                _ph_out(c, wait=[c.sWQd])
                return
            _ph_w1_dequant(c)
            _ph_w2_dequant_spill(c)
            if c.stop == "deq":
                _ph_out(c, wait=[c.sDQ, c.sW2d])
                return
            with contextlib.ExitStack() as esb:
                c.STAT = esb.enter_context(nc.sbuf_tensor("STAT", [128, 160], F32))
                _ph_ln(c, layer=0)
                if c.stop == "ln":
                    _ph_out(c, wait=[c.sBCd])
                    return
                _ph_xsum_a0(c)
                if c.stop == "a0":
                    _ph_out(c, wait=[c.sA0v])
                    return
        c.W2B = esA.enter_context(nc.sbuf_tensor("W2B", [128, 4 * D], BF16))
        c.HBB = esA.enter_context(nc.sbuf_tensor("HBB", [128, NCH * TSW], BF16))
        _ph_wq_spill(c)
        _ph_ffn(c, layer=0, nst=(1 if c.stop == "s1" else NST))
        if c.stop == "s1":
            _ph_out(c, wait=[c.sEP])
            return
        if c.stop == "l0":
            _ph_out(c, wait=[c.sEP])
            return
        with contextlib.ExitStack() as esb:
            c.STAT = esb.enter_context(nc.sbuf_tensor("STAT1", [128, 160], F32))
            _ph_ln(c, layer=1)
        _ph_w1_load_l1(c)
        _ph_ffn(c, layer=1)
    if c.stop == "ffn1":
        _ph_out(c, wait=[c.sEP])
        return
    with contextlib.ExitStack() as esq:
        c.VT = esq.enter_context(
            nc.sbuf_tensor("VT", [128, 16 * 1040], BF16))
        with contextlib.ExitStack() as esk:
            c.XNS = esk.enter_context(
                nc.sbuf_tensor("XNS", [128, NCH * TSW], BF16))
            c.QKS = esk.enter_context(
                nc.sbuf_tensor("QKS", [128, 2 * TSW], BF16))
            with contextlib.ExitStack() as esw:
                c.QKW = esw.enter_context(
                    nc.sbuf_tensor("QKW", [128, NCH * D], BF16))
                _ph_qk(c)
            with contextlib.ExitStack() as esv:
                c.VW = esv.enter_context(nc.sbuf_tensor("VW", [128, D], BF16))
                _ph_vt(c)
        c.M4 = esq.enter_context(nc.sbuf_tensor("M4", [128, 4 * TSW], BF16))
        c.PT = esq.enter_context(nc.sbuf_tensor("PT", [128, 4 * TSW], BF16))
        c.QKP = esq.enter_context(nc.sbuf_tensor("QKP", [128, 4 * T], BF16))
        _ph_heads(c)
        with contextlib.ExitStack() as esp:
            c.PRW = esp.enter_context(nc.sbuf_tensor("PRW", [128, NCH * D], BF16))
            _ph_proj(c)
    _ph_out(c, wait=[c.sPRt])


def _emit_bcast(c, e, s, rows):
    for i in rows:
        e.dma_start(out=c.BCA[:, TSW * i:TSW * (i + 1)],
                    in_=c.rowsp[i:i + 1, TSW * s:TSW * (s + 1)].broadcast_to([128, TSW])
                    ).then_inc(c.sBCd.h, 16)
        c.sBCd.v += 16


def _equant(c, e, src, dst, man, s1, s2, msk, signed, s3=None):
    if man == 3:
        Map, Cap, thr = c.CON[:, 3:4], c.CON[:, 4:5], 2.0 ** -5
    else:
        Map, Cap, thr = c.CON[:, 5:6], c.CON[:, 6:7], 2.0 ** -13
    x = src
    if signed:
        e.tensor_tensor(out=s3, in0=x, in1=x, op=ALU.mult)
        e.tensor_scalar(out=msk, in0=s3, scalar1=float(thr * thr), scalar2=None,
                        op0=ALU.is_ge)
    else:
        e.tensor_scalar(out=msk, in0=x, scalar1=float(thr), scalar2=None,
                        op0=ALU.is_ge)
    e.tensor_scalar(out=s1, in0=x, scalar1=Map, scalar2=None, op0=ALU.mult)
    e.tensor_tensor(out=s2, in0=s1, in1=x, op=ALU.subtract)
    e.tensor_tensor(out=s1, in0=s1, in1=s2, op=ALU.subtract)
    e.tensor_scalar(out=s2, in0=x, scalar1=Cap, scalar2=Cap,
                    op0=ALU.add, op1=ALU.subtract)
    e.tensor_tensor(out=s1, in0=s1, in1=s2, op=ALU.subtract)
    e.tensor_tensor(out=s1, in0=s1, in1=msk, op=ALU.mult)
    return e.tensor_tensor(out=dst, in0=s2, in1=s1, op=ALU.add)


def _ph_loads(c):
    blk = c.blk

    @blk.sync
    def _(e):
        e.dma_start(out=c.X[:, :], in_=c.x).then_inc(c.sINd.h, 16); c.sINd.v += 16
        for dst, src in [(c.GAM, c.gam), (c.BET, c.bet), (c.F1B, c.f1b),
                         (c.F2B, c.f2b), (c.B1L, c.b1l1), (c.BPJ, c.bpj),
                         (c.B2L, c.b2l1), (c.CON, c.cvec)]:
            e.dma_start(out=dst[:, :], in_=src).then_inc(c.sINd.h, 16)
            c.sINd.v += 16

    @blk.gpsimd
    def _(e):
        e.memset(c.ONE[:, :], 1.0)
        e.memset(c.ONE64[:, :], 1.0).then_inc(c.sME.h, 1); c.sME.v += 1

    c.nc.const_aps.register(F32, EPS, c.CON[:, 1:2])


def _ph_consts(c):
    blk = c.blk
    C52 = 1.5 * 2.0 ** 52

    @blk.vector
    def _(e):
        e.wait_ge(c.sINd.h, c.sINd.v)
        e.wait_ge(c.sME.h, 1)
        e.tensor_scalar(out=c.CON[:, 2:3], in0=c.CON[:, 0:1], scalar1=C52,
                        scalar2=C52, op0=ALU.add, op1=ALU.subtract)
        e.tensor_copy(out=c.SCR[:, 0:512], in_=c.X[:, 0:512])
        e.tensor_scalar(out=c.CON[:, 3:4], in0=c.CON[:, 2:3],
                        scalar1=float(2.0 ** 49 - 2.0 ** 20),
                        scalar2=float(2.0 ** 20 + 1.0), op0=ALU.mult, op1=ALU.add)
        e.tensor_scalar(out=c.CON[:, 4:5], in0=c.CON[:, 2:3],
                        scalar1=float(1.5 * 2.0 ** 43 - 1.5 * 2.0 ** 14),
                        scalar2=float(1.5 * 2.0 ** 14), op0=ALU.mult, op1=ALU.add)
        e.tensor_scalar(out=c.CON[:, 5:6], in0=c.CON[:, 2:3],
                        scalar1=float(2.0 ** 42 - 2.0 ** 13),
                        scalar2=float(2.0 ** 13 + 1.0), op0=ALU.mult, op1=ALU.add)
        e.tensor_scalar(out=c.CON[:, 6:7], in0=c.CON[:, 2:3],
                        scalar1=float(1.5 * 2.0 ** 28 - 0.75),
                        scalar2=0.75, op0=ALU.mult, op1=ALU.add)
        e.tensor_copy(out=c.SCR[:, 512:1024], in_=c.X[:, 0:512])
        _equant(c, e, c.B2L[:, :], c.BQ[:, :], man=10,
                s1=c.SCR[:, 8:16], s2=c.SCR[:, 16:24], s3=c.SCR[:, 0:8],
                msk=c.SCR[:, 24:32], signed=True).then_inc(c.sCN.h, 1)
        c.sCN.v += 1


def _ph_wq_spill(c):
    blk = c.blk
    wq0 = c.sWQ.v
    c._wq0 = wq0

    @blk.gpsimd
    def _(e):
        for g in range(34):
            if g >= 2:
                e.wait_ge(c.sWQ.h, wq0 + g - 1)
            if g < 32:
                sb_ = c.sSPe if g % 2 == 0 else c.sSPo
                e.dma_start(out=c.SCW[:, 1024 * (g % 2):1024 * (g % 2) + D],
                            in_=c.w2t[128 * g:128 * (g + 1), :]
                            ).then_inc(sb_.h, 16); sb_.v += 16
            if g >= 2:
                e.dma_start(out=c.wqsp[128 * (g - 2):128 * (g - 1), :],
                            in_=c.WQB[:, :]
                            ).then_inc(c.sWQd.h, 16); c.sWQd.v += 16


def _emit_wq_tiles(c, e, s):
    e.wait_ge(c.sHB.h, c.sHB.v)
    for g in range(8 * s, 8 * s + 8):
        sb_ = c.sSPe if g % 2 == 0 else c.sSPo
        e.wait_ge(sb_.h, 16 * (g // 2 + 1))
        if g >= 1:
            e.wait_ge(c.sWQd.h, 16 * g)
        bo = 1024 * (g % 2)
        _equant(c, e, c.SCW[:, bo:bo + D], c.WQB[:, :], man=3,
                s1=c.SCR[:, 0:D], s2=c.SCR[:, 1024:1024 + D],
                s3=c.SCR[:, 0:D], msk=c.SCR[:, 2048:2048 + D], signed=True
                ).then_inc(c.sWQ.h, 1); c.sWQ.v += 1


def _ph_w1_dequant(c):
    blk = c.blk
    dq0, dd0, sc0 = c.sDQ.v, c.sDQd.v, c.sSCd.v

    @blk.gpsimd
    def _(e):
        for g in range(NCH):
            if g >= 1:
                e.wait_ge(c.sDQ.h, dq0 + 2 * g)
            e.dma_start(out=c.QIB[:, :], in_=c.fc1q[128 * g:128 * (g + 1), :]
                        ).then_inc(c.sDQd.h, 16); c.sDQd.v += 16

    @blk.scalar
    def _(e):
        for g in range(NCH):
            for hf in range(2):
                if g + hf >= 1:
                    e.wait_ge(c.sDQ.h, dq0 + 2 * g + hf)
                e.dma_start(out=c.SCB[:, :],
                            in_=c.fc1sb[:, FF * g + 2048 * hf:FF * g + 2048 * (hf + 1)]
                            ).then_inc(c.sSCd.h, 16); c.sSCd.v += 16

    @blk.vector
    def _(e):
        for g in range(NCH):
            e.wait_ge(c.sDQd.h, dd0 + 16 * (g + 1))
            for hf in range(2):
                e.wait_ge(c.sSCd.h, sc0 + 16 * (2 * g + hf + 1))
                e.scalar_tensor_tensor(
                    out=c.WB[:, FF * g + 2048 * hf:FF * g + 2048 * (hf + 1)],
                    in0=c.QIB[:, 2048 * hf:2048 * (hf + 1)], scalar=-128.0,
                    in1=c.SCB[:, :], op0=ALU.add, op1=ALU.mult
                    ).then_inc(c.sDQ.h, 1); c.sDQ.v += 1


def _ph_w2_dequant_spill(c):
    blk = c.blk
    w20, wd0, sc0 = c.sW2.v, c.sW2d.v, c.sSCd.v

    @blk.gpsimd
    def _(e):
        for g in range(33):
            if g < 32:
                if g < 2:
                    e.wait_ge(c.sDQ.h, c.sDQ.v)
                if g >= 2:
                    e.wait_ge(c.sW2.h, w20 + g - 1)
                sb_ = c.sDQe if g % 2 == 0 else c.sDQo
                e.dma_start(out=c.QIB[:, D * (g % 2):D * (g % 2) + D],
                            in_=c.fc2q[128 * g:128 * (g + 1), :]
                            ).then_inc(sb_.h, 16); sb_.v += 16
            if g >= 1:
                e.wait_ge(c.sW2.h, w20 + g)
                e.dma_start(out=c.w2sp[128 * (g - 1):128 * g, :], in_=c.WQB[:, :]
                            ).then_inc(c.sW2d.h, 16); c.sW2d.v += 16

    @blk.scalar
    def _(e):
        for g in range(32):
            if g == 0:
                e.wait_ge(c.sDQ.h, c.sDQ.v)
            if g >= 1:
                e.wait_ge(c.sW2.h, w20 + g)
            e.dma_start(out=c.SCB[:, 0:D], in_=c.fc2sb[:, D * g:D * (g + 1)]
                        ).then_inc(c.sSCd.h, 16); c.sSCd.v += 16

    @blk.vector
    def _(e):
        for g in range(32):
            sb_ = c.sDQe if g % 2 == 0 else c.sDQo
            e.wait_ge(sb_.h, 16 * (g // 2 + 1))
            e.wait_ge(c.sSCd.h, sc0 + 16 * (g + 1))
            if g == 0:
                e.wait_ge(c.sWQd.h, c.sWQd.v)
            if g >= 1:
                e.wait_ge(c.sW2d.h, wd0 + 16 * g)
            e.scalar_tensor_tensor(out=c.WQB[:, :],
                                   in0=c.QIB[:, D * (g % 2):D * (g % 2) + D],
                                   scalar=-128.0, in1=c.SCB[:, 0:D],
                                   op0=ALU.add, op1=ALU.mult
                                   ).then_inc(c.sW2.h, 1); c.sW2.v += 1


def _ph_ln(c, layer):
    blk = c.blk
    stat = c.STAT
    xb = c.YA[:, 0:T]
    sq = c.YA[:, T:2 * T]
    st0, mm0, vr0, bc0 = c.sST.v, c.sMM.v, c.sVR.v, c.sBCd.v
    COL = lambda i: slice(16 * i, 16 * (i + 1))

    @blk.vector
    def _(e):
        if layer == 1:
            e.wait_ge(c.sEP.h, c.sEP.v)
        else:
            e.wait_ge(c.sCN.h, c.sCN.v)
        for g in range(NCH):
            if g >= 1:
                e.wait_ge(c.sMM.h, mm0 + g)
            xg = c.X[:, T * g:T * (g + 1)]
            e.tensor_copy(out=xb, in_=xg)
            e.tensor_tensor(out=sq, in0=xg, in1=xg, op=ALU.mult
                            ).then_inc(c.sST.h, 1)
            c.sST.v += 1

    @blk.tensor
    def _(e):
        for g in range(NCH):
            e.wait_ge(c.sST.h, st0 + g + 1)
            for s in range(NST):
                sl = slice(TSW * s, TSW * (s + 1))
                e.matmul(c.PS[0:1, sl], c.ONE[:, :], xb[:, sl],
                         start=(g == 0), stop=(g == NCH - 1))
                mm = e.matmul(c.PS[0:1, 2048 + TSW * s:2048 + TSW * (s + 1)],
                              c.ONE[:, :], sq[:, sl],
                              start=(g == 0), stop=(g == NCH - 1))
            mm.then_inc(c.sMM.h, 1)
            c.sMM.v += 1

    @blk.vector
    def _(e):
        e.wait_ge(c.sMM.h, mm0 + NCH)
        e.tensor_copy(out=c.SCR[0:1, 0:T], in_=c.PS[0:1, 0:T])
        e.tensor_copy(out=c.SCR[32:33, 0:T], in_=c.PS[0:1, 2048:2048 + T]
                      ).then_inc(c.sVR.h, 1)
        c.sVR.v += 1

    b0 = c.sBCd.v

    @blk.sync
    def _(e):
        e.wait_ge(c.sVR.h, vr0 + 1)
        e.dma_start(out=c.rowsp[0:1, :], in_=c.SCR[0:1, 0:T]
                    ).then_inc(c.sBCd.h, 16); c.sBCd.v += 16
        e.dma_start(out=c.rowsp[1:2, :], in_=c.SCR[32:33, 0:T]
                    ).then_inc(c.sBCd.h, 16); c.sBCd.v += 16
        e.wait_ge(c.sBCd.h, b0 + 32)
        e.dma_start(out=stat[:, COL(0)],
                    in_=c.rowsp[0:1, :].rearrange("o (t p) -> (o p) t", p=128),
                    allow_slow_non_contiguous=True
                    ).then_inc(c.sBCd.h, 16); c.sBCd.v += 16
        e.dma_start(out=stat[:, COL(1)],
                    in_=c.rowsp[1:2, :].rearrange("o (t p) -> (o p) t", p=128),
                    allow_slow_non_contiguous=True
                    ).then_inc(c.sBCd.h, 16); c.sBCd.v += 16

    b1 = c.sBCd.v

    @blk.vector
    def _(e):
        e.wait_ge(c.sBCd.h, b1)
        spc = lambda: e.tensor_copy(out=c.SCR[:, 1024:1536], in_=c.X[:, 0:512])
        e.tensor_scalar(out=stat[:, COL(2)], in0=stat[:, COL(0)],
                        scalar1=1.0 / D, scalar2=None, op0=ALU.mult)
        e.tensor_scalar(out=stat[:, COL(3)], in0=stat[:, COL(1)],
                        scalar1=1.0 / D, scalar2=None, op0=ALU.mult); spc()
        e.tensor_tensor(out=stat[:, COL(8)], in0=stat[:, COL(2)],
                        in1=stat[:, COL(2)], op=ALU.mult); spc()
        e.tensor_tensor(out=stat[:, COL(3)], in0=stat[:, COL(3)],
                        in1=stat[:, COL(8)], op=ALU.subtract
                        ).then_inc(c.sVR.h, 1)
        c.sVR.v += 1

    @blk.scalar
    def _(e):
        e.wait_ge(c.sVR.h, vr0 + 2)
        e.activation(out=stat[:, COL(8)], in_=stat[:, COL(3)], func=AF.Ln,
                     bias=EPS)
        e.activation(out=c.SCR[:, 1536:2048], in_=c.X[:, 0:512],
                     func=AF.Identity)
        e.activation(out=stat[:, COL(4)], in_=stat[:, COL(8)], func=AF.Exp,
                     scale=-0.5).then_inc(c.sVR.h, 1)
        c.sVR.v += 1

    @blk.vector
    def _(e, layer=layer):
        e.wait_ge(c.sVR.h, vr0 + 3)
        spc = lambda: e.tensor_copy(out=c.SCR[:, 1024:1536], in_=c.X[:, 0:512])
        e.scalar_tensor_tensor(out=stat[:, COL(5)], in0=stat[:, COL(2)],
                               scalar=-1.0, in1=stat[:, COL(4)],
                               op0=ALU.mult, op1=ALU.mult)
        if layer == 0:
            e.tensor_tensor(out=stat[:, COL(6)], in0=stat[:, COL(3)],
                            in1=stat[:, COL(4)], op=ALU.mult); spc()
            e.tensor_tensor(out=stat[:, COL(6)], in0=stat[:, COL(6)],
                            in1=stat[:, COL(4)], op=ALU.mult)
            spc()
            e.tensor_copy(out=c.YA[0:128, 2 * T:2 * T + 16], in_=stat[:, COL(5)])
        e.engine_nop().then_inc(c.sVR.h, 1)
        c.sVR.v += 1

    if layer == 0:
        @blk.scalar
        def _(e):
            e.wait_ge(c.sVR.h, vr0 + 4)
            e.activation(out=stat[:, COL(8)], in_=stat[:, COL(6)],
                         func=AF.Ln, bias=EPS)
            e.activation(out=c.SCR[:, 1536:2048], in_=c.X[:, 0:512],
                         func=AF.Identity)
            e.activation(out=stat[:, COL(7)], in_=stat[:, COL(8)],
                         func=AF.Exp, scale=-0.5).then_inc(c.sVR.h, 1)
            c.sVR.v += 1

        smb = c.sMM.v

        @blk.tensor
        def _(e):
            e.wait_ge(c.sVR.h, vr0 + 4)
            e.matmul(c.PS[0:1, 3072:3088], c.ONE[:, :],
                     c.YA[0:128, 2 * T:2 * T + 16], start=True, stop=True
                     ).then_inc(c.sMM.h, 1)
            c.sMM.v += 1

        @blk.vector
        def _(e, smb=smb):
            e.wait_ge(c.sMM.h, smb + 1)
            e.reduce_sum(out=c.SCR[0:1, 2560:2561], in_=c.PS[0:1, 3072:3088],
                         axis=AX.X).then_inc(c.sVR.h, 1)
            c.sVR.v += 1

    vfin = c.sVR.v
    b2 = c.sBCd.v

    @blk.sync
    def _(e, layer=layer):
        e.wait_ge(c.sVR.h, vfin)
        e.dma_start(out=c.rowsp[0:1, :].rearrange("o (t p) -> (o p) t", p=128),
                    in_=stat[:, COL(4)],
                    allow_slow_non_contiguous=True
                    ).then_inc(c.sBCd.h, 16); c.sBCd.v += 16
        e.dma_start(out=c.rowsp[1:2, :].rearrange("o (t p) -> (o p) t", p=128),
                    in_=stat[:, COL(5)],
                    allow_slow_non_contiguous=True
                    ).then_inc(c.sBCd.h, 16); c.sBCd.v += 16
        if layer == 0:
            e.dma_start(out=c.rowsp[2:3, :].rearrange(
                            "o (t p) -> (o p) t", p=128),
                        in_=stat[:, COL(7)],
                        allow_slow_non_contiguous=True
                        ).then_inc(c.sBCd.h, 16); c.sBCd.v += 16
            e.dma_start(out=c.rowsp[3:4, 0:1], in_=c.SCR[0:1, 2560:2561]
                        ).then_inc(c.sBCd.h, 16); c.sBCd.v += 16
        e.wait_ge(c.sBCd.h, c.sBCd.v)
        if layer == 0:
            e.dma_start(out=c.SSB[:, :],
                        in_=c.rowsp[3:4, 0:1].broadcast_to([128, 1])
                        ).then_inc(c.sBCd.h, 16); c.sBCd.v += 16


def _ph_xsum_a0(c):
    blk = c.blk
    xs0 = c.sXSv.v
    bb = c.sBCd.v

    @blk.sync
    def _(e):
        for s in range(NST):
            if s >= 1:
                e.wait_ge(c.sXSv.h, xs0 + s)
            _emit_bcast(c, e, s, rows=[0])

    @blk.vector
    def _(e):
        for s in range(NST):
            e.wait_ge(c.sBCd.h, bb + 16 * (s + 1))
            for g in range(NCH):
                e.tensor_tensor(out=c.SCR[:, 0:TSW],
                                in0=c.X[:, T * g + TSW * s:T * g + TSW * (s + 1)],
                                in1=c.BCA[:, 0:TSW], op=ALU.mult)
                e.reduce_sum(out=c.SCR[:, 512 + 8 * s + g:513 + 8 * s + g],
                             in_=c.SCR[:, 0:TSW], axis=AX.X)
            e.engine_nop().then_inc(c.sXSv.h, 1)
            c.sXSv.v += 1
        parts = c.SCR[:, 512:544].rearrange("p (s g) -> p g s", g=NCH)
        for g in range(NCH):
            e.reduce_sum(out=c.XS[:, g:g + 1], in_=parts[:, g, :], axis=AX.X)
        e.tensor_copy(out=c.SCR[:, 0:512], in_=c.X[:, 0:512])
        e.tensor_scalar(out=c.XS[:, :], in0=c.XS[:, :], scalar1=c.SSB[:, 0:1],
                        scalar2=None, op0=ALU.add)
        e.tensor_copy(out=c.SCR[:, 0:512], in_=c.X[:, 0:512])
        e.tensor_copy(out=c.XSB[:, :], in_=c.XS[:, :]
                      ).then_inc(c.sXSv.h, 1); c.sXSv.v += 1

    a0d, a0t = c.sA0d.v, c.sA0t.v

    @blk.gpsimd
    def _(e):
        e.wait_ge(c.sW2d.h, c.sW2d.v)
        for g in range(NCH):
            if g >= 1:
                e.wait_ge(c.sA0t.h, a0t + g)
            e.dma_start(out=c.WQB[:, :], in_=c.wvt[128 * g:128 * (g + 1), :]
                        ).then_inc(c.sA0d.h, 16); c.sA0d.v += 16

    @blk.tensor
    def _(e):
        e.wait_ge(c.sXSv.h, c.sXSv.v)
        for g in range(NCH):
            e.wait_ge(c.sA0d.h, a0d + 16 * (g + 1))
            for h in range(2):
                mm = e.matmul(c.PS[0:1, 512 * h:512 * (h + 1)],
                              c.XSB[:, g:g + 1], c.WQB[:, 512 * h:512 * (h + 1)],
                              start=(g == 0), stop=(g == NCH - 1))
            mm.then_inc(c.sA0t.h, 1)
            c.sA0t.v += 1

    a0v = c.sA0v.v

    @blk.vector
    def _(e):
        e.wait_ge(c.sA0t.h, c.sA0t.v)
        e.tensor_copy(out=c.YA[0:1, 0:D], in_=c.PS[0:1, 0:D]
                      ).then_inc(c.sA0v.h, 1); c.sA0v.v += 1

    xd1 = c.sXSd.v

    @blk.sync
    def _(e):
        e.wait_ge(c.sA0v.h, a0v + 1)
        e.dma_start(out=c.rowspb[0:1, :], in_=c.YA[0:1, 0:D]
                    ).then_inc(c.sXSd.h, 16); c.sXSd.v += 16
        e.wait_ge(c.sXSd.h, xd1 + 16)
        e.dma_start(out=c.VSB[:, :],
                    in_=c.rowspb[0:1, :].rearrange("o (g p) -> (o p) g", p=128),
                    allow_slow_non_contiguous=True
                    ).then_inc(c.sXSd.h, 16); c.sXSd.v += 16

    a1d, a1t = c.sA0d.v, c.sA0t.v

    @blk.gpsimd
    def _(e):
        for g in range(NCH):
            e.wait_ge(c.sA0t.h, a1t + g)
            e.dma_start(out=c.WQB[:, :], in_=c.wot[128 * g:128 * (g + 1), :]
                        ).then_inc(c.sA0d.h, 16); c.sA0d.v += 16

    @blk.tensor
    def _(e):
        e.wait_ge(c.sXSd.h, c.sXSd.v)
        for g in range(NCH):
            e.wait_ge(c.sA0d.h, a1d + 16 * (g + 1))
            for h in range(2):
                mm = e.matmul(c.PS[0:1, 1024 + 512 * h:1024 + 512 * (h + 1)],
                              c.VSB[:, g:g + 1], c.WQB[:, 512 * h:512 * (h + 1)],
                              start=(g == 0), stop=(g == NCH - 1))
            mm.then_inc(c.sA0t.h, 1)
            c.sA0t.v += 1

    a0v2 = c.sA0v.v

    @blk.vector
    def _(e):
        e.wait_ge(c.sA0t.h, c.sA0t.v)
        e.tensor_copy(out=c.SCR[0:1, 0:D], in_=c.PS[0:1, 1024:2048]
                      ).then_inc(c.sA0v.h, 1); c.sA0v.v += 1

    xd2 = c.sXSd.v

    @blk.sync
    def _(e):
        e.wait_ge(c.sA0v.h, a0v2 + 1)
        e.dma_start(out=c.rowsp[3:4, 0:D], in_=c.SCR[0:1, 0:D]
                    ).then_inc(c.sXSd.h, 16); c.sXSd.v += 16
        e.wait_ge(c.sXSd.h, xd2 + 16)
        e.dma_start(out=c.C0[:, :],
                    in_=c.rowsp[3:4, 0:D].rearrange("o (g p) -> (o p) g", p=128),
                    allow_slow_non_contiguous=True
                    ).then_inc(c.sXSd.h, 16); c.sXSd.v += 16

    @blk.vector
    def _(e):
        e.wait_ge(c.sXSd.h, c.sXSd.v)
        e.scalar_tensor_tensor(out=c.C0[:, :], in0=c.F2B[:, :], scalar=0.5,
                               in1=c.C0[:, :], op0=ALU.mult, op1=ALU.add
                               ).then_inc(c.sA0v.h, 1); c.sA0v.v += 1


def _ph_ffn(c, layer, nst=NST):
    blk = c.blk
    spill = c.w2sp if layer == 0 else c.wqsp
    rows = [0, 1, 2] if layer == 0 else [0, 1]
    for s in range(nst):
        bb = c.sBCd.v

        @blk.scalar
        def _(e, s=s, rows=rows, layer=layer):
            if s == 0:
                e.wait_ge(c.sBCd.h, c.sBCd.v)
                if layer == 0:
                    e.wait_ge(c.sA0v.h, c.sA0v.v)
            if s >= 1:
                e.wait_ge(c.sEP.h, c.sEP.v)
            _emit_bcast(c, e, s, rows=rows)

        hb0 = c.sHB.v

        @blk.vector
        def _(e, s=s, hb0=hb0, bb=bb, layer=layer, rows=rows):
            if s == 0:
                if layer == 0:
                    e.wait_ge(c.sA0v.h, c.sA0v.v)
                    e.wait_ge(c.sDQ.h, c.sDQ.v)
                    e.wait_ge(c.sW2d.h, c.sW2d.v)
                else:
                    e.wait_ge(c.sL1d.h, c.sL1d.v)
            if layer == 1 and s >= 1:
                e.wait_ge(c.sF1e.h, 64 * s)
                e.wait_ge(c.sF1o.h, 64 * s)
            e.wait_ge(c.sBCd.h, bb + 16 * len(rows))
            for g in range(NCH):
                xg = c.X[:, T * g + TSW * s:T * g + TSW * (s + 1)]
                if layer == 0:
                    if g >= 1:
                        e.wait_ge(c.sHB.h, hb0 + 2 * g)
                    e.tensor_tensor(out=c.SCR[:, 0:TSW], in0=xg,
                                    in1=c.BCA[:, 0:TSW], op=ALU.mult)
                    e.tensor_tensor(out=c.SCR[:, 512:1024], in0=c.SCR[:, 0:TSW],
                                    in1=c.BCA[:, TSW:2 * TSW], op=ALU.add)
                    e.tensor_tensor(out=c.SCR[:, 0:TSW], in0=c.SCR[:, 512:1024],
                                    in1=c.BCA[:, 2 * TSW:3 * TSW],
                                    op=ALU.mult).then_inc(c.sHB.h, 1)
                    c.sHB.v += 2
                else:
                    e.tensor_tensor(out=c.SCR[:, 0:TSW], in0=xg,
                                    in1=c.BCA[:, 0:TSW], op=ALU.mult)
                    e.tensor_tensor(out=c.HBB[:, TSW * g:TSW * (g + 1)],
                                    in0=c.SCR[:, 0:TSW],
                                    in1=c.BCA[:, TSW:2 * TSW], op=ALU.add
                                    ).then_inc(c.sHB.h, 1)
                    c.sHB.v += 1
            if layer == 0:
                _emit_wq_tiles(c, e, s)

        if layer == 0:
            @blk.scalar
            def _(e, hb0=hb0):
                for g in range(NCH):
                    e.wait_ge(c.sHB.h, hb0 + 2 * g + 1)
                    e.activation(out=c.HBB[:, TSW * g:TSW * (g + 1)],
                                 in_=c.SCR[:, 0:TSW], func=AF.Identity,
                                 bias=c.BET[:, g:g + 1], scale=c.GAM[:, g:g + 1]
                                 ).then_inc(c.sHB.h, 1)

        mm0, ev0 = c.sMM.v, c.sEV.v

        @blk.tensor
        def _(e, s=s, layer=layer, mm0=mm0, ev0=ev0):
            e.wait_ge(c.sHB.h, c.sHB.v)
            if s > 0 or layer == 1:
                e.wait_ge(c.sEP.h, c.sEP.v)
            for fo in range(NFO):
                if fo >= 2:
                    e.wait_ge(c.sEV.h, ev0 + (fo - 1) * (2 if layer else 1))
                pb = c.PS[:, 512 * (fo % 2):512 * (fo % 2 + 1)]
                for g in range(NCH):
                    mm = e.matmul(pb, c.WB[:, FF * g + 128 * fo:FF * g + 128 * (fo + 1)],
                                  c.HBB[:, TSW * g:TSW * (g + 1)],
                                  start=(g == 0), stop=(g == NCH - 1))
                mm.then_inc(c.sMM.h, 1)
                c.sMM.v += 1

        if layer == 0:
            @blk.scalar
            def _(e, mm0=mm0):
                for fo in range(NFO):
                    e.wait_ge(c.sMM.h, mm0 + fo + 1)
                    pb = c.PS[:, 512 * (fo % 2):512 * (fo % 2 + 1)]
                    e.activation(out=c.YA[:, TSW * fo:TSW * (fo + 1)], in_=pb,
                                 func=AF.Silu, bias=c.F1B[:, fo:fo + 1]
                                 ).then_inc(c.sEV.h, 1)
                    c.sEV.v += 1
        else:
            @blk.scalar
            def _(e, mm0=mm0, ev0=ev0):
                for fo in range(NFO):
                    e.wait_ge(c.sMM.h, mm0 + fo + 1)
                    if fo >= 1:
                        e.wait_ge(c.sEV.h, ev0 + 2 * fo)
                    pb = c.PS[:, 512 * (fo % 2):512 * (fo % 2 + 1)]
                    e.activation(out=c.SCR[:, 0:TSW], in_=pb, func=AF.Relu,
                                 bias=c.B1L[:, fo:fo + 1]).then_inc(c.sEV.h, 1)
                    c.sEV.v += 2

            @blk.vector
            def _(e, ev0=ev0):
                for fo in range(NFO):
                    e.wait_ge(c.sEV.h, ev0 + 2 * fo + 1)
                    _equant(c, e, c.SCR[:, 0:TSW], c.YA[:, TSW * fo:TSW * (fo + 1)],
                            man=3, s1=c.SCR[:, 512:1024], s2=c.SCR[:, 1024:1536],
                            msk=c.SCR[:, 1536:2048], signed=False
                            ).then_inc(c.sEV.h, 1)

        rb0, rt0 = (c.sRBe.v, c.sRBo.v), c.sRBt.v

        @blk.sync
        def _(e, spill=spill, rt0=rt0, s=s, layer=layer):
            if s == 0:
                e.wait_ge((c.sW2d if layer == 0 else c.sWQd).h,
                          (c.sW2d if layer == 0 else c.sWQd).v)
            for q in range(16):
                if rt0 + q >= 2:
                    e.wait_ge(c.sRBt.h, rt0 + q - 1)
                sb_ = c.sRBe if q % 2 == 0 else c.sRBo
                e.dma_start(out=c.W2B[:, 2 * D * (q % 2):2 * D * (q % 2 + 1)
                                      ].rearrange("p (g d) -> p g d", g=2),
                            in_=spill[256 * q:256 * (q + 1), :].rearrange(
                                "(g p) d -> p g d", p=128)
                            ).then_inc(sb_.h, 16); sb_.v += 16

        @blk.tensor
        def _(e, rb0=rb0, rt0=rt0):
            e.wait_ge(c.sEV.h, c.sEV.v)
            for q in range(16):
                sb_ = c.sRBe if q % 2 == 0 else c.sRBo
                e.wait_ge(sb_.h, rb0[q % 2] + 16 * (q // 2 + 1))
                for gg in range(2):
                    g2 = 2 * q + gg
                    off = 2 * D * (q % 2) + D * gg
                    for d in range(NCH):
                        mm = e.matmul(c.PS[:, 512 * d:512 * (d + 1)],
                                      c.W2B[:, off + 128 * d:off + 128 * (d + 1)],
                                      c.YA[:, TSW * g2:TSW * (g2 + 1)],
                                      start=(g2 == 0), stop=(g2 == 31))
                mm.then_inc(c.sRBt.h, 1)
                c.sRBt.v += 1

        ep0 = c.sEP.v

        @blk.scalar
        def _(e, rt0=rt0, layer=layer, ep0=ep0, wqv=c.sWQ.v):
            e.wait_ge(c.sRBt.h, rt0 + 16)
            for d in range(NCH):
                if d == 0 and layer == 0:
                    e.wait_ge(c.sWQ.h, wqv)
                if d >= 2:
                    e.wait_ge(c.sEP.h, ep0 + 2 * d - 1)
                bias = c.C0[:, d:d + 1] if layer == 0 else c.BQ[:, d:d + 1]
                e.activation(out=c.SCR[:, 2048 + TSW * (d % 2):2048 + TSW * (d % 2 + 1)],
                             in_=c.PS[:, 512 * d:512 * (d + 1)], func=AF.Identity,
                             bias=bias, scale=(0.5 if layer == 0 else 1.0)
                             ).then_inc(c.sEP.h, 1)
                c.sEP.v += 2

        @blk.vector
        def _(e, s=s, ep0=ep0, layer=layer):
            for d in range(NCH):
                e.wait_ge(c.sEP.h, ep0 + 2 * d + 1)
                zc = c.SCR[:, 2048 + TSW * (d % 2):2048 + TSW * (d % 2 + 1)]
                xg = c.X[:, T * d + TSW * s:T * d + TSW * (s + 1)]
                if layer == 0:
                    e.tensor_tensor(out=c.SCR[:, 0:TSW], in0=xg,
                                    in1=c.BCA[:, 0:TSW], op=ALU.mult)
                    e.tensor_tensor(out=c.SCR[:, 512:1024], in0=c.SCR[:, 0:TSW],
                                    in1=c.BCA[:, TSW:2 * TSW], op=ALU.add)
                    e.tensor_tensor(out=c.SCR[:, 0:TSW], in0=c.SCR[:, 512:1024],
                                    in1=zc, op=ALU.add)
                    e.tensor_tensor(out=xg, in0=c.SCR[:, 0:TSW], in1=xg,
                                    op=ALU.add).then_inc(c.sEP.h, 1)
                else:
                    i = NCH * s + d
                    if (i + 1) // 2 > 0:
                        e.wait_ge(c.sF1e.h, 16 * ((i + 1) // 2))
                    if i // 2 > 0:
                        e.wait_ge(c.sF1o.h, 16 * (i // 2))
                    _equant(c, e, zc, c.SCR[:, 0:TSW], man=10,
                            s1=c.SCR[:, 512:1024], s2=c.SCR[:, 1024:1536],
                            s3=c.SCR[:, 1536:2048], msk=c.SCR[:, 1536:2048],
                            signed=True).then_inc(c.sEP.h, 1)
                    e.engine_nop().then_inc(c.sF1v.h, 1)
                    c.sF1v.v += 1

        if layer == 1:
            @blk.gpsimd
            def _(e, s=s):
                for d in range(NCH):
                    i = NCH * s + d
                    e.wait_ge(c.sF1v.h, i + 1)
                    fs = c.sF1e if i % 2 == 0 else c.sF1o
                    e.dma_start(out=c.f1sp[:, T * d + TSW * s:
                                           T * d + TSW * (s + 1)],
                                in_=c.SCR[:, 0:TSW]).then_inc(fs.h, 16)
                    fs.v += 16


def _ph_w1_load_l1(c):
    blk = c.blk

    @blk.gpsimd
    def _(e):
        e.wait_ge(c.sEP.h, c.sEP.v)
        for g in range(NCH):
            e.dma_start(out=c.WB[:, FF * g:FF * (g + 1)],
                        in_=c.w1t[128 * g:128 * (g + 1), :]
                        ).then_inc(c.sL1d.h, 16); c.sL1d.v += 16


def _ph_qk(c):
    blk = c.blk
    qt0, qe0, ke0 = c.sQKt.v, c.sQKe.v, c.sEV.v
    bb, l10, qd0 = c.sBCd.v, c.sL1.v, c.sQKd.v

    @blk.gpsimd
    def _(e):
        e.wait_ge(c.sEP.h, c.sEP.v)
        for bk in range(2):
            if bk == 1:
                e.wait_ge(c.sQKt.h, qt0 + 32)
            for g in range(NCH):
                e.dma_start(out=c.QKW[:, 1024 * g:1024 * (g + 1)],
                            in_=c.qkvt[128 * g:128 * (g + 1),
                                       1024 * bk:1024 * (bk + 1)]
                            ).then_inc(c.sQKd.h, 16); c.sQKd.v += 16
        for s in range(NST):
            for pc in range(NCH):
                i = 8 * s + pc
                e.wait_ge(c.sEV.h, ke0 + i + 1)
                ks = c.sKSe if i % 2 == 0 else c.sKSo
                e.dma_start(out=c.qksp[:, NCH * T + T * pc + TSW * s:
                                       NCH * T + T * pc + TSW * (s + 1)],
                            in_=c.QKS[:, TSW * (i % 2):TSW * (i % 2 + 1)]
                            ).then_inc(ks.h, 16); ks.v += 16

    @blk.scalar
    def _(e):
        e.wait_ge(c.sEP.h, c.sEP.v)
        e.wait_ge(c.sBCd.h, c.sBCd.v)
        for bk in range(2):
            for s in range(NST):
                si = 4 * bk + s
                if si >= 1:
                    e.wait_ge(c.sL1.h, l10 + si)
                _emit_bcast(c, e, s, rows=[0, 1])
                if bk == 0:
                    for pc in range(NCH):
                        i = 8 * s + pc
                        e.wait_ge(c.sQKt.h, qt0 + i + 1)
                        qs = c.sQSe if i % 2 == 0 else c.sQSo
                        if i >= 2:
                            e.wait_ge(qs.h, 16 * (i // 2))
                        pb = c.PS[:, 512 * (i % 2):512 * (i % 2 + 1)]
                        e.activation(out=c.QKS[:, TSW * (i % 2):TSW * (i % 2 + 1)],
                                     in_=pb, func=AF.Identity, scale=0.125
                                     ).then_inc(c.sQKe.h, 1)
                        c.sQKe.v += 1
                        e.wait_ge(c.sQKe.h, qe0 + i + 1)
                        e.dma_start(out=c.qksp[:, T * pc + TSW * s:
                                               T * pc + TSW * (s + 1)],
                                    in_=c.QKS[:, TSW * (i % 2):TSW * (i % 2 + 1)]
                                    ).then_inc(qs.h, 16)
                        qs.v += 16

    @blk.vector
    def _(e):
        for bk in range(2):
            for s in range(NST):
                si = 4 * bk + s
                if si == 0:
                    e.wait_ge(c.sF1e.h, c.sF1e.v)
                    e.wait_ge(c.sF1o.h, c.sF1o.v)
                e.wait_ge(c.sBCd.h, bb + 32 * (si + 1))
                if si >= 1:
                    e.wait_ge(c.sQKt.h, qt0 + 8 * si)
                for g in range(NCH):
                    e.tensor_tensor(out=c.SCR[:, 0:TSW],
                                    in0=c.X[:, T * g + TSW * s:T * g + TSW * (s + 1)],
                                    in1=c.BCA[:, 0:TSW], op=ALU.mult)
                    e.tensor_tensor(out=c.XNS[:, TSW * g:TSW * (g + 1)],
                                    in0=c.SCR[:, 0:TSW], in1=c.BCA[:, TSW:2 * TSW],
                                    op=ALU.add)
                e.engine_nop().then_inc(c.sL1.h, 1)
                c.sL1.v += 1
                if bk == 1:
                    for pc in range(NCH):
                        i = 8 * s + pc
                        e.wait_ge(c.sQKt.h, qt0 + 32 + i + 1)
                        ks = c.sKSe if i % 2 == 0 else c.sKSo
                        if i < 2:
                            e.wait_ge(c.sQSe.h, c.sQSe.v)
                            e.wait_ge(c.sQSo.h, c.sQSo.v)
                        if i >= 2:
                            e.wait_ge(ks.h, 16 * (i // 2))
                        pb = c.PS[:, 512 * (i % 2):512 * (i % 2 + 1)]
                        e.tensor_copy(out=c.QKS[:, TSW * (i % 2):TSW * (i % 2 + 1)],
                                      in_=pb).then_inc(c.sEV.h, 1)
                        c.sEV.v += 1

    @blk.tensor
    def _(e):
        for bk in range(2):
            e.wait_ge(c.sQKd.h, qd0 + 128 * (bk + 1))
            for s in range(NST):
                si = 4 * bk + s
                e.wait_ge(c.sL1.h, l10 + si + 1)
                for pc in range(NCH):
                    i = 32 * bk + 8 * s + pc
                    if i >= 2:
                        if i - 2 < 32:
                            e.wait_ge(c.sQKe.h, qe0 + (i - 2) + 1)
                        else:
                            e.wait_ge(c.sEV.h, ke0 + (i - 2 - 32) + 1)
                    pb = c.PS[:, 512 * (i % 2):512 * (i % 2 + 1)]
                    for g in range(NCH):
                        mm = e.matmul(
                            pb,
                            c.QKW[:, 1024 * g + 128 * pc:1024 * g + 128 * (pc + 1)],
                            c.XNS[:, TSW * g:TSW * (g + 1)],
                            start=(g == 0), stop=(g == NCH - 1))
                    mm.then_inc(c.sQKt.h, 1)
                    c.sQKt.v += 1


def _ph_vt(c):
    blk = c.blk
    vd0, vt0, ve0 = c.sVTd.v, c.sVTt.v, c.sVTe.v
    bb, l10 = c.sBCd.v, c.sL1.v

    @blk.gpsimd
    def _(e):
        ap = c.VT[:, :].rearrange("p (t h u) -> p t h u", h=16, u=65)
        e.memset(ap[:, :, :, 64:65], 1.0).then_inc(c.sME.h, 1); c.sME.v += 1
        for tg in range(4):
            for g in range(NCH):
                i = 8 * tg + g
                if i >= 1:
                    e.wait_ge(c.sVTt.h, vt0 + i)
                e.dma_start(out=c.VW[:, :],
                            in_=c.qkvt[128 * g:128 * (g + 1), 2048:3072]
                            ).then_inc(c.sVTd.h, 16); c.sVTd.v += 16

    @blk.scalar
    def _(e):
        e.wait_ge(c.sBCd.h, c.sBCd.v)
        e.wait_ge(c.sL1.h, l10)
        for tg in range(4):
            if tg >= 1:
                e.wait_ge(c.sL1.h, l10 + tg)
            _emit_bcast(c, e, tg, rows=[0, 1])

    @blk.vector
    def _(e):
        e.wait_ge(c.sME.h, c.sME.v)
        for tg in range(4):
            e.wait_ge(c.sBCd.h, bb + 32 * (tg + 1))
            if tg >= 1:
                e.wait_ge(c.sVTt.h, vt0 + 8 * tg)
            for g in range(NCH):
                e.tensor_tensor(out=c.SCR[:, 0:TSW],
                                in0=c.X[:, T * g + TSW * tg:T * g + TSW * (tg + 1)],
                                in1=c.BCA[:, 0:TSW], op=ALU.mult)
                e.tensor_tensor(out=c.XNS[:, TSW * g:TSW * (g + 1)],
                                in0=c.SCR[:, 0:TSW], in1=c.BCA[:, TSW:2 * TSW],
                                op=ALU.add)
            e.engine_nop().then_inc(c.sL1.h, 1)
            c.sL1.v += 1
            e.wait_ge(c.sVTt.h, vt0 + 8 * (tg + 1))
            for tt in range(4 * tg, 4 * tg + 4):
                dst = c.VT[:, 1040 * tt:1040 * (tt + 1)].rearrange(
                    "p (h u) -> p h u", u=65)[:, :, 0:64]
                src = c.PS[:, 1024 * (tt % 4):1024 * (tt % 4) + 1024].rearrange(
                    "p (h u) -> p h u", u=64)
                e.tensor_copy(out=dst, in_=src).then_inc(c.sVTe.h, 1)
                c.sVTe.v += 1

    @blk.tensor
    def _(e):
        for tg in range(4):
            e.wait_ge(c.sL1.h, l10 + tg + 1)
            if tg >= 1:
                e.wait_ge(c.sVTe.h, ve0 + 4 * tg)
            for g in range(NCH):
                i = 8 * tg + g
                e.wait_ge(c.sVTd.h, vd0 + 16 * (i + 1))
                for tt in range(4):
                    st = c.XNS[:, TSW * g + 128 * tt:TSW * g + 128 * (tt + 1)]
                    for h2 in range(2):
                        mm = e.matmul(
                            c.PS[:, 1024 * tt + 512 * h2:1024 * tt + 512 * (h2 + 1)],
                            st, c.VW[:, 512 * h2:512 * (h2 + 1)],
                            start=(g == 0), stop=(g == NCH - 1))
                mm.then_inc(c.sVTt.h, 1)
                c.sVTt.v += 1


def _ph_heads(c):
    blk = c.blk
    ex0, mk0, pc0, pt0, nr0 = (c.sEXP.v, c.sMSKv.v, c.sPVc.v, c.sPVt.v, c.sNRM.v)
    PTICK = sum((4 * s + 5) for s in range(NST)) * 2

    @blk.gpsimd
    def _(e):
        e.wait_ge(c.sVTe.h, c.sVTe.v)
        for ss in (c.sQSe, c.sQSo, c.sKSe, c.sKSo):
            e.wait_ge(ss.h, ss.v)
        e.dma_start(out=c.M4[:, :], in_=c.m4).then_inc(c.sQKd.h, 16)
        c.sQKd.v += 16
        for p in range(NCH):
            if p >= 2:
                e.wait_ge(c.sPVt.h, pt0 + PTICK * (p - 1))
            b = 2 * T * (p % 2)
            qp = c.sQPe if p % 2 == 0 else c.sQPo
            e.dma_start(out=c.QKP[:, b:b + T], in_=c.qksp[:, T * p:T * (p + 1)]
                        ).then_inc(qp.h, 16); qp.v += 16
            e.dma_start(out=c.QKP[:, b + T:b + 2 * T],
                        in_=c.qksp[:, NCH * T + T * p:NCH * T + T * (p + 1)]
                        ).then_inc(qp.h, 16); qp.v += 16

    @blk.tensor
    def _(e):
        e.wait_ge(c.sVTe.h, c.sVTe.v)
        exp_n = mk_n = nrm_n = 0
        for h in range(NH):
            pc, hh = h // 2, h % 2
            b = 2 * T * (pc % 2)
            prow = slice(64 * hh, 64 * hh + 64)
            if hh == 0:
                qp = c.sQPe if pc % 2 == 0 else c.sQPo
                e.wait_ge(qp.h, 32 * (pc // 2 + 1))
            for s in range(NST):
                jmax = 4 * s + 3
                for j in range(jmax + 1):
                    e.wait_ge(c.sEXP.h, ex0 + exp_n + max(0, j - 1))
                    e.matmul(
                        c.PS[:, 512 * (j % 2):512 * (j % 2 + 1)],
                        c.QKP[prow, b + T + 128 * j:b + T + 128 * (j + 1)],
                        c.QKP[prow, b + TSW * s:b + TSW * (s + 1)],
                        start=True, stop=True).then_inc(c.sPVt.h, 1)
                    c.sPVt.v += 1
                    if j >= 4 * s:
                        e.wait_ge(c.sMSKv.h, mk0 + mk_n + (j - 4 * s) + 1)
                    else:
                        e.wait_ge(c.sEXP.h, ex0 + exp_n + j + 1)
                    e.matmul(c.PS[0:65, 1024:1536],
                             c.VT[:, 1040 * j + 65 * h:1040 * j + 65 * (h + 1)],
                             c.PT[:, TSW * (j % 4):TSW * (j % 4 + 1)],
                             start=(j == 0), stop=(j == jmax)
                             ).then_inc(c.sPVc.h, 1)
                    c.sPVc.v += 1
                exp_n += jmax + 1
                mk_n += 4
                e.wait_ge(c.sNRM.h, nr0 + nrm_n + 1)
                e.matmul(c.PS[0:64, 1536:2048], c.ONE64[:, :], c.RR,
                         start=True, stop=True).then_inc(c.sPVt.h, 1)
                c.sPVt.v += 1
                nrm_n += 3
                e.wait_ge(c.sNRM.h, nr0 + nrm_n)

    @blk.scalar
    def _(e):
        sco_n = pv_n = 0
        for h in range(NH):
            for s in range(NST):
                jmax = 4 * s + 3
                for j in range(jmax + 1):
                    e.wait_ge(c.sPVt.h, pt0 + sco_n + j + 1)
                    if j >= 4:
                        e.wait_ge(c.sPVc.h, pc0 + pv_n + j - 3)
                    e.activation(out=c.PT[:, TSW * (j % 4):TSW * (j % 4 + 1)],
                                 in_=c.PS[:, 512 * (j % 2):512 * (j % 2 + 1)],
                                 func=AF.Exp).then_inc(c.sEXP.h, 1)
                    c.sEXP.v += 1
                sco_n += jmax + 2
                pv_n += jmax + 1
                e.wait_ge(c.sPVc.h, pc0 + pv_n)
                e.activation(out=c.RR, in_=c.PS[64:65, 1024:1536], func=AF.Ln)
                e.activation(out=c.RR, in_=c.RR, func=AF.Exp, scale=-1.0
                             ).then_inc(c.sNRM.h, 1)
                c.sNRM.v += 1

    @blk.vector
    def _(e):
        e.wait_ge(c.sQKd.h, c.sQKd.v)
        exp_n = sco_n = 0
        for h in range(NH):
            pc, hh = h // 2, h % 2
            for s in range(NST):
                jmax = 4 * s + 3
                for j in range(4 * s, jmax + 1):
                    r = j - 4 * s
                    e.wait_ge(c.sEXP.h, ex0 + exp_n + j + 1)
                    pt = c.PT[:, TSW * (j % 4):TSW * (j % 4 + 1)]
                    e.tensor_tensor(out=pt, in0=pt,
                                    in1=c.M4[:, TSW * r:TSW * (r + 1)],
                                    op=ALU.mult).then_inc(c.sMSKv.h, 1)
                    c.sMSKv.v += 1
                exp_n += jmax + 1
                sco_n += jmax + 2
                e.wait_ge(c.sPVt.h, pt0 + sco_n)
                dst = c.YA[64 * hh:64 * hh + 64,
                           T * pc + TSW * s:T * pc + TSW * (s + 1)]
                e.tensor_copy(out=dst, in_=c.PS[0:64, 1024:1536]
                              ).then_inc(c.sNRM.h, 1)
                c.sNRM.v += 1
                e.tensor_tensor(out=dst, in0=dst, in1=c.PS[0:64, 1536:2048],
                                op=ALU.mult).then_inc(c.sNRM.h, 1)
                c.sNRM.v += 1


def _ph_proj(c):
    blk = c.blk
    pd0, pvt0, pr0 = c.sPRd.v, c.sPVt.v, c.sPRt.v

    @blk.gpsimd
    def _(e):
        e.wait_ge(c.sNRM.h, c.sNRM.v)
        for g in range(NCH):
            e.dma_start(out=c.PRW[:, D * g:D * (g + 1)],
                        in_=c.projt[128 * g:128 * (g + 1), :]
                        ).then_inc(c.sPRd.h, 16); c.sPRd.v += 16
        e.wait_ge(c.sF1e.h, c.sF1e.v)
        e.wait_ge(c.sF1o.h, c.sF1o.v)
        for s in range(NST):
            for oc in range(NCH):
                i = NCH * s + oc
                if i >= 2:
                    e.wait_ge(c.sPRt.h, pr0 + i - 1)
                fp = c.sFPe if i % 2 == 0 else c.sFPo
                e.dma_start(out=c.BCA[:, 512 * (i % 2):512 * (i % 2 + 1)],
                            in_=c.f1sp[:, T * oc + TSW * s:
                                       T * oc + TSW * (s + 1)]
                            ).then_inc(fp.h, 16); fp.v += 16

    @blk.tensor
    def _(e):
        e.wait_ge(c.sPRd.h, pd0 + 16 * NCH)
        for s in range(NST):
            for oc in range(NCH):
                i = NCH * s + oc
                if i >= 2:
                    e.wait_ge(c.sPRt.h, pr0 + i - 1)
                pb = c.PS[:, 512 * (i % 2):512 * (i % 2 + 1)]
                for g in range(NCH):
                    mm = e.matmul(pb, c.PRW[:, D * g + 128 * oc:D * g + 128 * (oc + 1)],
                                  c.YA[:, T * g + TSW * s:T * g + TSW * (s + 1)],
                                  start=(g == 0), stop=(g == NCH - 1))
                mm.then_inc(c.sPVt.h, 1)
                c.sPVt.v += 1

    @blk.vector
    def _(e):
        for s in range(NST):
            for oc in range(NCH):
                i = NCH * s + oc
                e.wait_ge(c.sPVt.h, pvt0 + i + 1)
                pb = c.PS[:, 512 * (i % 2):512 * (i % 2 + 1)]
                xg = c.X[:, T * oc + TSW * s:T * oc + TSW * (s + 1)]
                e.scalar_tensor_tensor(out=xg, in0=pb, scalar=c.BPJ[:, oc:oc + 1],
                                       in1=xg, op0=ALU.add, op1=ALU.add)
                fp = c.sFPe if i % 2 == 0 else c.sFPo
                e.wait_ge(fp.h, 16 * (i // 2 + 1))
                e.tensor_tensor(out=xg, in0=xg,
                                in1=c.BCA[:, 512 * (i % 2):512 * (i % 2 + 1)],
                                op=ALU.add).then_inc(c.sPRt.h, 1)
                c.sPRt.v += 1


def _ph_out(c, wait):
    blk = c.blk

    @blk.sync
    def _(e):
        for s in wait:
            e.wait_ge(s.h, s.v)
        e.dma_start(out=c.out, in_=c.X[:, :]).then_inc(c.sOUd.h, 16)
        c.sOUd.v += 16
    @blk.gpsimd
    def _(e):
        e.wait_ge(c.sOUd.h, c.sOUd.v)


def host_prep(L0_attn_x, L0_attn_w_qkv, L0_attn_w_o, L0_ffn_ln_gamma,
              L0_ffn_ln_beta, L0_ffn_fc1_q, L0_ffn_fc1_scales, L0_ffn_fc1_bias,
              L0_ffn_fc2_q, L0_ffn_fc2_scales, L0_ffn_fc2_bias, L1_attn_w_qkv,
              L1_attn_w_proj, L1_attn_b_proj, L1_ffn_w1, L1_ffn_b1, L1_ffn_w2,
              L1_ffn_b2):
    import ml_dtypes
    A = lambda v: np.ascontiguousarray(v)
    f32 = np.float32
    cm = lambda vec, k: A(np.asarray(vec, f32).reshape(k, 128).T)

    sh = {}
    sh["wvt"] = A(np.asarray(L0_attn_w_qkv, f32)[2048:3072, :].T)
    sh["wot"] = A(np.asarray(L0_attn_w_o, f32).T)
    sh["fc1q"] = A(np.asarray(L0_ffn_fc1_q, np.int32))
    sh["fc1sb"] = A(np.broadcast_to(
        np.asarray(L0_ffn_fc1_scales, f32).reshape(1, 8 * FF), (128, 8 * FF)))
    sh["fc2q"] = A(np.asarray(L0_ffn_fc2_q, np.int32))
    sh["fc2sb"] = A(np.broadcast_to(
        np.asarray(L0_ffn_fc2_scales, f32).reshape(1, 32 * D), (128, 32 * D)))
    sh["gam"] = cm(L0_ffn_ln_gamma, NCH)
    sh["bet"] = cm(L0_ffn_ln_beta, NCH)
    sh["f1b"] = cm(L0_ffn_fc1_bias, NFO)
    sh["f2b"] = cm(L0_ffn_fc2_bias, NCH)
    wq = np.asarray(L1_attn_w_qkv, f32)
    qkv = np.concatenate([wq[:, 0:64, :].reshape(NH * 64, D),
                          wq[:, 64:128, :].reshape(NH * 64, D),
                          wq[:, 128:192, :].reshape(NH * 64, D)], axis=0)
    sh["qkvt"] = A(qkv.T)
    sh["projt"] = A(np.asarray(L1_attn_w_proj, f32).T)
    sh["bpj"] = cm(L1_attn_b_proj, NCH)
    sh["w1t"] = A(np.asarray(L1_ffn_w1, f32).T)
    sh["b1l1"] = cm(L1_ffn_b1, NFO)
    sh["w2t"] = A(np.asarray(L1_ffn_w2, f32).T)
    sh["b2l1"] = cm(L1_ffn_b2, NCH)
    cvec = np.zeros((128, 8), f32)
    cvec[:, 0] = 0.5 + 2.0 ** -20
    cvec[:, 1] = EPS
    sh["cvec"] = cvec
    p = np.arange(128)[:, None]
    qq = np.arange(TSW)[None, :]
    sh["m4"] = A(np.concatenate([(qq >= 128 * r + p) for r in range(4)],
                                axis=1).astype(ml_dtypes.bfloat16))
    x = np.asarray(L0_attn_x, f32)
    in_maps = []
    for b in range(x.shape[0]):
        m = dict(sh)
        m["x"] = A(x[b].T.reshape(NCH, 128, T).transpose(1, 0, 2
                                                         ).reshape(128, NCH * T))
        in_maps.append(m)
    return in_maps


def gather_out(results):
    outs = []
    for r in results:
        o = np.asarray(r["out"]).reshape(128, NCH, T).transpose(1, 0, 2
                                                                ).reshape(D, T)
        outs.append(o.T)
    return np.stack(outs, axis=0)


_NC_CACHE = {}


def kernel(**inputs):
    in_maps = host_prep(**inputs)
    if "nc" not in _NC_CACHE:
        _NC_CACHE["nc"] = build_nc(stop="full")
    res = run_bass_kernel_spmd(_NC_CACHE["nc"], in_maps, core_ids=list(range(8)))
    out = gather_out(res.results)
    return np.ascontiguousarray(out, dtype=np.float32)


# version 11 (final)
# speedup vs baseline: 1.3396x; 1.2637x over previous
import contextlib

import numpy as np

import concourse.bass as bass
import concourse.mybir as mybir
from concourse.bass_utils import run_bass_kernel_spmd

F32 = mybir.dt.float32
BF16 = mybir.dt.bfloat16
I32 = mybir.dt.int32
U8 = mybir.dt.uint8
ALU = mybir.AluOpType
AF = mybir.ActivationFunctionType
AX = mybir.AxisListType

D = 1024
T = 2048
FF = 4096
NCH = 8
NST = 4
TSW = 512
NH = 16
NFO = 32
EPS = 1e-5


class Ctx:
    pass


_EMIT_OPS = {
    "tensor_tensor", "tensor_scalar", "tensor_copy", "tensor_add",
    "tensor_sub", "tensor_mul", "tensor_max", "scalar_tensor_tensor",
    "tensor_single_scalar", "tensor_scalar_add", "tensor_scalar_sub",
    "tensor_scalar_mul", "tensor_scalar_max", "tensor_scalar_min",
    "tensor_relu", "tensor_reduce", "reduce_max", "reduce_sum", "select",
    "copy_predicated", "activation", "square", "sqrt", "mul", "copy", "add",
    "memset", "memzero", "engine_nop", "reciprocal",
}

_INC_REC = {}
_SEM_TOT = {}
_ORIG_THEN_INC = bass.BassInstruction.then_inc


def _rec_then_inc(self, sem, n):
    tot = _SEM_TOT.get(id(sem), 0) + n
    _SEM_TOT[id(sem)] = tot
    _INC_REC[id(self)] = (self, sem, tot)
    return _ORIG_THEN_INC(self, sem, n)


bass.BassInstruction.then_inc = _rec_then_inc


class _Chained:

    def __init__(self, eng, seq_sem, nop_fn=None):
        self._eng = eng
        self._seq = seq_sem
        self._nop = nop_fn
        self._prev = None

    def wait_ge(self, sem, val):
        return self._eng.wait_ge(sem, val)

    def _edge(self):
        if self._prev is None:
            return
        rec = _INC_REC.get(id(self._prev))
        if rec is None or rec[1] is not self._seq:
            if rec is None:
                tgt = self._prev
            elif self._nop is not None:
                tgt = self._nop(self._eng)
            else:
                tgt = self._eng.engine_nop()
            _rec_then_inc(tgt, self._seq, 1)
            rec = _INC_REC[id(tgt)]
        self._eng.wait_ge(rec[1], rec[2])

    def __getattr__(self, name):
        f = getattr(self._eng, name)
        if name == "dma_start":
            def gd(*a, **k):
                self._edge()
                ins = f(*a, **k)
                self._prev = None
                return ins
            return gd
        if name not in _EMIT_OPS:
            return f

        def g(*a, **k):
            self._edge()
            ins = f(*a, **k)
            self._prev = ins
            return ins

        return g




def build_nc(stop="full"):
    _INC_REC.clear()
    _SEM_TOT.clear()
    nc = bass.Bass("TRN2", target_bir_lowering=False)
    c = Ctx()
    c.nc = nc
    c.stop = stop

    dram = lambda n, sh, ty, kind="ExternalInput": nc.dram_tensor(n, sh, ty, kind=kind).ap()
    c.x = dram("x", [128, NCH * T], F32)
    c.wvt = dram("wvt", [D, D], F32)
    c.wot = dram("wot", [D, D], F32)
    c.fc1q = dram("fc1q", [D, FF], I32)
    c.fc1sb = dram("fc1sb", [128, 8 * FF], F32)
    c.fc2q = dram("fc2q", [FF, D], I32)
    c.fc2sb = dram("fc2sb", [128, 32 * D], F32)
    c.gam = dram("gam", [128, NCH], F32)
    c.bet = dram("bet", [128, NCH], F32)
    c.f1b = dram("f1b", [128, NFO], F32)
    c.f2b = dram("f2b", [128, NCH], F32)
    c.qkvt = dram("qkvt", [D, 3 * D], F32)
    c.projt = dram("projt", [D, D], F32)
    c.bpj = dram("bpj", [128, NCH], F32)
    c.w1t = dram("w1t", [D, FF], F32)
    c.b1l1 = dram("b1l1", [128, NFO], F32)
    c.w2t = dram("w2t", [FF, D], F32)
    c.b2l1 = dram("b2l1", [128, NCH], F32)
    c.cvec = dram("cvec", [128, 8], F32)
    c.m4 = dram("m4", [128, 4 * TSW], BF16)
    c.out = dram("out", [128, NCH * T], F32, "ExternalOutput")
    c.w2sp = dram("w2sp", [FF, D], BF16, "Internal")
    c.wqsp = dram("wqsp", [FF, D], BF16, "Internal")
    c.rowsp = dram("rowsp", [4, T], F32, "Internal")
    c.rowspb = dram("rowspb", [1, D], BF16, "Internal")
    c.qksp = dram("qksp", [128, 2 * NCH * T], BF16, "Internal")
    c.f1sp = dram("f1sp", [128, NCH * T], F32, "Internal")

    with contextlib.ExitStack() as es:
        _body(c, es)
    return nc


def _body(c, es):
    nc = c.nc
    sb = lambda n, sh, ty: es.enter_context(nc.sbuf_tensor(n, sh, ty))

    c.X = sb("X", [128, NCH * T], F32)
    c.SCR = sb("SCR", [128, 3072], F32)
    c.YA = sb("YA", [128, NFO * TSW], BF16)
    c.CON = sb("CON", [128, 8], F32)
    c.ONE = sb("ONE", [128, 1], BF16)
    c.ONE64 = sb("ONE64", [1, 64], F32)
    c.GAM = sb("GAM", [128, NCH], F32)
    c.BET = sb("BET", [128, NCH], F32)
    c.F1B = sb("F1B", [128, NFO], F32)
    c.F2B = sb("F2B", [128, NCH], F32)
    c.C0 = sb("C0", [128, NCH], F32)
    c.B1L = sb("B1L", [128, NFO], F32)
    c.BPJ = sb("BPJ", [128, NCH], F32)
    c.BQ = sb("BQ", [128, NCH], F32)
    c.B2L = sb("B2L", [128, NCH], F32)
    c.XS = sb("XS", [128, NCH], F32)
    c.XSB = sb("XSB", [128, NCH], BF16)
    c.VSB = sb("VSB", [128, NCH], BF16)
    c.SSB = sb("SSB", [128, 1], F32)
    c.BCA = sb("BCA", [128, 3 * TSW], F32)
    c.RR = c.SCR[0:1, 2048:2560]

    c.PS = es.enter_context(nc.psum_tensor("PS", [128, 4096], F32))

    for name in ["sINd", "sCN", "sSPd", "sWQ", "sWQd", "sST", "sMM", "sVR",
                 "sBCd", "sXSv", "sXSd", "sA0d", "sA0t", "sA0v", "sDQ",
                 "sDQd", "sSCd", "sW2", "sW2d", "sHB", "sEV", "sEP", "sRBd",
                 "sRBt", "sL1", "sL1d", "sQKd", "sQKt", "sQKe", "sVTd",
                 "sVTt", "sVTe", "sEXP", "sMSKv", "sPVc", "sPVt", "sNRM",
                 "sPRd", "sPRt", "sOUd", "sME", "sSPe", "sSPo", "sDQe",
                 "sDQo", "sRBe", "sRBo", "sQPe", "sQPo", "sQSe", "sQSo",
                 "sKSe", "sKSo", "sF1e", "sF1o", "sF1v", "sFPe", "sFPo", "sRR"]:
        s = Ctx(); s.h = es.enter_context(nc.semaphore(name)); s.v = 0
        setattr(c, name, s)

    for name in ["qV", "qS", "qG", "qY"]:
        s = Ctx(); s.h = es.enter_context(nc.semaphore(name)); s.v = 0
        setattr(c, name, s)
    c.blk = es.enter_context(nc.Block())
    c._chains = {}

    def _wrap(engname, e):
        key = (engname,)
        if key not in c._chains:
            seq = {"vector": c.qV, "scalar": c.qS, "gpsimd": c.qG,
                   "sync": c.qY}[engname]
            nop = None
            if engname == "scalar":
                nop = lambda eng: eng.copy(out=c.CON[:, 7:8], in_=c.CON[:, 7:8])
            c._chains[key] = _Chained(e, seq.h, nop)
        ch = c._chains[key]
        ch._eng = e
        return ch
    c.wrap = _wrap

    class _BlkProxy:
        def __init__(self, blk):
            self._blk = blk

        def _mk(self, engname):
            dec = getattr(self._blk, engname)

            def deco(fn):
                return dec(lambda e, fn=fn: fn(c.wrap(engname, e)))
            return deco

        @property
        def vector(self):
            return self._mk("vector")

        @property
        def scalar(self):
            return self._mk("scalar")

        @property
        def gpsimd(self):
            return getattr(self._blk, "gpsimd")

        @property
        def sync(self):
            return getattr(self._blk, "sync")

        @property
        def tensor(self):
            return self._blk.tensor

    c.blk = _BlkProxy(c.blk)

    _ph_loads(c)
    _ph_consts(c)
    if c.stop == "consts":
        _ph_out(c, wait=[c.sCN])
        return

    with contextlib.ExitStack() as esA:
        c.WB = esA.enter_context(nc.sbuf_tensor("WB", [128, NCH * FF], BF16))
        c.WQB = esA.enter_context(nc.sbuf_tensor("WQB", [128, D], BF16))
        c.SCW = esA.enter_context(nc.sbuf_tensor("SCW", [128, 2 * D], F32))
        with contextlib.ExitStack() as esd:
            c.QIB = esd.enter_context(nc.sbuf_tensor("QIB", [128, FF], BF16))
            c.SCB = esd.enter_context(nc.sbuf_tensor("SCB", [128, FF // 2], F32))
            if c.stop == "wq":
                _ph_out(c, wait=[c.sWQd])
                return
            _ph_w1_dequant(c)
            _ph_w2_dequant_spill(c)
            if c.stop == "deq":
                _ph_out(c, wait=[c.sDQ, c.sW2d])
                return
            with contextlib.ExitStack() as esb:
                c.STAT = esb.enter_context(nc.sbuf_tensor("STAT", [128, 160], F32))
                _ph_ln(c, layer=0)
                if c.stop == "ln":
                    _ph_out(c, wait=[c.sBCd])
                    return
                _ph_xsum_a0(c)
                if c.stop == "a0":
                    _ph_out(c, wait=[c.sA0v])
                    return
        c.W2B = esA.enter_context(nc.sbuf_tensor("W2B", [128, 4 * D], BF16))
        c.HBB = esA.enter_context(nc.sbuf_tensor("HBB", [128, NCH * TSW], BF16))
        _ph_wq_spill(c)
        _ph_ffn(c, layer=0, nst=(1 if c.stop == "s1" else NST))
        if c.stop == "s1":
            _ph_out(c, wait=[c.sEP])
            return
        if c.stop == "l0":
            _ph_out(c, wait=[c.sEP])
            return
        with contextlib.ExitStack() as esb:
            c.STAT = esb.enter_context(nc.sbuf_tensor("STAT1", [128, 160], F32))
            _ph_ln(c, layer=1)
        _ph_w1_load_l1(c)
        _ph_ffn(c, layer=1)
    if c.stop == "ffn1":
        _ph_out(c, wait=[c.sEP])
        return
    with contextlib.ExitStack() as esq:
        c.VT = esq.enter_context(
            nc.sbuf_tensor("VT", [128, 16 * 1040], BF16))
        with contextlib.ExitStack() as esk:
            c.XNS = esk.enter_context(
                nc.sbuf_tensor("XNS", [128, NCH * TSW], BF16))
            c.QKS = esk.enter_context(
                nc.sbuf_tensor("QKS", [128, 2 * TSW], BF16))
            with contextlib.ExitStack() as esw:
                c.QKW = esw.enter_context(
                    nc.sbuf_tensor("QKW", [128, NCH * D], BF16))
                _ph_qk(c)
            with contextlib.ExitStack() as esv:
                c.VW = esv.enter_context(nc.sbuf_tensor("VW", [128, D], BF16))
                _ph_vt(c)
        c.M4 = esq.enter_context(nc.sbuf_tensor("M4", [128, 4 * TSW], BF16))
        c.PT = esq.enter_context(nc.sbuf_tensor("PT", [128, 8 * TSW], BF16))
        c.QKP = esq.enter_context(nc.sbuf_tensor("QKP", [128, 4 * T], BF16))
        _ph_heads(c)
        with contextlib.ExitStack() as esp:
            c.PRW = esp.enter_context(nc.sbuf_tensor("PRW", [128, NCH * D], BF16))
            _ph_proj(c)
    _ph_out(c, wait=[c.sPRt])


def _emit_bcast(c, e, s, rows):
    for i in rows:
        e.dma_start(out=c.BCA[:, TSW * i:TSW * (i + 1)],
                    in_=c.rowsp[i:i + 1, TSW * s:TSW * (s + 1)].broadcast_to([128, TSW])
                    ).then_inc(c.sBCd.h, 16)
        c.sBCd.v += 16


def _equant(c, e, src, dst, man, s1, s2, msk, signed, s3=None):
    if man == 3:
        Map, Cap, thr = c.CON[:, 3:4], c.CON[:, 4:5], 2.0 ** -5
    else:
        Map, Cap, thr = c.CON[:, 5:6], c.CON[:, 6:7], 2.0 ** -13
    x = src
    if signed:
        e.tensor_tensor(out=s3, in0=x, in1=x, op=ALU.mult)
        e.tensor_scalar(out=msk, in0=s3, scalar1=float(thr * thr), scalar2=None,
                        op0=ALU.is_ge)
    else:
        e.tensor_scalar(out=msk, in0=x, scalar1=float(thr), scalar2=None,
                        op0=ALU.is_ge)
    e.tensor_scalar(out=s1, in0=x, scalar1=Map, scalar2=None, op0=ALU.mult)
    e.tensor_tensor(out=s2, in0=s1, in1=x, op=ALU.subtract)
    e.tensor_tensor(out=s1, in0=s1, in1=s2, op=ALU.subtract)
    e.tensor_scalar(out=s2, in0=x, scalar1=Cap, scalar2=Cap,
                    op0=ALU.add, op1=ALU.subtract)
    e.tensor_tensor(out=s1, in0=s1, in1=s2, op=ALU.subtract)
    e.tensor_tensor(out=s1, in0=s1, in1=msk, op=ALU.mult)
    return e.tensor_tensor(out=dst, in0=s2, in1=s1, op=ALU.add)


def _ph_loads(c):
    blk = c.blk

    @blk.sync
    def _(e):
        e.dma_start(out=c.X[:, :], in_=c.x).then_inc(c.sINd.h, 16); c.sINd.v += 16
        for dst, src in [(c.GAM, c.gam), (c.BET, c.bet), (c.F1B, c.f1b),
                         (c.F2B, c.f2b), (c.B1L, c.b1l1), (c.BPJ, c.bpj),
                         (c.B2L, c.b2l1), (c.CON, c.cvec)]:
            e.dma_start(out=dst[:, :], in_=src).then_inc(c.sINd.h, 16)
            c.sINd.v += 16

    @blk.gpsimd
    def _(e):
        e.memset(c.ONE[:, :], 1.0)
        e.memset(c.ONE64[:, :], 1.0).then_inc(c.sME.h, 1); c.sME.v += 1

    c.nc.const_aps.register(F32, EPS, c.CON[:, 1:2])


def _ph_consts(c):
    blk = c.blk
    C52 = 1.5 * 2.0 ** 52

    @blk.vector
    def _(e):
        e.wait_ge(c.sINd.h, c.sINd.v)
        e.wait_ge(c.sME.h, 1)
        e.tensor_scalar(out=c.CON[:, 2:3], in0=c.CON[:, 0:1], scalar1=C52,
                        scalar2=C52, op0=ALU.add, op1=ALU.subtract)
        e.tensor_copy(out=c.SCR[:, 0:512], in_=c.X[:, 0:512])
        e.tensor_scalar(out=c.CON[:, 3:4], in0=c.CON[:, 2:3],
                        scalar1=float(2.0 ** 49 - 2.0 ** 20),
                        scalar2=float(2.0 ** 20 + 1.0), op0=ALU.mult, op1=ALU.add)
        e.tensor_scalar(out=c.CON[:, 4:5], in0=c.CON[:, 2:3],
                        scalar1=float(1.5 * 2.0 ** 43 - 1.5 * 2.0 ** 14),
                        scalar2=float(1.5 * 2.0 ** 14), op0=ALU.mult, op1=ALU.add)
        e.tensor_scalar(out=c.CON[:, 5:6], in0=c.CON[:, 2:3],
                        scalar1=float(2.0 ** 42 - 2.0 ** 13),
                        scalar2=float(2.0 ** 13 + 1.0), op0=ALU.mult, op1=ALU.add)
        e.tensor_scalar(out=c.CON[:, 6:7], in0=c.CON[:, 2:3],
                        scalar1=float(1.5 * 2.0 ** 28 - 0.75),
                        scalar2=0.75, op0=ALU.mult, op1=ALU.add)
        e.tensor_copy(out=c.SCR[:, 512:1024], in_=c.X[:, 0:512])
        _equant(c, e, c.B2L[:, :], c.BQ[:, :], man=10,
                s1=c.SCR[:, 8:16], s2=c.SCR[:, 16:24], s3=c.SCR[:, 0:8],
                msk=c.SCR[:, 24:32], signed=True).then_inc(c.sCN.h, 1)
        c.sCN.v += 1


def _ph_wq_spill(c):
    blk = c.blk
    wq0 = c.sWQ.v
    c._wq0 = wq0

    @blk.gpsimd
    def _(e):
        for g in range(34):
            if g >= 2:
                e.wait_ge(c.sWQ.h, wq0 + g - 1)
            if g < 32:
                sb_ = c.sSPe if g % 2 == 0 else c.sSPo
                e.dma_start(out=c.SCW[:, 1024 * (g % 2):1024 * (g % 2) + D],
                            in_=c.w2t[128 * g:128 * (g + 1), :]
                            ).then_inc(sb_.h, 16); sb_.v += 16
            if g >= 2:
                e.dma_start(out=c.wqsp[128 * (g - 2):128 * (g - 1), :],
                            in_=c.WQB[:, :]
                            ).then_inc(c.sWQd.h, 16); c.sWQd.v += 16


def _emit_wq_tiles(c, e, s):
    e.wait_ge(c.sHB.h, c.sHB.v)
    for g in range(8 * s, 8 * s + 8):
        sb_ = c.sSPe if g % 2 == 0 else c.sSPo
        e.wait_ge(sb_.h, 16 * (g // 2 + 1))
        if g >= 1:
            e.wait_ge(c.sWQd.h, 16 * g)
        bo = 1024 * (g % 2)
        _equant(c, e, c.SCW[:, bo:bo + D], c.WQB[:, :], man=3,
                s1=c.SCR[:, 0:D], s2=c.SCR[:, 1024:1024 + D],
                s3=c.SCR[:, 0:D], msk=c.SCR[:, 2048:2048 + D], signed=True
                ).then_inc(c.sWQ.h, 1); c.sWQ.v += 1


def _ph_w1_dequant(c):
    blk = c.blk
    dq0, dd0, sc0 = c.sDQ.v, c.sDQd.v, c.sSCd.v

    @blk.gpsimd
    def _(e):
        for g in range(NCH):
            if g >= 1:
                e.wait_ge(c.sDQ.h, dq0 + 2 * g)
            e.dma_start(out=c.QIB[:, :], in_=c.fc1q[128 * g:128 * (g + 1), :]
                        ).then_inc(c.sDQd.h, 16); c.sDQd.v += 16

    @blk.scalar
    def _(e):
        for g in range(NCH):
            for hf in range(2):
                if g + hf >= 1:
                    e.wait_ge(c.sDQ.h, dq0 + 2 * g + hf)
                e.dma_start(out=c.SCB[:, :],
                            in_=c.fc1sb[:, FF * g + 2048 * hf:FF * g + 2048 * (hf + 1)]
                            ).then_inc(c.sSCd.h, 16); c.sSCd.v += 16

    @blk.vector
    def _(e):
        for g in range(NCH):
            e.wait_ge(c.sDQd.h, dd0 + 16 * (g + 1))
            for hf in range(2):
                e.wait_ge(c.sSCd.h, sc0 + 16 * (2 * g + hf + 1))
                e.scalar_tensor_tensor(
                    out=c.WB[:, FF * g + 2048 * hf:FF * g + 2048 * (hf + 1)],
                    in0=c.QIB[:, 2048 * hf:2048 * (hf + 1)], scalar=-128.0,
                    in1=c.SCB[:, :], op0=ALU.add, op1=ALU.mult
                    ).then_inc(c.sDQ.h, 1); c.sDQ.v += 1


def _ph_w2_dequant_spill(c):
    blk = c.blk
    w20, wd0, sc0 = c.sW2.v, c.sW2d.v, c.sSCd.v

    @blk.gpsimd
    def _(e):
        for g in range(33):
            if g < 32:
                if g < 2:
                    e.wait_ge(c.sDQ.h, c.sDQ.v)
                if g >= 2:
                    e.wait_ge(c.sW2.h, w20 + g - 1)
                sb_ = c.sDQe if g % 2 == 0 else c.sDQo
                e.dma_start(out=c.QIB[:, D * (g % 2):D * (g % 2) + D],
                            in_=c.fc2q[128 * g:128 * (g + 1), :]
                            ).then_inc(sb_.h, 16); sb_.v += 16
            if g >= 1:
                e.wait_ge(c.sW2.h, w20 + g)
                e.dma_start(out=c.w2sp[128 * (g - 1):128 * g, :], in_=c.WQB[:, :]
                            ).then_inc(c.sW2d.h, 16); c.sW2d.v += 16

    @blk.scalar
    def _(e):
        for g in range(32):
            if g == 0:
                e.wait_ge(c.sDQ.h, c.sDQ.v)
            if g >= 1:
                e.wait_ge(c.sW2.h, w20 + g)
            e.dma_start(out=c.SCB[:, 0:D], in_=c.fc2sb[:, D * g:D * (g + 1)]
                        ).then_inc(c.sSCd.h, 16); c.sSCd.v += 16

    @blk.vector
    def _(e):
        for g in range(32):
            sb_ = c.sDQe if g % 2 == 0 else c.sDQo
            e.wait_ge(sb_.h, 16 * (g // 2 + 1))
            e.wait_ge(c.sSCd.h, sc0 + 16 * (g + 1))
            if g == 0:
                e.wait_ge(c.sWQd.h, c.sWQd.v)
            if g >= 1:
                e.wait_ge(c.sW2d.h, wd0 + 16 * g)
            e.scalar_tensor_tensor(out=c.WQB[:, :],
                                   in0=c.QIB[:, D * (g % 2):D * (g % 2) + D],
                                   scalar=-128.0, in1=c.SCB[:, 0:D],
                                   op0=ALU.add, op1=ALU.mult
                                   ).then_inc(c.sW2.h, 1); c.sW2.v += 1


def _ph_ln(c, layer):
    blk = c.blk
    stat = c.STAT
    xb = c.YA[:, 0:T]
    sq = c.YA[:, T:2 * T]
    st0, mm0, vr0, bc0 = c.sST.v, c.sMM.v, c.sVR.v, c.sBCd.v
    COL = lambda i: slice(16 * i, 16 * (i + 1))

    @blk.vector
    def _(e):
        if layer == 1:
            e.wait_ge(c.sEP.h, c.sEP.v)
        else:
            e.wait_ge(c.sCN.h, c.sCN.v)
        for g in range(NCH):
            if g >= 1:
                e.wait_ge(c.sMM.h, mm0 + g)
            xg = c.X[:, T * g:T * (g + 1)]
            e.tensor_copy(out=xb, in_=xg)
            e.tensor_tensor(out=sq, in0=xg, in1=xg, op=ALU.mult
                            ).then_inc(c.sST.h, 1)
            c.sST.v += 1

    @blk.tensor
    def _(e):
        for g in range(NCH):
            e.wait_ge(c.sST.h, st0 + g + 1)
            for s in range(NST):
                sl = slice(TSW * s, TSW * (s + 1))
                e.matmul(c.PS[0:1, sl], c.ONE[:, :], xb[:, sl],
                         start=(g == 0), stop=(g == NCH - 1))
                mm = e.matmul(c.PS[0:1, 2048 + TSW * s:2048 + TSW * (s + 1)],
                              c.ONE[:, :], sq[:, sl],
                              start=(g == 0), stop=(g == NCH - 1))
            mm.then_inc(c.sMM.h, 1)
            c.sMM.v += 1

    @blk.vector
    def _(e):
        e.wait_ge(c.sMM.h, mm0 + NCH)
        e.tensor_copy(out=c.SCR[0:1, 0:T], in_=c.PS[0:1, 0:T])
        e.tensor_copy(out=c.SCR[32:33, 0:T], in_=c.PS[0:1, 2048:2048 + T]
                      ).then_inc(c.sVR.h, 1)
        c.sVR.v += 1

    b0 = c.sBCd.v

    @blk.sync
    def _(e):
        e.wait_ge(c.sVR.h, vr0 + 1)
        e.dma_start(out=c.rowsp[0:1, :], in_=c.SCR[0:1, 0:T]
                    ).then_inc(c.sBCd.h, 16); c.sBCd.v += 16
        e.dma_start(out=c.rowsp[1:2, :], in_=c.SCR[32:33, 0:T]
                    ).then_inc(c.sBCd.h, 16); c.sBCd.v += 16
        e.wait_ge(c.sBCd.h, b0 + 32)
        e.dma_start(out=stat[:, COL(0)],
                    in_=c.rowsp[0:1, :].rearrange("o (t p) -> (o p) t", p=128),
                    allow_slow_non_contiguous=True
                    ).then_inc(c.sBCd.h, 16); c.sBCd.v += 16
        e.dma_start(out=stat[:, COL(1)],
                    in_=c.rowsp[1:2, :].rearrange("o (t p) -> (o p) t", p=128),
                    allow_slow_non_contiguous=True
                    ).then_inc(c.sBCd.h, 16); c.sBCd.v += 16

    b1 = c.sBCd.v

    @blk.vector
    def _(e):
        e.wait_ge(c.sBCd.h, b1)
        spc = lambda: e.tensor_copy(out=c.SCR[:, 1024:1536], in_=c.X[:, 0:512])
        e.tensor_scalar(out=stat[:, COL(2)], in0=stat[:, COL(0)],
                        scalar1=1.0 / D, scalar2=None, op0=ALU.mult)
        e.tensor_scalar(out=stat[:, COL(3)], in0=stat[:, COL(1)],
                        scalar1=1.0 / D, scalar2=None, op0=ALU.mult); spc()
        e.tensor_tensor(out=stat[:, COL(8)], in0=stat[:, COL(2)],
                        in1=stat[:, COL(2)], op=ALU.mult); spc()
        e.tensor_tensor(out=stat[:, COL(3)], in0=stat[:, COL(3)],
                        in1=stat[:, COL(8)], op=ALU.subtract
                        ).then_inc(c.sVR.h, 1)
        c.sVR.v += 1

    @blk.scalar
    def _(e):
        e.wait_ge(c.sVR.h, vr0 + 2)
        e.activation(out=stat[:, COL(8)], in_=stat[:, COL(3)], func=AF.Ln,
                     bias=EPS)
        e.activation(out=c.SCR[:, 1536:2048], in_=c.X[:, 0:512],
                     func=AF.Identity)
        e.activation(out=stat[:, COL(4)], in_=stat[:, COL(8)], func=AF.Exp,
                     scale=-0.5).then_inc(c.sVR.h, 1)
        c.sVR.v += 1

    @blk.vector
    def _(e, layer=layer):
        e.wait_ge(c.sVR.h, vr0 + 3)
        spc = lambda: e.tensor_copy(out=c.SCR[:, 1024:1536], in_=c.X[:, 0:512])
        e.scalar_tensor_tensor(out=stat[:, COL(5)], in0=stat[:, COL(2)],
                               scalar=-1.0, in1=stat[:, COL(4)],
                               op0=ALU.mult, op1=ALU.mult)
        if layer == 0:
            e.tensor_tensor(out=stat[:, COL(6)], in0=stat[:, COL(3)],
                            in1=stat[:, COL(4)], op=ALU.mult); spc()
            e.tensor_tensor(out=stat[:, COL(6)], in0=stat[:, COL(6)],
                            in1=stat[:, COL(4)], op=ALU.mult)
            spc()
            e.tensor_copy(out=c.YA[0:128, 2 * T:2 * T + 16], in_=stat[:, COL(5)])
        e.engine_nop().then_inc(c.sVR.h, 1)
        c.sVR.v += 1

    if layer == 0:
        @blk.scalar
        def _(e):
            e.wait_ge(c.sVR.h, vr0 + 4)
            e.activation(out=stat[:, COL(8)], in_=stat[:, COL(6)],
                         func=AF.Ln, bias=EPS)
            e.activation(out=c.SCR[:, 1536:2048], in_=c.X[:, 0:512],
                         func=AF.Identity)
            e.activation(out=stat[:, COL(7)], in_=stat[:, COL(8)],
                         func=AF.Exp, scale=-0.5).then_inc(c.sVR.h, 1)
            c.sVR.v += 1

        smb = c.sMM.v

        @blk.tensor
        def _(e):
            e.wait_ge(c.sVR.h, vr0 + 4)
            e.matmul(c.PS[0:1, 3072:3088], c.ONE[:, :],
                     c.YA[0:128, 2 * T:2 * T + 16], start=True, stop=True
                     ).then_inc(c.sMM.h, 1)
            c.sMM.v += 1

        @blk.vector
        def _(e, smb=smb):
            e.wait_ge(c.sMM.h, smb + 1)
            e.reduce_sum(out=c.SCR[0:1, 2560:2561], in_=c.PS[0:1, 3072:3088],
                         axis=AX.X).then_inc(c.sVR.h, 1)
            c.sVR.v += 1

    vfin = c.sVR.v
    b2 = c.sBCd.v

    @blk.sync
    def _(e, layer=layer):
        e.wait_ge(c.sVR.h, vfin)
        e.dma_start(out=c.rowsp[0:1, :].rearrange("o (t p) -> (o p) t", p=128),
                    in_=stat[:, COL(4)],
                    allow_slow_non_contiguous=True
                    ).then_inc(c.sBCd.h, 16); c.sBCd.v += 16
        e.dma_start(out=c.rowsp[1:2, :].rearrange("o (t p) -> (o p) t", p=128),
                    in_=stat[:, COL(5)],
                    allow_slow_non_contiguous=True
                    ).then_inc(c.sBCd.h, 16); c.sBCd.v += 16
        if layer == 0:
            e.dma_start(out=c.rowsp[2:3, :].rearrange(
                            "o (t p) -> (o p) t", p=128),
                        in_=stat[:, COL(7)],
                        allow_slow_non_contiguous=True
                        ).then_inc(c.sBCd.h, 16); c.sBCd.v += 16
            e.dma_start(out=c.rowsp[3:4, 0:1], in_=c.SCR[0:1, 2560:2561]
                        ).then_inc(c.sBCd.h, 16); c.sBCd.v += 16
        e.wait_ge(c.sBCd.h, c.sBCd.v)
        if layer == 0:
            e.dma_start(out=c.SSB[:, :],
                        in_=c.rowsp[3:4, 0:1].broadcast_to([128, 1])
                        ).then_inc(c.sBCd.h, 16); c.sBCd.v += 16


def _ph_xsum_a0(c):
    blk = c.blk
    xs0 = c.sXSv.v
    bb = c.sBCd.v

    @blk.sync
    def _(e):
        for s in range(NST):
            if s >= 1:
                e.wait_ge(c.sXSv.h, xs0 + s)
            _emit_bcast(c, e, s, rows=[0])

    @blk.vector
    def _(e):
        for s in range(NST):
            e.wait_ge(c.sBCd.h, bb + 16 * (s + 1))
            for g in range(NCH):
                e.tensor_tensor(out=c.SCR[:, 0:TSW],
                                in0=c.X[:, T * g + TSW * s:T * g + TSW * (s + 1)],
                                in1=c.BCA[:, 0:TSW], op=ALU.mult)
                e.reduce_sum(out=c.SCR[:, 512 + 8 * s + g:513 + 8 * s + g],
                             in_=c.SCR[:, 0:TSW], axis=AX.X)
            e.engine_nop().then_inc(c.sXSv.h, 1)
            c.sXSv.v += 1
        parts = c.SCR[:, 512:544].rearrange("p (s g) -> p g s", g=NCH)
        for g in range(NCH):
            e.reduce_sum(out=c.XS[:, g:g + 1], in_=parts[:, g, :], axis=AX.X)
        e.tensor_copy(out=c.SCR[:, 0:512], in_=c.X[:, 0:512])
        e.tensor_scalar(out=c.XS[:, :], in0=c.XS[:, :], scalar1=c.SSB[:, 0:1],
                        scalar2=None, op0=ALU.add)
        e.tensor_copy(out=c.SCR[:, 0:512], in_=c.X[:, 0:512])
        e.tensor_copy(out=c.XSB[:, :], in_=c.XS[:, :]
                      ).then_inc(c.sXSv.h, 1); c.sXSv.v += 1

    a0d, a0t = c.sA0d.v, c.sA0t.v

    @blk.gpsimd
    def _(e):
        e.wait_ge(c.sW2d.h, c.sW2d.v)
        for g in range(NCH):
            if g >= 1:
                e.wait_ge(c.sA0t.h, a0t + g)
            e.dma_start(out=c.WQB[:, :], in_=c.wvt[128 * g:128 * (g + 1), :]
                        ).then_inc(c.sA0d.h, 16); c.sA0d.v += 16

    @blk.tensor
    def _(e):
        e.wait_ge(c.sXSv.h, c.sXSv.v)
        for g in range(NCH):
            e.wait_ge(c.sA0d.h, a0d + 16 * (g + 1))
            for h in range(2):
                mm = e.matmul(c.PS[0:1, 512 * h:512 * (h + 1)],
                              c.XSB[:, g:g + 1], c.WQB[:, 512 * h:512 * (h + 1)],
                              start=(g == 0), stop=(g == NCH - 1))
            mm.then_inc(c.sA0t.h, 1)
            c.sA0t.v += 1

    a0v = c.sA0v.v

    @blk.vector
    def _(e):
        e.wait_ge(c.sA0t.h, c.sA0t.v)
        e.tensor_copy(out=c.YA[0:1, 0:D], in_=c.PS[0:1, 0:D]
                      ).then_inc(c.sA0v.h, 1); c.sA0v.v += 1

    xd1 = c.sXSd.v

    @blk.sync
    def _(e):
        e.wait_ge(c.sA0v.h, a0v + 1)
        e.dma_start(out=c.rowspb[0:1, :], in_=c.YA[0:1, 0:D]
                    ).then_inc(c.sXSd.h, 16); c.sXSd.v += 16
        e.wait_ge(c.sXSd.h, xd1 + 16)
        e.dma_start(out=c.VSB[:, :],
                    in_=c.rowspb[0:1, :].rearrange("o (g p) -> (o p) g", p=128),
                    allow_slow_non_contiguous=True
                    ).then_inc(c.sXSd.h, 16); c.sXSd.v += 16

    a1d, a1t = c.sA0d.v, c.sA0t.v

    @blk.gpsimd
    def _(e):
        for g in range(NCH):
            e.wait_ge(c.sA0t.h, a1t + g)
            e.dma_start(out=c.WQB[:, :], in_=c.wot[128 * g:128 * (g + 1), :]
                        ).then_inc(c.sA0d.h, 16); c.sA0d.v += 16

    @blk.tensor
    def _(e):
        e.wait_ge(c.sXSd.h, c.sXSd.v)
        for g in range(NCH):
            e.wait_ge(c.sA0d.h, a1d + 16 * (g + 1))
            for h in range(2):
                mm = e.matmul(c.PS[0:1, 1024 + 512 * h:1024 + 512 * (h + 1)],
                              c.VSB[:, g:g + 1], c.WQB[:, 512 * h:512 * (h + 1)],
                              start=(g == 0), stop=(g == NCH - 1))
            mm.then_inc(c.sA0t.h, 1)
            c.sA0t.v += 1

    a0v2 = c.sA0v.v

    @blk.vector
    def _(e):
        e.wait_ge(c.sA0t.h, c.sA0t.v)
        e.tensor_copy(out=c.SCR[0:1, 0:D], in_=c.PS[0:1, 1024:2048]
                      ).then_inc(c.sA0v.h, 1); c.sA0v.v += 1

    xd2 = c.sXSd.v

    @blk.sync
    def _(e):
        e.wait_ge(c.sA0v.h, a0v2 + 1)
        e.dma_start(out=c.rowsp[3:4, 0:D], in_=c.SCR[0:1, 0:D]
                    ).then_inc(c.sXSd.h, 16); c.sXSd.v += 16
        e.wait_ge(c.sXSd.h, xd2 + 16)
        e.dma_start(out=c.C0[:, :],
                    in_=c.rowsp[3:4, 0:D].rearrange("o (g p) -> (o p) g", p=128),
                    allow_slow_non_contiguous=True
                    ).then_inc(c.sXSd.h, 16); c.sXSd.v += 16

    @blk.vector
    def _(e):
        e.wait_ge(c.sXSd.h, c.sXSd.v)
        e.scalar_tensor_tensor(out=c.C0[:, :], in0=c.F2B[:, :], scalar=0.5,
                               in1=c.C0[:, :], op0=ALU.mult, op1=ALU.add
                               ).then_inc(c.sA0v.h, 1); c.sA0v.v += 1


def _ph_ffn(c, layer, nst=NST):
    blk = c.blk
    spill = c.w2sp if layer == 0 else c.wqsp
    rows = [0, 1, 2] if layer == 0 else [0, 1]
    for s in range(nst):
        bb = c.sBCd.v

        @blk.scalar
        def _(e, s=s, rows=rows, layer=layer):
            if s == 0:
                e.wait_ge(c.sBCd.h, c.sBCd.v)
                if layer == 0:
                    e.wait_ge(c.sA0v.h, c.sA0v.v)
            if s >= 1:
                e.wait_ge(c.sEP.h, c.sEP.v)
            _emit_bcast(c, e, s, rows=rows)

        hb0 = c.sHB.v

        @blk.vector
        def _(e, s=s, hb0=hb0, bb=bb, layer=layer, rows=rows):
            if s == 0:
                if layer == 0:
                    e.wait_ge(c.sA0v.h, c.sA0v.v)
                    e.wait_ge(c.sDQ.h, c.sDQ.v)
                    e.wait_ge(c.sW2d.h, c.sW2d.v)
                else:
                    e.wait_ge(c.sL1d.h, c.sL1d.v)
            if layer == 1 and s >= 1:
                e.wait_ge(c.sF1e.h, 64 * s)
                e.wait_ge(c.sF1o.h, 64 * s)
            e.wait_ge(c.sBCd.h, bb + 16 * len(rows))
            for g in range(NCH):
                xg = c.X[:, T * g + TSW * s:T * g + TSW * (s + 1)]
                if layer == 0:
                    if g >= 1:
                        e.wait_ge(c.sHB.h, hb0 + 2 * g)
                    e.tensor_tensor(out=c.SCR[:, 0:TSW], in0=xg,
                                    in1=c.BCA[:, 0:TSW], op=ALU.mult)
                    e.tensor_tensor(out=c.SCR[:, 512:1024], in0=c.SCR[:, 0:TSW],
                                    in1=c.BCA[:, TSW:2 * TSW], op=ALU.add)
                    e.tensor_tensor(out=c.SCR[:, 0:TSW], in0=c.SCR[:, 512:1024],
                                    in1=c.BCA[:, 2 * TSW:3 * TSW],
                                    op=ALU.mult).then_inc(c.sHB.h, 1)
                    c.sHB.v += 2
                else:
                    e.tensor_tensor(out=c.SCR[:, 0:TSW], in0=xg,
                                    in1=c.BCA[:, 0:TSW], op=ALU.mult)
                    e.tensor_tensor(out=c.HBB[:, TSW * g:TSW * (g + 1)],
                                    in0=c.SCR[:, 0:TSW],
                                    in1=c.BCA[:, TSW:2 * TSW], op=ALU.add
                                    ).then_inc(c.sHB.h, 1)
                    c.sHB.v += 1
            if layer == 0:
                _emit_wq_tiles(c, e, s)

        if layer == 0:
            @blk.scalar
            def _(e, hb0=hb0):
                for g in range(NCH):
                    e.wait_ge(c.sHB.h, hb0 + 2 * g + 1)
                    e.activation(out=c.HBB[:, TSW * g:TSW * (g + 1)],
                                 in_=c.SCR[:, 0:TSW], func=AF.Identity,
                                 bias=c.BET[:, g:g + 1], scale=c.GAM[:, g:g + 1]
                                 ).then_inc(c.sHB.h, 1)

        mm0, ev0 = c.sMM.v, c.sEV.v

        @blk.tensor
        def _(e, s=s, layer=layer, mm0=mm0, ev0=ev0):
            e.wait_ge(c.sHB.h, c.sHB.v)
            if s > 0 or layer == 1:
                e.wait_ge(c.sEP.h, c.sEP.v)
            for fo in range(NFO):
                if fo >= 2:
                    e.wait_ge(c.sEV.h, ev0 + (fo - 1) * (2 if layer else 1))
                pb = c.PS[:, 512 * (fo % 2):512 * (fo % 2 + 1)]
                for g in range(NCH):
                    mm = e.matmul(pb, c.WB[:, FF * g + 128 * fo:FF * g + 128 * (fo + 1)],
                                  c.HBB[:, TSW * g:TSW * (g + 1)],
                                  start=(g == 0), stop=(g == NCH - 1))
                mm.then_inc(c.sMM.h, 1)
                c.sMM.v += 1

        if layer == 0:
            @blk.scalar
            def _(e, mm0=mm0):
                for fo in range(NFO):
                    e.wait_ge(c.sMM.h, mm0 + fo + 1)
                    pb = c.PS[:, 512 * (fo % 2):512 * (fo % 2 + 1)]
                    e.activation(out=c.YA[:, TSW * fo:TSW * (fo + 1)], in_=pb,
                                 func=AF.Silu, bias=c.F1B[:, fo:fo + 1]
                                 ).then_inc(c.sEV.h, 1)
                    c.sEV.v += 1
        else:
            @blk.scalar
            def _(e, mm0=mm0, ev0=ev0):
                for fo in range(NFO):
                    e.wait_ge(c.sMM.h, mm0 + fo + 1)
                    if fo >= 1:
                        e.wait_ge(c.sEV.h, ev0 + 2 * fo)
                    pb = c.PS[:, 512 * (fo % 2):512 * (fo % 2 + 1)]
                    e.activation(out=c.SCR[:, 0:TSW], in_=pb, func=AF.Relu,
                                 bias=c.B1L[:, fo:fo + 1]).then_inc(c.sEV.h, 1)
                    c.sEV.v += 2

            @blk.vector
            def _(e, ev0=ev0):
                for fo in range(NFO):
                    e.wait_ge(c.sEV.h, ev0 + 2 * fo + 1)
                    _equant(c, e, c.SCR[:, 0:TSW], c.YA[:, TSW * fo:TSW * (fo + 1)],
                            man=3, s1=c.SCR[:, 512:1024], s2=c.SCR[:, 1024:1536],
                            msk=c.SCR[:, 1536:2048], signed=False
                            ).then_inc(c.sEV.h, 1)

        rb0, rt0 = (c.sRBe.v, c.sRBo.v), c.sRBt.v

        @blk.sync
        def _(e, spill=spill, rt0=rt0, s=s, layer=layer):
            if s == 0:
                e.wait_ge((c.sW2d if layer == 0 else c.sWQd).h,
                          (c.sW2d if layer == 0 else c.sWQd).v)
            for q in range(16):
                if rt0 + q >= 2:
                    e.wait_ge(c.sRBt.h, rt0 + q - 1)
                sb_ = c.sRBe if q % 2 == 0 else c.sRBo
                e.dma_start(out=c.W2B[:, 2 * D * (q % 2):2 * D * (q % 2 + 1)
                                      ].rearrange("p (g d) -> p g d", g=2),
                            in_=spill[256 * q:256 * (q + 1), :].rearrange(
                                "(g p) d -> p g d", p=128)
                            ).then_inc(sb_.h, 16); sb_.v += 16

        @blk.tensor
        def _(e, rb0=rb0, rt0=rt0):
            e.wait_ge(c.sEV.h, c.sEV.v)
            for q in range(16):
                sb_ = c.sRBe if q % 2 == 0 else c.sRBo
                e.wait_ge(sb_.h, rb0[q % 2] + 16 * (q // 2 + 1))
                for gg in range(2):
                    g2 = 2 * q + gg
                    off = 2 * D * (q % 2) + D * gg
                    for d in range(NCH):
                        mm = e.matmul(c.PS[:, 512 * d:512 * (d + 1)],
                                      c.W2B[:, off + 128 * d:off + 128 * (d + 1)],
                                      c.YA[:, TSW * g2:TSW * (g2 + 1)],
                                      start=(g2 == 0), stop=(g2 == 31))
                mm.then_inc(c.sRBt.h, 1)
                c.sRBt.v += 1

        ep0 = c.sEP.v

        @blk.scalar
        def _(e, rt0=rt0, layer=layer, ep0=ep0, wqv=c.sWQ.v):
            e.wait_ge(c.sRBt.h, rt0 + 16)
            for d in range(NCH):
                if d == 0 and layer == 0:
                    e.wait_ge(c.sWQ.h, wqv)
                if d >= 2:
                    e.wait_ge(c.sEP.h, ep0 + 2 * d - 1)
                bias = c.C0[:, d:d + 1] if layer == 0 else c.BQ[:, d:d + 1]
                e.activation(out=c.SCR[:, 2048 + TSW * (d % 2):2048 + TSW * (d % 2 + 1)],
                             in_=c.PS[:, 512 * d:512 * (d + 1)], func=AF.Identity,
                             bias=bias, scale=(0.5 if layer == 0 else 1.0)
                             ).then_inc(c.sEP.h, 1)
                c.sEP.v += 2

        @blk.vector
        def _(e, s=s, ep0=ep0, layer=layer):
            for d in range(NCH):
                e.wait_ge(c.sEP.h, ep0 + 2 * d + 1)
                zc = c.SCR[:, 2048 + TSW * (d % 2):2048 + TSW * (d % 2 + 1)]
                xg = c.X[:, T * d + TSW * s:T * d + TSW * (s + 1)]
                if layer == 0:
                    e.tensor_tensor(out=c.SCR[:, 0:TSW], in0=xg,
                                    in1=c.BCA[:, 0:TSW], op=ALU.mult)
                    e.tensor_tensor(out=c.SCR[:, 512:1024], in0=c.SCR[:, 0:TSW],
                                    in1=c.BCA[:, TSW:2 * TSW], op=ALU.add)
                    e.tensor_tensor(out=c.SCR[:, 0:TSW], in0=c.SCR[:, 512:1024],
                                    in1=zc, op=ALU.add)
                    e.tensor_tensor(out=xg, in0=c.SCR[:, 0:TSW], in1=xg,
                                    op=ALU.add).then_inc(c.sEP.h, 1)
                else:
                    i = NCH * s + d
                    if (i + 1) // 2 > 0:
                        e.wait_ge(c.sF1e.h, 16 * ((i + 1) // 2))
                    if i // 2 > 0:
                        e.wait_ge(c.sF1o.h, 16 * (i // 2))
                    _equant(c, e, zc, c.SCR[:, 0:TSW], man=10,
                            s1=c.SCR[:, 512:1024], s2=c.SCR[:, 1024:1536],
                            s3=c.SCR[:, 1536:2048], msk=c.SCR[:, 1536:2048],
                            signed=True).then_inc(c.sEP.h, 1)
                    e.engine_nop().then_inc(c.sF1v.h, 1)
                    c.sF1v.v += 1

        if layer == 1:
            @blk.gpsimd
            def _(e, s=s):
                for d in range(NCH):
                    i = NCH * s + d
                    e.wait_ge(c.sF1v.h, i + 1)
                    fs = c.sF1e if i % 2 == 0 else c.sF1o
                    e.dma_start(out=c.f1sp[:, T * d + TSW * s:
                                           T * d + TSW * (s + 1)],
                                in_=c.SCR[:, 0:TSW]).then_inc(fs.h, 16)
                    fs.v += 16


def _ph_w1_load_l1(c):
    blk = c.blk

    @blk.gpsimd
    def _(e):
        e.wait_ge(c.sEP.h, c.sEP.v)
        for g in range(NCH):
            e.dma_start(out=c.WB[:, FF * g:FF * (g + 1)],
                        in_=c.w1t[128 * g:128 * (g + 1), :]
                        ).then_inc(c.sL1d.h, 16); c.sL1d.v += 16


def _ph_qk(c):
    blk = c.blk
    qt0, qe0, ke0 = c.sQKt.v, c.sQKe.v, c.sEV.v
    bb, l10, qd0 = c.sBCd.v, c.sL1.v, c.sQKd.v

    @blk.gpsimd
    def _(e):
        e.wait_ge(c.sEP.h, c.sEP.v)
        for bk in range(2):
            if bk == 1:
                e.wait_ge(c.sQKt.h, qt0 + 32)
            for g in range(NCH):
                e.dma_start(out=c.QKW[:, 1024 * g:1024 * (g + 1)],
                            in_=c.qkvt[128 * g:128 * (g + 1),
                                       1024 * bk:1024 * (bk + 1)]
                            ).then_inc(c.sQKd.h, 16); c.sQKd.v += 16
        for s in range(NST):
            for pc in range(NCH):
                i = 8 * s + pc
                e.wait_ge(c.sEV.h, ke0 + i + 1)
                ks = c.sKSe if i % 2 == 0 else c.sKSo
                e.dma_start(out=c.qksp[:, NCH * T + T * pc + TSW * s:
                                       NCH * T + T * pc + TSW * (s + 1)],
                            in_=c.QKS[:, TSW * (i % 2):TSW * (i % 2 + 1)]
                            ).then_inc(ks.h, 16); ks.v += 16

    @blk.scalar
    def _(e):
        e.wait_ge(c.sEP.h, c.sEP.v)
        e.wait_ge(c.sBCd.h, c.sBCd.v)
        for bk in range(2):
            for s in range(NST):
                si = 4 * bk + s
                if si >= 1:
                    e.wait_ge(c.sL1.h, l10 + si)
                _emit_bcast(c, e, s, rows=[0, 1])
                if bk == 0:
                    for pc in range(NCH):
                        i = 8 * s + pc
                        e.wait_ge(c.sQKt.h, qt0 + i + 1)
                        qs = c.sQSe if i % 2 == 0 else c.sQSo
                        if i >= 2:
                            e.wait_ge(qs.h, 16 * (i // 2))
                        pb = c.PS[:, 512 * (i % 2):512 * (i % 2 + 1)]
                        e.activation(out=c.QKS[:, TSW * (i % 2):TSW * (i % 2 + 1)],
                                     in_=pb, func=AF.Identity, scale=0.125
                                     ).then_inc(c.sQKe.h, 1)
                        c.sQKe.v += 1
                        e.wait_ge(c.sQKe.h, qe0 + i + 1)
                        e.dma_start(out=c.qksp[:, T * pc + TSW * s:
                                               T * pc + TSW * (s + 1)],
                                    in_=c.QKS[:, TSW * (i % 2):TSW * (i % 2 + 1)]
                                    ).then_inc(qs.h, 16)
                        qs.v += 16

    @blk.vector
    def _(e):
        for bk in range(2):
            for s in range(NST):
                si = 4 * bk + s
                if si == 0:
                    e.wait_ge(c.sF1e.h, c.sF1e.v)
                    e.wait_ge(c.sF1o.h, c.sF1o.v)
                e.wait_ge(c.sBCd.h, bb + 32 * (si + 1))
                if si >= 1:
                    e.wait_ge(c.sQKt.h, qt0 + 8 * si)
                for g in range(NCH):
                    e.tensor_tensor(out=c.SCR[:, 0:TSW],
                                    in0=c.X[:, T * g + TSW * s:T * g + TSW * (s + 1)],
                                    in1=c.BCA[:, 0:TSW], op=ALU.mult)
                    e.tensor_tensor(out=c.XNS[:, TSW * g:TSW * (g + 1)],
                                    in0=c.SCR[:, 0:TSW], in1=c.BCA[:, TSW:2 * TSW],
                                    op=ALU.add)
                e.engine_nop().then_inc(c.sL1.h, 1)
                c.sL1.v += 1
                if bk == 1:
                    for pc in range(NCH):
                        i = 8 * s + pc
                        e.wait_ge(c.sQKt.h, qt0 + 32 + i + 1)
                        ks = c.sKSe if i % 2 == 0 else c.sKSo
                        if i < 2:
                            e.wait_ge(c.sQSe.h, c.sQSe.v)
                            e.wait_ge(c.sQSo.h, c.sQSo.v)
                        if i >= 2:
                            e.wait_ge(ks.h, 16 * (i // 2))
                        pb = c.PS[:, 512 * (i % 2):512 * (i % 2 + 1)]
                        e.tensor_copy(out=c.QKS[:, TSW * (i % 2):TSW * (i % 2 + 1)],
                                      in_=pb).then_inc(c.sEV.h, 1)
                        c.sEV.v += 1

    @blk.tensor
    def _(e):
        for bk in range(2):
            e.wait_ge(c.sQKd.h, qd0 + 128 * (bk + 1))
            for s in range(NST):
                si = 4 * bk + s
                e.wait_ge(c.sL1.h, l10 + si + 1)
                for pc in range(NCH):
                    i = 32 * bk + 8 * s + pc
                    if i >= 2:
                        if i - 2 < 32:
                            e.wait_ge(c.sQKe.h, qe0 + (i - 2) + 1)
                        else:
                            e.wait_ge(c.sEV.h, ke0 + (i - 2 - 32) + 1)
                    pb = c.PS[:, 512 * (i % 2):512 * (i % 2 + 1)]
                    for g in range(NCH):
                        mm = e.matmul(
                            pb,
                            c.QKW[:, 1024 * g + 128 * pc:1024 * g + 128 * (pc + 1)],
                            c.XNS[:, TSW * g:TSW * (g + 1)],
                            start=(g == 0), stop=(g == NCH - 1))
                    mm.then_inc(c.sQKt.h, 1)
                    c.sQKt.v += 1


def _ph_vt(c):
    blk = c.blk
    vd0, vt0, ve0 = c.sVTd.v, c.sVTt.v, c.sVTe.v
    bb, l10 = c.sBCd.v, c.sL1.v

    @blk.gpsimd
    def _(e):
        ap = c.VT[:, :].rearrange("p (t h u) -> p t h u", h=16, u=65)
        e.memset(ap[:, :, :, 64:65], 1.0).then_inc(c.sME.h, 1); c.sME.v += 1
        for tg in range(4):
            for g in range(NCH):
                i = 8 * tg + g
                if i >= 1:
                    e.wait_ge(c.sVTt.h, vt0 + i)
                e.dma_start(out=c.VW[:, :],
                            in_=c.qkvt[128 * g:128 * (g + 1), 2048:3072]
                            ).then_inc(c.sVTd.h, 16); c.sVTd.v += 16

    @blk.scalar
    def _(e):
        e.wait_ge(c.sBCd.h, c.sBCd.v)
        e.wait_ge(c.sL1.h, l10)
        for tg in range(4):
            if tg >= 1:
                e.wait_ge(c.sL1.h, l10 + tg)
            _emit_bcast(c, e, tg, rows=[0, 1])

    @blk.vector
    def _(e):
        e.wait_ge(c.sME.h, c.sME.v)
        for tg in range(4):
            e.wait_ge(c.sBCd.h, bb + 32 * (tg + 1))
            if tg >= 1:
                e.wait_ge(c.sVTt.h, vt0 + 8 * tg)
            for g in range(NCH):
                e.tensor_tensor(out=c.SCR[:, 0:TSW],
                                in0=c.X[:, T * g + TSW * tg:T * g + TSW * (tg + 1)],
                                in1=c.BCA[:, 0:TSW], op=ALU.mult)
                e.tensor_tensor(out=c.XNS[:, TSW * g:TSW * (g + 1)],
                                in0=c.SCR[:, 0:TSW], in1=c.BCA[:, TSW:2 * TSW],
                                op=ALU.add)
            e.engine_nop().then_inc(c.sL1.h, 1)
            c.sL1.v += 1
            e.wait_ge(c.sVTt.h, vt0 + 8 * (tg + 1))
            for tt in range(4 * tg, 4 * tg + 4):
                dst = c.VT[:, 1040 * tt:1040 * (tt + 1)].rearrange(
                    "p (h u) -> p h u", u=65)[:, :, 0:64]
                src = c.PS[:, 1024 * (tt % 4):1024 * (tt % 4) + 1024].rearrange(
                    "p (h u) -> p h u", u=64)
                e.tensor_copy(out=dst, in_=src).then_inc(c.sVTe.h, 1)
                c.sVTe.v += 1

    @blk.tensor
    def _(e):
        for tg in range(4):
            e.wait_ge(c.sL1.h, l10 + tg + 1)
            if tg >= 1:
                e.wait_ge(c.sVTe.h, ve0 + 4 * tg)
            for g in range(NCH):
                i = 8 * tg + g
                e.wait_ge(c.sVTd.h, vd0 + 16 * (i + 1))
                for tt in range(4):
                    st = c.XNS[:, TSW * g + 128 * tt:TSW * g + 128 * (tt + 1)]
                    for h2 in range(2):
                        mm = e.matmul(
                            c.PS[:, 1024 * tt + 512 * h2:1024 * tt + 512 * (h2 + 1)],
                            st, c.VW[:, 512 * h2:512 * (h2 + 1)],
                            start=(g == 0), stop=(g == NCH - 1))
                mm.then_inc(c.sVTt.h, 1)
                c.sVTt.v += 1


def _ph_heads(c):
    blk = c.blk
    ex0, mk0, pc0, pt0 = c.sEXP.v, c.sMSKv.v, c.sPVc.v, c.sPVt.v
    nr0, rr0 = c.sNRM.v, c.sRR.v
    NHS = NH * NST

    score_t, pv_t, exp_t, mask_t = {}, {}, {}, {}
    rr_t, norm_t, copy_t, mult_t = {}, {}, {}, {}
    grp_of, grp_last_pv = {}, {}
    tpv = tpc = tex = tmk = trr = tnr = 0
    gidx = 0
    for hs in range(NHS):
        s = hs % NST
        for g in range(s + 1):
            for u in range(4):
                tpv += 1; score_t[(hs, 4 * g + u)] = tpv
            if g == 0 and hs >= 1:
                tpv += 1; norm_t[hs - 1] = tpv
            if g >= 1:
                for u in range(4):
                    tpc += 1; pv_t[(hs, 4 * (g - 1) + u)] = tpc
        for u in range(4):
            tpc += 1; pv_t[(hs, 4 * s + u)] = tpc
        for g in range(s + 1):
            tex += 1; exp_t[(hs, g)] = tex
            grp_of[(hs, g)] = gidx
            grp_last_pv[gidx] = pv_t[(hs, 4 * g + 3)]
            gidx += 1
        trr += 1; rr_t[hs] = trr
        tmk += 1; mask_t[hs] = tmk
        tnr += 1; copy_t[hs] = tnr
        tnr += 1; mult_t[hs] = tnr
    tpv += 1; norm_t[NHS - 1] = tpv

    pvb = lambda hs: 2048 + 512 * (hs % 2)
    nmb = lambda hs: 3072 + 512 * (hs % 2)
    rrb = lambda hs: c.SCR[0:1, 2048 + 512 * (hs % 2):2560 + 512 * (hs % 2)]

    @blk.gpsimd
    def _(e):
        e.wait_ge(c.sVTe.h, c.sVTe.v)
        for ss in (c.sQSe, c.sQSo, c.sKSe, c.sKSo):
            e.wait_ge(ss.h, ss.v)
        e.dma_start(out=c.M4[:, :], in_=c.m4).then_inc(c.sQKd.h, 16)
        c.sQKd.v += 16
        for p in range(NCH):
            if p >= 2:
                e.wait_ge(c.sPVt.h, pt0 + score_t[(8 * (p - 2) + 7, 15)])
            b = 2 * T * (p % 2)
            qp = c.sQPe if p % 2 == 0 else c.sQPo
            e.dma_start(out=c.QKP[:, b:b + T], in_=c.qksp[:, T * p:T * (p + 1)]
                        ).then_inc(qp.h, 16); qp.v += 16
            e.dma_start(out=c.QKP[:, b + T:b + 2 * T],
                        in_=c.qksp[:, NCH * T + T * p:NCH * T + T * (p + 1)]
                        ).then_inc(qp.h, 16); qp.v += 16

    @blk.tensor
    def _(e):
        e.wait_ge(c.sVTe.h, c.sVTe.v)

        def _norm(hs2):
            e.wait_ge(c.sRR.h, rr0 + rr_t[hs2])
            if hs2 >= 2:
                e.wait_ge(c.sNRM.h, nr0 + mult_t[hs2 - 2])
            e.matmul(c.PS[0:64, nmb(hs2):nmb(hs2) + 512], c.ONE64[:, :],
                     rrb(hs2), start=True, stop=True).then_inc(c.sPVt.h, 1)
            c.sPVt.v += 1

        def _pv(hs, g):
            h, s = hs // NST, hs % NST
            k = grp_of[(hs, g)]
            if g == s:
                e.wait_ge(c.sMSKv.h, mk0 + mask_t[hs])
            else:
                e.wait_ge(c.sEXP.h, ex0 + exp_t[(hs, g)])
            for u in range(4):
                j = 4 * g + u
                if j == 0 and hs >= 2:
                    e.wait_ge(c.sNRM.h, nr0 + copy_t[hs - 2])
                    e.wait_ge(c.sRR.h, rr0 + rr_t[hs - 2])
                e.matmul(c.PS[0:65, pvb(hs):pvb(hs) + 512],
                         c.VT[:, 1040 * j + 65 * h:1040 * j + 65 * (h + 1)],
                         c.PT[:, 2048 * (k % 2) + 512 * u:
                              2048 * (k % 2) + 512 * (u + 1)],
                         start=(j == 0), stop=(j == 4 * s + 3)
                         ).then_inc(c.sPVc.h, 1)
                c.sPVc.v += 1

        for hs in range(NHS):
            h, s = hs // NST, hs % NST
            pc, hh = h // 2, h % 2
            b = 2 * T * (pc % 2)
            prow = slice(64 * hh, 64 * hh + 64)
            if hh == 0 and s == 0:
                qp = c.sQPe if pc % 2 == 0 else c.sQPo
                e.wait_ge(qp.h, 32 * (pc // 2 + 1))
            for g in range(s + 1):
                if exp_t[(hs, g)] >= 2:
                    e.wait_ge(c.sEXP.h, ex0 + exp_t[(hs, g)] - 1)
                for u in range(4):
                    j = 4 * g + u
                    e.matmul(c.PS[:, 512 * u:512 * (u + 1)],
                             c.QKP[prow, b + T + 128 * j:b + T + 128 * (j + 1)],
                             c.QKP[prow, b + TSW * s:b + TSW * (s + 1)],
                             start=True, stop=True).then_inc(c.sPVt.h, 1)
                    c.sPVt.v += 1
                if g == 0 and hs >= 1:
                    _norm(hs - 1)
                if g >= 1:
                    _pv(hs, g - 1)
            _pv(hs, s)
        _norm(NHS - 1)

    @blk.scalar
    def _(e):
        for hs in range(NHS):
            s = hs % NST
            for g in range(s + 1):
                k = grp_of[(hs, g)]
                e.wait_ge(c.sPVt.h, pt0 + score_t[(hs, 4 * g + 3)])
                if k >= 2:
                    e.wait_ge(c.sPVc.h, pc0 + grp_last_pv[k - 2])
                e.activation(out=c.PT[:, 2048 * (k % 2):2048 * (k % 2) + 2048],
                             in_=c.PS[:, 0:2048], func=AF.Exp
                             ).then_inc(c.sEXP.h, 1)
                c.sEXP.v += 1
            e.wait_ge(c.sPVc.h, pc0 + pv_t[(hs, 4 * s + 3)])
            if hs >= 2:
                e.wait_ge(c.sPVt.h, pt0 + norm_t[hs - 2])
            rr = rrb(hs)
            e.activation(out=rr, in_=c.PS[64:65, pvb(hs):pvb(hs) + 512],
                         func=AF.Ln)
            e.activation(out=rr, in_=rr, func=AF.Exp, scale=-1.0
                         ).then_inc(c.sRR.h, 1)
            c.sRR.v += 1

    @blk.vector
    def _(e):
        e.wait_ge(c.sQKd.h, c.sQKd.v)
        for hs in range(NHS):
            h, s = hs // NST, hs % NST
            pc, hh = h // 2, h % 2
            k = grp_of[(hs, s)]
            e.wait_ge(c.sEXP.h, ex0 + exp_t[(hs, s)])
            pt = c.PT[:, 2048 * (k % 2):2048 * (k % 2) + 2048]
            e.tensor_tensor(out=pt, in0=pt, in1=c.M4[:, :], op=ALU.mult
                            ).then_inc(c.sMSKv.h, 1)
            c.sMSKv.v += 1
            dst = c.YA[64 * hh:64 * hh + 64,
                       T * pc + TSW * s:T * pc + TSW * (s + 1)]
            e.wait_ge(c.sPVc.h, pc0 + pv_t[(hs, 4 * s + 3)])
            e.tensor_copy(out=dst, in_=c.PS[0:64, pvb(hs):pvb(hs) + 512]
                          ).then_inc(c.sNRM.h, 1)
            c.sNRM.v += 1
            e.wait_ge(c.sPVt.h, pt0 + norm_t[hs])
            e.tensor_tensor(out=dst, in0=dst,
                            in1=c.PS[0:64, nmb(hs):nmb(hs) + 512],
                            op=ALU.mult).then_inc(c.sNRM.h, 1)
            c.sNRM.v += 1


def _ph_proj(c):
    blk = c.blk
    pd0, pvt0, pr0 = c.sPRd.v, c.sPVt.v, c.sPRt.v

    @blk.gpsimd
    def _(e):
        e.wait_ge(c.sNRM.h, c.sNRM.v)
        for g in range(NCH):
            e.dma_start(out=c.PRW[:, D * g:D * (g + 1)],
                        in_=c.projt[128 * g:128 * (g + 1), :]
                        ).then_inc(c.sPRd.h, 16); c.sPRd.v += 16
        e.wait_ge(c.sF1e.h, c.sF1e.v)
        e.wait_ge(c.sF1o.h, c.sF1o.v)
        for s in range(NST):
            for oc in range(NCH):
                i = NCH * s + oc
                if i >= 2:
                    e.wait_ge(c.sPRt.h, pr0 + i - 1)
                fp = c.sFPe if i % 2 == 0 else c.sFPo
                e.dma_start(out=c.BCA[:, 512 * (i % 2):512 * (i % 2 + 1)],
                            in_=c.f1sp[:, T * oc + TSW * s:
                                       T * oc + TSW * (s + 1)]
                            ).then_inc(fp.h, 16); fp.v += 16

    @blk.tensor
    def _(e):
        e.wait_ge(c.sPRd.h, pd0 + 16 * NCH)
        for s in range(NST):
            for oc in range(NCH):
                i = NCH * s + oc
                if i >= 2:
                    e.wait_ge(c.sPRt.h, pr0 + i - 1)
                pb = c.PS[:, 512 * (i % 2):512 * (i % 2 + 1)]
                for g in range(NCH):
                    mm = e.matmul(pb, c.PRW[:, D * g + 128 * oc:D * g + 128 * (oc + 1)],
                                  c.YA[:, T * g + TSW * s:T * g + TSW * (s + 1)],
                                  start=(g == 0), stop=(g == NCH - 1))
                mm.then_inc(c.sPVt.h, 1)
                c.sPVt.v += 1

    @blk.vector
    def _(e):
        for s in range(NST):
            for oc in range(NCH):
                i = NCH * s + oc
                e.wait_ge(c.sPVt.h, pvt0 + i + 1)
                pb = c.PS[:, 512 * (i % 2):512 * (i % 2 + 1)]
                xg = c.X[:, T * oc + TSW * s:T * oc + TSW * (s + 1)]
                e.scalar_tensor_tensor(out=xg, in0=pb, scalar=c.BPJ[:, oc:oc + 1],
                                       in1=xg, op0=ALU.add, op1=ALU.add)
                fp = c.sFPe if i % 2 == 0 else c.sFPo
                e.wait_ge(fp.h, 16 * (i // 2 + 1))
                e.tensor_tensor(out=xg, in0=xg,
                                in1=c.BCA[:, 512 * (i % 2):512 * (i % 2 + 1)],
                                op=ALU.add).then_inc(c.sPRt.h, 1)
                c.sPRt.v += 1


def _ph_out(c, wait):
    blk = c.blk

    @blk.sync
    def _(e):
        for s in wait:
            e.wait_ge(s.h, s.v)
        e.dma_start(out=c.out, in_=c.X[:, :]).then_inc(c.sOUd.h, 16)
        c.sOUd.v += 16
    @blk.gpsimd
    def _(e):
        e.wait_ge(c.sOUd.h, c.sOUd.v)


def host_prep(L0_attn_x, L0_attn_w_qkv, L0_attn_w_o, L0_ffn_ln_gamma,
              L0_ffn_ln_beta, L0_ffn_fc1_q, L0_ffn_fc1_scales, L0_ffn_fc1_bias,
              L0_ffn_fc2_q, L0_ffn_fc2_scales, L0_ffn_fc2_bias, L1_attn_w_qkv,
              L1_attn_w_proj, L1_attn_b_proj, L1_ffn_w1, L1_ffn_b1, L1_ffn_w2,
              L1_ffn_b2):
    import ml_dtypes
    A = lambda v: np.ascontiguousarray(v)
    f32 = np.float32
    cm = lambda vec, k: A(np.asarray(vec, f32).reshape(k, 128).T)

    sh = {}
    sh["wvt"] = A(np.asarray(L0_attn_w_qkv, f32)[2048:3072, :].T)
    sh["wot"] = A(np.asarray(L0_attn_w_o, f32).T)
    sh["fc1q"] = A(np.asarray(L0_ffn_fc1_q, np.int32))
    sh["fc1sb"] = A(np.broadcast_to(
        np.asarray(L0_ffn_fc1_scales, f32).reshape(1, 8 * FF), (128, 8 * FF)))
    sh["fc2q"] = A(np.asarray(L0_ffn_fc2_q, np.int32))
    sh["fc2sb"] = A(np.broadcast_to(
        np.asarray(L0_ffn_fc2_scales, f32).reshape(1, 32 * D), (128, 32 * D)))
    sh["gam"] = cm(L0_ffn_ln_gamma, NCH)
    sh["bet"] = cm(L0_ffn_ln_beta, NCH)
    sh["f1b"] = cm(L0_ffn_fc1_bias, NFO)
    sh["f2b"] = cm(L0_ffn_fc2_bias, NCH)
    wq = np.asarray(L1_attn_w_qkv, f32)
    qkv = np.concatenate([wq[:, 0:64, :].reshape(NH * 64, D),
                          wq[:, 64:128, :].reshape(NH * 64, D),
                          wq[:, 128:192, :].reshape(NH * 64, D)], axis=0)
    sh["qkvt"] = A(qkv.T)
    sh["projt"] = A(np.asarray(L1_attn_w_proj, f32).T)
    sh["bpj"] = cm(L1_attn_b_proj, NCH)
    sh["w1t"] = A(np.asarray(L1_ffn_w1, f32).T)
    sh["b1l1"] = cm(L1_ffn_b1, NFO)
    sh["w2t"] = A(np.asarray(L1_ffn_w2, f32).T)
    sh["b2l1"] = cm(L1_ffn_b2, NCH)
    cvec = np.zeros((128, 8), f32)
    cvec[:, 0] = 0.5 + 2.0 ** -20
    cvec[:, 1] = EPS
    sh["cvec"] = cvec
    p = np.arange(128)[:, None]
    qq = np.arange(TSW)[None, :]
    sh["m4"] = A(np.concatenate([(qq >= 128 * r + p) for r in range(4)],
                                axis=1).astype(ml_dtypes.bfloat16))
    x = np.asarray(L0_attn_x, f32)
    in_maps = []
    for b in range(x.shape[0]):
        m = dict(sh)
        m["x"] = A(x[b].T.reshape(NCH, 128, T).transpose(1, 0, 2
                                                         ).reshape(128, NCH * T))
        in_maps.append(m)
    return in_maps


def gather_out(results):
    outs = []
    for r in results:
        o = np.asarray(r["out"]).reshape(128, NCH, T).transpose(1, 0, 2
                                                                ).reshape(D, T)
        outs.append(o.T)
    return np.stack(outs, axis=0)


_NC_CACHE = {}


def kernel(**inputs):
    in_maps = host_prep(**inputs)
    if "nc" not in _NC_CACHE:
        _NC_CACHE["nc"] = build_nc(stop="full")
    res = run_bass_kernel_spmd(_NC_CACHE["nc"], in_maps, core_ids=list(range(8)))
    out = gather_out(res.results)
    return np.ascontiguousarray(out, dtype=np.float32)
